# version 05
# baseline (speedup 1.0000x reference)
import contextlib
import math
import numpy as np

import concourse.bass as bass
import concourse.mybir as mybir
from concourse.bass_utils import run_bass_kernel_spmd

F32 = mybir.dt.float32
BF16 = mybir.dt.bfloat16
AF = mybir.ActivationFunctionType
ALU = mybir.AluOpType
AX = mybir.AxisListType

D_MODEL = 1024
T_CORE = 2048
N_CORES = 8


ENGINES = ("sp", "act", "dve", "pool", "pe")


class Buf:
    __slots__ = ("name", "last_w", "readers")

    def __init__(self, name):
        self.name = name
        self.last_w = None
        self.readers = []


class Step:
    __slots__ = ("engine", "fn", "rd", "wr", "is_dma", "seq", "eseq",
                 "deps", "tagged", "tagidx", "vclock", "coll", "_waits")

    def __init__(self, engine, fn, rd, wr, is_dma, coll=False):
        self.engine = engine
        self.fn = fn
        self.rd = rd
        self.wr = wr
        self.is_dma = is_dma
        self.coll = coll
        self.deps = []
        self.tagged = False
        self.tagidx = 0
        self.vclock = {}


PARANOID = [1]


class MK:
    def __init__(self, name="mk"):
        self.nc = bass.Bass("TRN2", target_bir_lowering=False)
        self.steps = []
        self.stack = contextlib.ExitStack()
        self._nbuf = 0
        self.sb_used = 0
        self._finalized = False
        self.paranoid = PARANOID[0]

    def sbuf(self, name, shape, dtype):
        t = self.stack.enter_context(self.nc.sbuf_tensor(name, list(shape), dtype))
        nb = int(np.prod(shape[1:])) * (2 if dtype == BF16 else 4)
        self.sb_used += nb
        return t, Buf(name)

    def psum(self, name, shape=(128, 512), dtype=F32):
        t = self.stack.enter_context(self.nc.psum_tensor(name, list(shape), dtype))
        return t, Buf(name)

    def buf(self, name=None):
        self._nbuf += 1
        return Buf(name or f"b{self._nbuf}")

    def step(self, engine, fn, rd=(), wr=(), is_dma=False, coll=False):
        s = Step(engine, fn, list(rd), list(wr), is_dma, coll)
        s.seq = len(self.steps)
        self.steps.append(s)
        return s

    def op(self, engine, fn, rd=(), wr=()):
        return self.step(engine, fn, rd, wr, False)

    def dma(self, queue, out_ap, in_ap, rd=(), wr=(), **kw):
        return self.step(queue, lambda e: e.dma_start(out=out_ap, in_=in_ap, **kw),
                         rd, wr, True)

    def resolve(self):
        for s in self.steps:
            def add_dep(t):
                if t is None or t is s:
                    return
                if (not self.paranoid) and t.engine == s.engine \
                        and not t.is_dma and not s.is_dma:
                    return
                if t.seq < s.seq and t not in s.deps:
                    s.deps.append(t)
            for b in s.rd:
                add_dep(b.last_w)
            for b in s.wr:
                add_dep(b.last_w)
                for r in b.readers:
                    add_dep(r)
            for b in s.rd:
                b.readers.append(s)
            for b in s.wr:
                b.last_w = s
                b.readers = []
        for s in self.steps:
            if s.is_dma:
                s.tagged = True
            for d in s.deps:
                d.tagged = True
        counts = {}
        for s in self.steps:
            key = (s.engine, s.is_dma)
            if s.tagged:
                inc = 1 if s.coll else (16 if s.is_dma else 1)
                counts[key] = counts.get(key, 0) + inc
                s.tagidx = counts[key]

    def finalize(self):
        assert not self._finalized
        self._finalized = True
        self.resolve()
        nc = self.nc
        semkeys = set()
        for s in self.steps:
            if s.tagged:
                semkeys.add((s.engine, s.is_dma))
        sems = {}
        for key in sorted(semkeys):
            sems[key] = self.stack.enter_context(
                nc.semaphore(f"s_{key[0]}_{'d' if key[1] else 'c'}"))
        per_engine = {e: [] for e in ENGINES}
        for s in self.steps:
            per_engine[s.engine].append(s)
        known = {}
        last_on_engine = {e: None for e in ENGINES}
        for s in self.steps:
            vc = {}
            prev = last_on_engine[s.engine]
            if prev is not None:
                vc.update(known[prev])
            waits = []
            for d in s.deps:
                key = (d.engine, d.is_dma)
                need = d.tagidx
                if self.paranoid and key == (s.engine, s.is_dma):
                    if (key, need) not in waits:
                        waits.append((key, need))
                    if vc.get(key, 0) < need:
                        vc[key] = need
                    continue
                if vc.get(key, 0) < need:
                    waits.append((key, need))
                    vc[key] = need
                    if not self.paranoid:
                        for k2, v2 in known[d].items():
                            if vc.get(k2, 0) < v2:
                                vc[k2] = v2
            s._waits = waits
            k = dict(vc)
            if s.tagged:
                k[(s.engine, s.is_dma)] = s.tagidx
            known[s] = k
            last_on_engine[s.engine] = s

        block = self.stack.enter_context(nc.Block())

        max_wait = {}
        for s in self.steps:
            for (key, need) in s._waits:
                if key[1]:
                    max_wait[key] = max(max_wait.get(key, 0), need)

        def run_engine(eng_obj, engine_name):
            for s in per_engine[engine_name]:
                for (key, need) in s._waits:
                    eng_obj.wait_ge(sems[key], need)
                inst = s.fn(eng_obj)
                if s.tagged:
                    key = (s.engine, s.is_dma)
                    inc = 1 if s.coll else (16 if s.is_dma else 1)
                    if inst is None:
                        inst = eng_obj.nop()
                    inst.then_inc(sems[key], inc)
                    if s.is_dma and s.tagidx <= max_wait.get(key, 0):
                        eng_obj.wait_ge(sems[key], s.tagidx)

        finals = [(sems[key], counts_total) for key, counts_total in
                  [(k, max((s.tagidx for s in self.steps
                            if s.tagged and (s.engine, s.is_dma) == k), default=0))
                   for k in sems]
                  if counts_total > 0]

        @block.sync
        def _(sync):
            run_engine(sync, "sp")
            for sem, tot in finals:
                sync.wait_ge(sem, tot)

        @block.scalar
        def _(scalar):
            run_engine(scalar, "act")

        @block.vector
        def _(vector):
            run_engine(vector, "dve")

        @block.gpsimd
        def _(gpsimd):
            run_engine(gpsimd, "pool")

        @block.tensor
        def _(tensor):
            run_engine(tensor, "pe")

        self.stack.close()
        return nc



def ceil_div(a, b):
    return (a + b - 1) // b


def chunks_of(total, size):
    out = []
    o = 0
    while o < total:
        c = min(size, total - o)
        out.append((o, c))
        o += c
    return out


def chunks_ov(total, size=512):
    if total <= size:
        return [(0, total)]
    out = []
    o = 0
    while o + size < total:
        out.append((o, size))
        o += size
    out.append((total - size, size))
    return out



class CP:

    def __init__(self, name):
        self.mk = MK(name)
        self.nc = self.mk.nc
        self.pb = []
        for i in range(8):
            self.pb.append(self.mk.psum(f"pb{i}", (128, 512), F32))
        self._scr = {}
        self._uid = 0
        self._lnslot = 0
        self.lnw = 2112
        self.const_vals = []
        self._cconst_in = self.nc.dram_tensor(
            "cconst", [128, 16], F32, kind="ExternalInput").ap()
        self._cconst = self.sb("cconst_sb", (128, 16), F32)
        self.mk.dma("sp", self._cconst[0][:], self._cconst_in,
                    rd=(), wr=(self._cconst[1],))

    def const_col(self, val):
        val = float(val)
        if val not in self.const_vals:
            assert len(self.const_vals) < 16
            self.const_vals.append(val)
        i = self.const_vals.index(val)
        t, b = self._cconst
        return t[:, i:i + 1], b

    def uid(self):
        self._uid += 1
        return self._uid

    def sb(self, name, shape, dtype):
        return self.mk.sbuf(name, shape, dtype)

    def scratch(self, key, shape, dtype):
        k = (key, tuple(shape), str(dtype))
        if k not in self._scr:
            if dtype == "psum_bf16":
                self._scr[k] = self.mk.psum(f"scr_{key}", shape, BF16)
            else:
                self._scr[k] = self.sb(f"scr_{key}", shape, dtype)
        return self._scr[k]

    def din(self, name, shape, dtype=F32):
        return self.nc.dram_tensor(name, list(shape), dtype, kind="ExternalInput").ap()

    def dout(self, name, shape, dtype=F32):
        return self.nc.dram_tensor(name, list(shape), dtype, kind="ExternalOutput").ap()

    def load_tiles(self, hbm_ap, rows, cols, dtype_out=BF16, name=None,
                   conv_engine="pool", direct=False):
        name = name or f"w{self.uid()}"
        tiles = []
        nt = ceil_div(rows, 128)
        for j in range(nt):
            r = min(128, rows - j * 128)
            stage, sbuf_b = self.sb(f"{name}_s{j}", (r, cols), F32) if (dtype_out != F32 and not direct) else (None, None)
            dst, dst_b = self.sb(f"{name}_{j}", (r, cols), dtype_out)
            if dtype_out == F32 or direct:
                self.mk.dma("sp", dst[:], hbm_ap[j * 128:j * 128 + r, :],
                            rd=(), wr=(dst_b,))
            else:
                self.mk.dma("sp", stage[:], hbm_ap[j * 128:j * 128 + r, :],
                            rd=(), wr=(sbuf_b,))
                self.mk.op(conv_engine,
                           lambda e, o=dst, i=stage: e.tensor_copy(out=o[:], in_=i[:]),
                           rd=(sbuf_b,), wr=(dst_b,))
            tiles.append((dst, dst_b))
        return tiles

    def load_tiles_shared_stage(self, hbm_ap, rows, cols, name=None,
                                conv_engine="pool", stage=None):
        name = name or f"w{self.uid()}"
        if stage is None:
            stage = self.sb(f"{name}_stage", (128, cols), F32)
        st, st_b = stage
        tiles = []
        nt = rows // 128
        for j in range(nt):
            dst, dst_b = self.sb(f"{name}_{j}", (128, cols), BF16)
            self.mk.dma("sp", st[:, 0:cols], hbm_ap[j * 128:(j + 1) * 128, :],
                        rd=(st_b,), wr=(st_b,))
            self.mk.op(conv_engine,
                       lambda e, o=dst, i=st, c=cols: e.tensor_copy(out=o[:], in_=i[:, 0:c]),
                       rd=(st_b,), wr=(dst_b,))
            tiles.append((dst, dst_b))
        return tiles

    def mm_acc(self, bank, m, n_len, lhs_list, rhs_list, bank_row0=0):
        pt, pbuf = self.pb[bank]
        nk = len(lhs_list)
        for ki in range(nk):
            lap, lbuf = lhs_list[ki]
            rap, rbuf = rhs_list[ki]
            self.mk.op("pe",
                       lambda e, lap=lap, rap=rap, ki=ki, nk=nk: e.matmul(
                           pt[bank_row0:bank_row0 + m, 0:n_len], lap, rap,
                           start=(ki == 0), stop=(ki == nk - 1)),
                       rd=(lbuf, rbuf) + ((pbuf,) if ki == 0 else ()),
                       wr=(pbuf,))
        return pt, pbuf

    def replicate_row(self, bank, row_ap, row_buf, n_len, ones):
        pt, pbuf = self.pb[bank]
        ones_ap, ones_buf = ones
        self.mk.op("pe",
                   lambda e: e.matmul(pt[:, 0:n_len], ones_ap, row_ap,
                                      start=True, stop=True),
                   rd=(ones_buf, row_buf, pbuf), wr=(pbuf,))
        return pt, pbuf


def emit_ln_stats(cp, tiles, width, ones_bf, banks=(0, 1, 2, 3), eps=1e-5,
                  affine=None, mean_free=False):
    mk = cp.mk
    nt = len(tiles)
    D = nt * 128
    onesap, ones_buf = ones_bf
    nch = chunks_ov(width, 512)
    srow, srow_b = cp.scratch("ln_srow", (1, cp.lnw), F32)
    qrow, qrow_b = cp.scratch("ln_qrow", (1, cp.lnw), F32)
    casts = []
    for j, (tap, tbuf) in enumerate(tiles):
        cj, cjb = cp.scratch(("lncast", j), (128, cp.lnw), BF16)
        qj, qjb = cp.scratch(("lnsq", j), (128, cp.lnw), BF16)
        if not mean_free:
            mk.op("act", lambda e, i=tap, o=cj, w=width: e.activation(
                out=o[:, 0:w], in_=i[:, 0:w], func=AF.Copy), rd=(tbuf,), wr=(cjb,))
        mk.op("act", lambda e, i=tap, o=qj, w=width: e.activation(
            out=o[:, 0:w], in_=i[:, 0:w], func=AF.Square), rd=(tbuf,), wr=(qjb,))
        casts.append(((cj, cjb), (qj, qjb)))
    for ci, (off, ln) in enumerate(nch):
        pt_q, pb_q = cp.pb[banks[2 + ci % 2]]
        if not mean_free:
            pt_s, pb_s = cp.pb[banks[ci % 2]]
            for j in range(nt):
                cj, cjb = casts[j][0]
                mk.op("pe", lambda e, off=off, ln=ln, j=j, cj=cj, pt_s=pt_s: e.matmul(
                    pt_s[0:1, 0:ln], onesap, cj[:, off:off + ln],
                    start=(j == 0), stop=(j == nt - 1)),
                    rd=(ones_buf, cjb) + ((pb_s,) if j == 0 else ()), wr=(pb_s,))
            mk.op("dve", lambda e, off=off, ln=ln, pt_s=pt_s: e.tensor_scalar_mul(
                srow[0:1, off:off + ln], pt_s[0:1, 0:ln], 1.0 / D),
                rd=(pb_s,), wr=(srow_b,))
        for j in range(nt):
            qj, qjb = casts[j][1]
            mk.op("pe", lambda e, off=off, ln=ln, j=j, qj=qj, pt_q=pt_q: e.matmul(
                pt_q[0:1, 0:ln], onesap, qj[:, off:off + ln],
                start=(j == 0), stop=(j == nt - 1)),
                rd=(ones_buf, qjb) + ((pb_q,) if j == 0 else ()), wr=(pb_q,))
        mk.op("dve", lambda e, off=off, ln=ln, pt_q=pt_q: e.tensor_scalar_mul(
            qrow[0:1, off:off + ln], pt_q[0:1, 0:ln], 1.0 / D),
            rd=(pb_q,), wr=(qrow_b,))
    arow, arow_b = cp.scratch(("ln_a", cp._lnslot), (1, cp.lnw), F32)
    brow, brow_b = cp.scratch(("ln_b", cp._lnslot), (1, cp.lnw), F32)
    cp._lnslot ^= 1
    tr, tr_b = cp.scratch("ln_t", (1, cp.lnw), F32)
    if mean_free:
        ep_ap, ep_b = cp.const_col(eps)
        mh_ap, mh_b = cp.const_col(-0.5)
        mk.op("act", lambda e: e.activation(out=tr[0:1, 0:width], in_=qrow[0:1, 0:width], func=AF.Ln,
                                            bias=ep_ap[0:1, :]), rd=(qrow_b, ep_b), wr=(tr_b,))
        mk.op("act", lambda e: e.activation(out=arow[0:1, 0:width], in_=tr[0:1, 0:width], func=AF.Exp,
                                            scale=mh_ap[0:1, :]), rd=(tr_b, mh_b), wr=(arow_b,))
        mk.op("dve", lambda e: e.memset(brow[0:1, 0:width], 0.0), rd=(), wr=(brow_b,))
    else:
        mk.op("act", lambda e: e.activation(out=tr[0:1, 0:width], in_=srow[0:1, 0:width], func=AF.Square),
              rd=(srow_b,), wr=(tr_b,))
        mk.op("dve", lambda e: e.tensor_sub(tr[0:1, 0:width], qrow[0:1, 0:width], tr[0:1, 0:width]),
              rd=(qrow_b, tr_b), wr=(tr_b,))
        ep_ap, ep_b = cp.const_col(eps)
        mh_ap, mh_b = cp.const_col(-0.5)
        mk.op("act", lambda e: e.activation(out=tr[0:1, 0:width], in_=tr[0:1, 0:width], func=AF.Ln,
                                            bias=ep_ap[0:1, :]), rd=(tr_b, ep_b), wr=(tr_b,))
        mk.op("act", lambda e: e.activation(out=arow[0:1, 0:width], in_=tr[0:1, 0:width], func=AF.Exp,
                                            scale=mh_ap[0:1, :]), rd=(tr_b, mh_b), wr=(arow_b,))
        mk.op("dve", lambda e: e.scalar_tensor_tensor(
            out=brow[0:1, 0:width], in0=srow[0:1, 0:width], scalar=-1.0, in1=arow[0:1, 0:width],
            op0=ALU.mult, op1=ALU.mult), rd=(srow_b, arow_b), wr=(brow_b,))
    return (arow, arow_b), (brow, brow_b)


def emit_rowscale_apply(cp, tiles_in, tiles_out, width, arow, brow, ones_f32,
                        bank_a=4, bank_b=5, affine=None, out_dtype=None):
    mk = cp.mk
    a_ap, a_b = arow
    b_ap, b_b = brow
    nch = chunks_ov(width, 512)
    tmp, tmp_b = cp.scratch("rowscale_tmp", (128, 512), F32)
    applied = 0
    for ci, (off, ln) in enumerate(nch):
        pa, pa_b = cp.replicate_row(bank_a, a_ap[0:1, off:off + ln], a_b, ln,
                                    ones_f32)
        pbk, pb_b = cp.replicate_row(bank_b, b_ap[0:1, off:off + ln], b_b, ln,
                                     ones_f32)
        s = max(off, applied)
        e_ = off + ln
        applied = e_
        w = e_ - s
        so = s - off
        for j, (tin, tin_b) in enumerate(tiles_in):
            tout, tout_b = tiles_out[j]
            if affine is None:
                mk.op("dve", lambda e, tin=tin, s=s, w=w, so=so, pa=pa: e.tensor_mul(
                    tmp[:, 0:w], tin[:, s:s + w], pa[:, so:so + w]),
                    rd=(tin_b, pa_b), wr=(tmp_b,))
                mk.op("dve", lambda e, tout=tout, s=s, w=w, so=so,
                      pbk=pbk: e.tensor_add(
                          tout[:, s:s + w], tmp[:, 0:w], pbk[:, so:so + w]),
                    rd=(tmp_b, pb_b), wr=(tout_b,))
            else:
                (g_ap, g_b), (be_ap, be_b) = affine[j]
                mk.op("dve", lambda e, tin=tin, s=s, w=w, so=so, pa=pa: e.tensor_mul(
                    tmp[:, 0:w], tin[:, s:s + w], pa[:, so:so + w]),
                    rd=(tin_b, pa_b), wr=(tmp_b,))
                mk.op("dve", lambda e, w=w, so=so, pbk=pbk: e.tensor_add(
                    tmp[:, 0:w], tmp[:, 0:w], pbk[:, so:so + w]),
                    rd=(tmp_b, pb_b), wr=(tmp_b,))
                mk.op("dve", lambda e, tout=tout, s=s, w=w,
                      g_ap=g_ap, be_ap=be_ap: e.scalar_tensor_tensor(
                          out=tout[:, s:s + w], in0=tmp[:, 0:w],
                          scalar=g_ap, in1=be_ap.broadcast_to([128, w]),
                          op0=ALU.mult, op1=ALU.add),
                    rd=(tmp_b, g_b, be_b), wr=(tout_b,))





def build_l0a(debug=False):
    cp = CP("l0a")
    cp.lnw = 520
    mk = cp.mk
    TQ = 512
    NT_D = 8
    NT_E = 16
    MAIN0 = 1

    R = cp.din("R", (1024, 2112))
    Win = cp.din("Win", (1024, 4096))
    Wdt = cp.din("Wdt", (2048, 2048))
    WB = cp.din("WB", (2048, 16))
    WC = cp.din("WC", (2048, 16))
    Wout = cp.din("Wout", (2048, 1024))
    Alog = cp.din("Alog", (2048, 16))
    vecs = cp.din("vecs", (2048, 3))
    convw = cp.din("convw", (2048, 3))
    nsb = cp.din("nsb", (1024, 2))
    hinit = cp.din("hinit", (2048, 16))
    sel16 = cp.din("sel16", (16, 2048), BF16)
    Rout = cp.dout("Rout", (1024, 2048))
    hout = cp.dout("hout", (2048, 16))
    if debug:
        d_xn = cp.dout("d_xn", (1024, 514), BF16)
        d_xb = cp.dout("d_xb", (2048, 512), BF16)
        d_dt0 = cp.dout("d_dt0", (128, 512))
        d_B = cp.dout("d_B", (16, 512), BF16)
        d_C = cp.dout("d_C", (16, 512), BF16)
        d_z0 = cp.dout("d_z0", (128, 512), BF16)
        d_y0 = cp.dout("d_y0", (128, 512))
        d_y20 = cp.dout("d_y20", (128, 512), BF16)
        d_y2a = cp.dout("d_y2a", (2048, 512), BF16)
        d_xn0 = cp.dout("d_xn0", (1024, 514))
        d_rows = cp.dout("d_rows", (4, 520))
        d_dta = cp.dout("d_dta", (2048, 512))
        d_za = cp.dout("d_za", (2048, 512), BF16)

    winb = cp.nc.dram_tensor("winb", [1024, 4096], BF16, kind="Internal").ap()
    wdtb = cp.nc.dram_tensor("wdtb", [2048, 2048], BF16, kind="Internal").ap()
    woutbh = cp.nc.dram_tensor("woutbh", [2048, 1024], BF16, kind="Internal").ap()
    winb_buf = cp.mk.buf("winb")
    wdtb_buf = cp.mk.buf("wdtb")
    woutbh_buf = cp.mk.buf("woutbh")

    ones_bf = cp.sb("ones_bf", (128, 1), BF16)
    mk.op("dve", lambda e: e.memset(ones_bf[0][:], 1.0), wr=(ones_bf[1],))
    ones_f = cp.sb("ones_f", (1, 128), F32)
    mk.op("dve", lambda e: e.memset(ones_f[0][:], 1.0), wr=(ones_f[1],))

    vec_t = cp.load_tiles(vecs, 2048, 3, dtype_out=F32, name="vec")
    cw_t = cp.load_tiles(convw, 2048, 3, dtype_out=F32, name="cw")
    nsb_t = cp.load_tiles(nsb, 1024, 2, dtype_out=F32, name="nsb")
    alog_t = cp.load_tiles(Alog, 2048, 16, dtype_out=F32, name="alog")
    hcur = cp.load_tiles(hinit, 2048, 16, dtype_out=F32, name="hcur")
    sel_t = cp.load_tiles(sel16, 16, 2048, dtype_out=BF16, name="sel", direct=True)
    wb_t = cp.load_tiles(WB, 2048, 16, dtype_out=BF16, name="wbt")
    wc_t = cp.load_tiles(WC, 2048, 16, dtype_out=BF16, name="wct")

    aneg = []
    for j in range(NT_E):
        t, tb = cp.sb(f"aneg{j}", (128, 16), F32)
        src, srcb = alog_t[j]
        mk.op("act", lambda e, src=src, t=t: e.activation(
            out=t[:], in_=src[:], func=AF.Exp), rd=(srcb,), wr=(tb,))
        mk.op("dve", lambda e, t=t: e.tensor_scalar_mul(t[:], t[:], -1.0),
              rd=(tb,), wr=(tb,))
        aneg.append((t, tb))

    brep, brep_b = cp.sb("brep", (128, 16 * 512), BF16)
    crep, crep_b = cp.sb("crep", (128, 16 * 512), BF16)
    stg = crep[:].bitcast(F32)[:, 0:2048]
    stg_b = crep_b
    stgb = brep[:, 0:4096]
    stgb_b = brep_b
    for j in range(8):
        for hh in range(2):
            mk.dma("sp", stg[:, 0:2048], Win[j * 128:(j + 1) * 128, hh * 2048:(hh + 1) * 2048],
                   rd=(stg_b,), wr=(stg_b,))
            mk.op("pool", lambda e, hh=hh: e.tensor_copy(
                out=stgb[:, hh * 2048:(hh + 1) * 2048], in_=stg[:, 0:2048]),
                rd=(stg_b, stgb_b), wr=(stgb_b,))
        mk.dma("sp", winb[j * 128:(j + 1) * 128, :], stgb[:],
               rd=(stgb_b,), wr=(winb_buf,))

    for j in range(16):
        mk.dma("sp", stg[:, 0:1024], Wout[j * 128:(j + 1) * 128, :],
               rd=(stg_b,), wr=(stg_b,))
        mk.op("pool", lambda e: e.tensor_copy(out=stgb[:, 0:1024], in_=stg[:, 0:1024]),
              rd=(stg_b, stgb_b), wr=(stgb_b,))
        mk.dma("sp", woutbh[j * 128:(j + 1) * 128, :], stgb[:, 0:1024],
               rd=(stgb_b,), wr=(woutbh_buf,))
    for j in range(16):
        mk.dma("sp", stg[:, 0:2048], Wdt[j * 128:(j + 1) * 128, :],
               rd=(stg_b,), wr=(stg_b,))
        mk.op("pool", lambda e: e.tensor_copy(out=stgb[:, 0:2048], in_=stg[:, 0:2048]),
              rd=(stg_b, stgb_b), wr=(stgb_b,))
        mk.dma("sp", wdtb[j * 128:(j + 1) * 128, :], stgb[:, 0:2048],
               rd=(stgb_b,), wr=(wdtb_buf,))

    WQ = TQ + 2
    Rt = [cp.sb(f"Rt{j}", (128, WQ), F32) for j in range(NT_D)]
    xn = [cp.sb(f"xn{j}", (128, WQ), BF16) for j in range(NT_D)]
    xb = [cp.sb(f"xb{j}", (128, TQ), BF16) for j in range(NT_E)]
    xzf, xzf_b = cp.sb("xzf", (128, WQ), F32)
    wstr = [cp.sb(f"wstr{i}", (128, 2048), BF16) for i in range(2)]
    bsb, bsb_b = cp.sb("bsb", (16, TQ), BF16)
    csb, csb_b = cp.sb("csb", (16, TQ), BF16)
    dtt, dtt_b = cp.sb("dtt", (128, TQ), F32)
    sps, sps_b = cp.sb("sps", (128, TQ), F32)
    spm, spm_b = cp.sb("spm", (128, TQ), F32)
    dtx, dtx_b = cp.sb("dtx", (128, TQ), BF16)
    a_scr = [cp.sb(f"ascr{i}", (128, TQ), F32) for i in range(2)]
    u_scr = [cp.sb(f"uscr{i}", (128, TQ), BF16) for i in range(2)]
    h_all, h_all_b = cp.sb("h_all", (128, 16 * TQ), BF16)
    yv, yv_b = cp.sb("yv", (128, TQ), F32)
    y2 = [cp.sb(f"y2{j}", (128, TQ), BF16) for j in range(NT_E)]
    zs, zs_b = cp.sb("zs", (128, TQ), BF16)
    t1s, t1s_b = cp.sb("t1s", (128, TQ), F32)
    rm, rm_b = cp.sb("rm", (128, TQ), F32)
    rout_s, rout_b = cp.sb("rout_s", (128, TQ), F32)

    for q in range(4):
        e0 = q * TQ
        for j in range(NT_D):
            mk.dma("sp", Rt[j][0][:], R[j * 128:(j + 1) * 128, e0:e0 + WQ],
                   rd=(Rt[j][1],), wr=(Rt[j][1],))
        arow, brow = emit_ln_stats(cp, Rt, WQ, (ones_bf[0][:], ones_bf[1]))
        if debug and q == 0:
            sr, sr_b = cp.scratch("ln_srow", (1, cp.lnw), F32)
            qr, qr_b = cp.scratch("ln_qrow", (1, cp.lnw), F32)
            mk.dma("sp", d_rows[0:1, 0:514], sr[0:1, 0:514], rd=(sr_b,), wr=())
            mk.dma("sp", d_rows[1:2, 0:514], qr[0:1, 0:514], rd=(qr_b,), wr=())
            mk.dma("sp", d_rows[2:3, 0:514], arow[0][0:1, 0:514], rd=(arow[1],), wr=())
            mk.dma("sp", d_rows[3:4, 0:514], brow[0][0:1, 0:514], rd=(brow[1],), wr=())
        emit_rowscale_apply(cp, Rt, Rt, WQ, arow, brow, (ones_f[0][:], ones_f[1]))
        arow2, brow2 = emit_ln_stats(cp, Rt, WQ, (ones_bf[0][:], ones_bf[1]))
        affine = [((nsb_t[j][0][:, 0:1], nsb_t[j][1]),
                   (nsb_t[j][0][:, 1:2], nsb_t[j][1])) for j in range(NT_D)]
        emit_rowscale_apply(cp, Rt, xn, WQ, arow2, brow2, (ones_f[0][:], ones_f[1]),
                            affine=affine)
        if debug and q == 0:
            for j in range(NT_D):
                mk.dma("sp", d_xn[j * 128:(j + 1) * 128, :], xn[j][0][:],
                       rd=(xn[j][1],), wr=())
                mk.dma("sp", d_xn0[j * 128:(j + 1) * 128, :], Rt[j][0][:],
                       rd=(Rt[j][1],), wr=())

        for ot in range(NT_E):
            ws, ws_b = wstr[ot % 2]
            mk.dma("sp", ws[:, 0:1024].rearrange("p (k c) -> p k c", k=8),
                   winb.rearrange("(k p) o -> p k o", p=128)[:, :, ot * 128:(ot + 1) * 128],
                   rd=(ws_b,), wr=(ws_b,))
            lhs = [(ws[:, k * 128:(k + 1) * 128], ws_b) for k in range(8)]
            rhs0 = [(xn[j][0][:, 0:512], xn[j][1]) for j in range(NT_D)]
            rhs1 = [(xn[j][0][:, 2:514], xn[j][1]) for j in range(NT_D)]
            cp.mm_acc(2, 128, 512, lhs, rhs0)
            cp.mm_acc(3, 128, 512, lhs, rhs1)
            pt2, pb2 = cp.pb[2]
            pt3, pb3 = cp.pb[3]
            mk.op("act", lambda e: e.activation(out=xzf[:, 0:512], in_=pt2[:, 0:512],
                                                func=AF.Copy), rd=(pb2,), wr=(xzf_b,))
            mk.op("act", lambda e: e.activation(out=xzf[:, 512:514], in_=pt3[:, 510:512],
                                                func=AF.Copy), rd=(pb3,), wr=(xzf_b,))
            cwj, cwj_b = cw_t[ot]
            cbj, cbj_b = vec_t[ot]
            xbj, xbj_b = xb[ot]
            mk.op("dve", lambda e, cwj=cwj, cbj=cbj: e.scalar_tensor_tensor(
                out=t1s[:], in0=xzf[:, 0:512], scalar=cwj[:, 0:1],
                in1=cbj[:, 0:1].broadcast_to([128, 512]), op0=ALU.mult, op1=ALU.add),
                rd=(xzf_b, cwj_b, cbj_b), wr=(t1s_b,))
            mk.op("dve", lambda e, cwj=cwj: e.scalar_tensor_tensor(
                out=t1s[:], in0=xzf[:, 1:513], scalar=cwj[:, 1:2],
                in1=t1s[:], op0=ALU.mult, op1=ALU.add),
                rd=(xzf_b, cwj_b, t1s_b), wr=(t1s_b,))
            mk.op("dve", lambda e, cwj=cwj: e.scalar_tensor_tensor(
                out=t1s[:], in0=xzf[:, 2:514], scalar=cwj[:, 2:3],
                in1=t1s[:], op0=ALU.mult, op1=ALU.add),
                rd=(xzf_b, cwj_b, t1s_b), wr=(t1s_b,))
            mk.op("act", lambda e, xbj=xbj: e.activation(
                out=xbj[:], in_=t1s[:], func=AF.Silu), rd=(t1s_b,), wr=(xbj_b,))
            if debug and q == 0:
                mk.dma("sp", d_xb[ot * 128:(ot + 1) * 128, :], xbj[:],
                       rd=(xbj_b,), wr=())

        for (wt, dst, dst_b) in ((wb_t, bsb, bsb_b), (wc_t, csb, csb_b)):
            lhs = [(wt[k][0][:], wt[k][1]) for k in range(NT_E)]
            rhs = [(xb[k][0][:], xb[k][1]) for k in range(NT_E)]
            pt, pbx = cp.mm_acc(0, 16, 512, lhs, rhs)
            mk.op("dve", lambda e, dst=dst, pt=pt: e.tensor_copy(
                out=dst[:], in_=pt[0:16, :]), rd=(pbx,), wr=(dst_b,))
            if debug and q == 0:
                mk.dma("sp", d_B if dst is bsb else d_C, dst[:],
                       rd=(dst_b,), wr=())
        selap, sel_b = sel_t[0]
        for n in range(16):
            for (src, src_b, dst, dst_b2) in ((bsb, bsb_b, brep, brep_b),
                                              (csb, csb_b, crep, crep_b)):
                pt, pbx = cp.pb[1]
                mk.op("pe", lambda e, n=n, src=src, pt=pt: e.matmul(
                    pt[:, 0:512], selap[:, n * 128:(n + 1) * 128], src[:],
                    start=True, stop=True), rd=(sel_b, src_b, pbx), wr=(pbx,))
                mk.op("dve", lambda e, n=n, dst=dst, pt=pt: e.tensor_copy(
                    out=dst[:, n * TQ:(n + 1) * TQ], in_=pt[:, 0:512]),
                    rd=(pbx,), wr=(dst_b2,))

        for j in range(NT_E):
            ws, ws_b = wstr[j % 2]
            mk.dma("sp", ws[:, 0:2048].rearrange("p (k c) -> p k c", k=16),
                   wdtb.rearrange("(k p) o -> p k o", p=128)[:, :, j * 128:(j + 1) * 128],
                   rd=(ws_b,), wr=(ws_b,))
            lhs = [(ws[:, k * 128:(k + 1) * 128], ws_b) for k in range(16)]
            rhs = [(xb[k][0][:], xb[k][1]) for k in range(NT_E)]
            ptd, pbd = cp.mm_acc(2, 128, 512, lhs, rhs)
            bdt_col = vec_t[j][0][:, 1:2]
            mk.op("dve", lambda e, bdt_col=bdt_col, ptd=ptd: e.tensor_scalar_add(
                sps[:], ptd[:, 0:512], bdt_col),
                rd=(pbd, vec_t[j][1]), wr=(sps_b,))
            mk.op("dve", lambda e: e.tensor_scalar_max(spm[:], sps[:], 0.0),
                  rd=(sps_b,), wr=(spm_b,))
            mk.op("dve", lambda e: e.scalar_tensor_tensor(
                out=sps[:], in0=sps[:], scalar=-1.0, in1=sps[:],
                op0=ALU.mult, op1=ALU.min), rd=(sps_b,), wr=(sps_b,))
            mk.op("act", lambda e: e.activation(out=sps[:], in_=sps[:], func=AF.Exp),
                  rd=(sps_b,), wr=(sps_b,))
            mk.op("act", lambda e: e.activation(out=sps[:], in_=sps[:], func=AF.Ln,
                                                bias=1.0), rd=(sps_b,), wr=(sps_b,))
            mk.op("dve", lambda e: e.tensor_add(dtt[:], spm[:], sps[:]),
                  rd=(spm_b, sps_b), wr=(dtt_b,))
            mk.op("dve", lambda e, j=j: e.tensor_mul(dtx[:], dtt[:], xb[j][0][:]),
                  rd=(dtt_b, xb[j][1]), wr=(dtx_b,))
            if debug and q == 0 and j == 0:
                mk.dma("sp", d_dt0, dtt[:], rd=(dtt_b,), wr=())
            if debug and q == 0:
                mk.dma("sp", d_dta[j * 128:(j + 1) * 128, :], dtt[:],
                       rd=(dtt_b,), wr=())
            ws2, ws2_b = wstr[(j + 1) % 2]
            mk.dma("sp", ws2[:, 0:1024].rearrange("p (k c) -> p k c", k=8),
                   winb.rearrange("(k p) o -> p k o", p=128)[:, :, 2048 + j * 128:2048 + (j + 1) * 128],
                   rd=(ws2_b,), wr=(ws2_b,))
            lhsz = [(ws2[:, k * 128:(k + 1) * 128], ws2_b) for k in range(8)]
            rhsz = [(xn[jj][0][:, MAIN0:MAIN0 + 512], xn[jj][1]) for jj in range(NT_D)]
            ptz, pbz = cp.mm_acc(3, 128, 512, lhsz, rhsz)
            mk.op("act", lambda e: e.activation(out=zs[:], in_=ptz[:, 0:512],
                                                func=AF.Silu), rd=(pbz,), wr=(zs_b,))
            if debug and q == 0 and j == 0:
                mk.dma("sp", d_z0, zs[:], rd=(zs_b,), wr=())
            if debug and q == 0:
                mk.dma("sp", d_za[j * 128:(j + 1) * 128, :], zs[:],
                       rd=(zs_b,), wr=())
            for n in range(16):
                asc, asc_b = a_scr[n % 2]
                usc, usc_b = u_scr[n % 2]
                an_col = aneg[j][0][:, n:n + 1]
                mk.op("act", lambda e, an_col=an_col, asc=asc: e.activation(
                    out=asc[:], in_=dtt[:], func=AF.Exp, scale=an_col),
                    rd=(dtt_b, aneg[j][1]), wr=(asc_b,))
                mk.op("pool", lambda e, n=n, usc=usc: e.tensor_tensor(
                    out=usc[:], in0=dtx[:], in1=brep[:, n * TQ:(n + 1) * TQ],
                    op=ALU.mult), rd=(dtx_b, brep_b), wr=(usc_b,))
                hj, hj_b = hcur[j]
                mk.op("dve", lambda e, n=n, asc=asc, usc=usc, hj=hj: e.tensor_tensor_scan(
                    out=h_all[:, n * TQ:(n + 1) * TQ], data0=asc[:], data1=usc[:],
                    initial=hj[:, n:n + 1], op0=ALU.mult, op1=ALU.add),
                    rd=(asc_b, usc_b, hj_b), wr=(h_all_b,))
            hj, hj_b = hcur[j]
            for n in range(16):
                mk.op("dve", lambda e, hj=hj, n=n: e.tensor_copy(
                    out=hj[:, n:n + 1],
                    in_=h_all[:, (n + 1) * TQ - 1:(n + 1) * TQ]),
                    rd=(h_all_b,), wr=(hj_b,))
            mk.op("pool", lambda e: e.tensor_tensor(
                out=h_all[:], in0=h_all[:], in1=crep[:], op=ALU.mult),
                rd=(h_all_b, crep_b), wr=(h_all_b,))
            mk.op("dve", lambda e: e.tensor_reduce(
                out=yv[:], in_=h_all[:].rearrange("p (n t) -> p t n", n=16),
                axis=AX.X, op=ALU.add), rd=(h_all_b,), wr=(yv_b,))
            if debug and q == 0 and j == 0:
                mk.dma("sp", d_y0, yv[:], rd=(yv_b,), wr=())
            d_col = vec_t[j][0][:, 2:3]
            mk.op("dve", lambda e, j=j, d_col=d_col: e.scalar_tensor_tensor(
                out=t1s[:], in0=xb[j][0][:], scalar=d_col, in1=yv[:],
                op0=ALU.mult, op1=ALU.add), rd=(xb[j][1], vec_t[j][1], yv_b),
                wr=(t1s_b,))
            mk.op("dve", lambda e, j=j: e.tensor_mul(y2[j][0][:], t1s[:], zs[:]),
                  rd=(t1s_b, zs_b), wr=(y2[j][1],))
            if debug and q == 0 and j == 0:
                mk.dma("sp", d_y20, y2[j][0][:], rd=(y2[j][1],), wr=())
            if debug and q == 0:
                mk.dma("sp", d_y2a[j * 128:(j + 1) * 128, :], y2[j][0][:],
                       rd=(y2[j][1],), wr=())

        for k in range(NT_E):
            wk, wk_b = wstr[k % 2]
            mk.dma("sp", wk[:, 0:1024], woutbh[k * 128:(k + 1) * 128, :],
                   rd=(wk_b, woutbh_buf), wr=(wk_b,))
            for ot in range(8):
                pt, pbx = cp.pb[ot]
                mk.op("pe", lambda e, k=k, ot=ot, wk=wk, pt=pt: e.matmul(
                    pt[:], wk[:, ot * 128:(ot + 1) * 128], y2[k][0][:],
                    start=(k == 0), stop=(k == NT_E - 1)),
                    rd=(wk_b, y2[k][1]) + ((pbx,) if k == 0 else ()), wr=(pbx,))
        for ot in range(8):
            pt, pbx = cp.pb[ot]
            mk.dma("sp", rm[:], R[ot * 128:(ot + 1) * 128, MAIN0 + e0:MAIN0 + e0 + TQ],
                   rd=(rm_b,), wr=(rm_b,))
            mk.op("dve", lambda e, pt=pt: e.tensor_add(rout_s[:], pt[:], rm[:]),
                  rd=(pbx, rm_b), wr=(rout_b,))
            mk.op("dve", lambda e, ot=ot: e.tensor_add(
                rout_s[:], rout_s[:], Rt[ot][0][:, 1:1 + TQ]),
                rd=(rout_b, Rt[ot][1]), wr=(rout_b,))
            mk.dma("sp", Rout[ot * 128:(ot + 1) * 128, e0:e0 + TQ], rout_s[:],
                   rd=(rout_b,), wr=())

    for j in range(NT_E):
        hj, hj_b = hcur[j]
        mk.dma("sp", hout[j * 128:(j + 1) * 128, :], hj[:], rd=(hj_b,), wr=())

    _CONST_VALS["l0a"] = list(cp.const_vals)
    return cp.mk.finalize()



_BUILD_CACHE = {}
_CONST_VALS = {}


def _get_built(name, builder):
    if name not in _BUILD_CACHE:
        _BUILD_CACHE[name] = builder()
    return _BUILD_CACHE[name]


def _cconst_arr(name):
    vals = _CONST_VALS.get(name, [])
    a = np.zeros((128, 16), np.float32)
    for i, v in enumerate(vals):
        a[:, i] = v
    return a


def _sel16():
    import ml_dtypes
    sel = np.zeros((16, 2048), ml_dtypes.bfloat16)
    for n in range(16):
        sel[n, n * 128:(n + 1) * 128] = 1.0
    return sel


def _feat_major_cores(x):
    out = []
    for b in range(4):
        for h in range(2):
            sl = x[b, h * 2048:(h + 1) * 2048, :]
            out.append(np.ascontiguousarray(sl.T.astype(np.float32)))
    return out


def _cores_to_full(Rs):
    x = np.zeros((4, 4096, 1024), np.float32)
    for b in range(4):
        for h in range(2):
            x[b, h * 2048:(h + 1) * 2048, :] = Rs[b * 2 + h].T
    return x


def _run_l0a(Rs, P):
    nc = _get_built("l0a", build_l0a)
    vecs = np.stack([P["conv_b"], P["b_dt"], P["D_param"]], axis=1).astype(np.float32)
    nsb = np.stack([P["norm_scale"], P["norm_bias"]], axis=1).astype(np.float32)
    base = dict(Win=P["W_in"], Wdt=P["W_dt"], WB=P["W_B"], WC=P["W_C"],
                Wout=P["W_out"], Alog=P["A_log"], vecs=vecs,
                convw=P["conv_w"], nsb=nsb, sel16=_sel16(),
                cconst=_cconst_arr("l0a"))
    base = {k: np.ascontiguousarray(v) for k, v in base.items()}

    def mk_maps(houts):
        maps = []
        for c in range(8):
            h = c % 2
            Rext = np.zeros((1024, 2112), np.float32)
            Rext[:, 1:2049] = Rs[c]
            if h == 1:
                Rext[:, 0] = Rs[c - 1][:, -1]
            if h == 0:
                Rext[:, 2049] = Rs[c + 1][:, 0]
            hin = (houts[c - 1] if (h == 1 and houts is not None)
                   else np.zeros((2048, 16), np.float32))
            m = dict(base)
            m["R"] = Rext
            m["hinit"] = hin
            maps.append(m)
        return maps

    r1 = run_bass_kernel_spmd(nc, mk_maps(None), core_ids=list(range(8)))
    houts = [r1.results[c]["hout"] for c in range(8)]
    r2 = run_bass_kernel_spmd(nc, mk_maps(houts), core_ids=list(range(8)))
    out = []
    for c in range(8):
        res = r1 if c % 2 == 0 else r2
        out.append(np.asarray(res.results[c]["Rout"], np.float32))
    return out



def stage_weight_hbm(cp, hbm_in, rows, cols, name, transpose=False,
                     stage=None, ident=None, quant=None):
    mk = cp.mk
    nc = cp.nc
    shape = [cols, rows] if transpose else [rows, cols]
    wout = nc.dram_tensor(name, shape, BF16, kind="Internal").ap()
    wbuf = mk.buf(name)
    (stg, stg_b), (stgb, stgb_b) = stage
    nt = ceil_div(rows, 128)
    for j in range(nt):
        r = min(128, rows - j * 128)
        mk.dma("sp", stg[0:r, 0:cols], hbm_in[j * 128:j * 128 + r, :],
               rd=(stg_b,), wr=(stg_b,))
        if quant is not None:
            inv_ap, inv_b = quant
            t_q, t_qb = cp.scratch("quant_t", (128, 4096), F32)
            g1, g1b = cp.scratch("quant_g1", (128, 4096), F32)
            mk.op("dve", lambda e, r=r: e.tensor_scalar_mul(
                t_q[0:r, 0:cols], stg[0:r, 0:cols], inv_ap),
                rd=(stg_b, inv_b), wr=(t_qb,))
            mk.op("dve", lambda e, r=r: e.tensor_single_scalar(
                out=g1[0:r, 0:cols], in_=t_q[0:r, 0:cols], scalar=0.5,
                op=ALU.is_ge), rd=(t_qb,), wr=(g1b,))
            mk.op("dve", lambda e, r=r: e.tensor_single_scalar(
                out=t_q[0:r, 0:cols], in_=t_q[0:r, 0:cols], scalar=-0.5,
                op=ALU.is_le), rd=(t_qb,), wr=(t_qb,))
            mk.op("dve", lambda e, r=r: e.tensor_sub(
                stgb[0:r, 0:cols], g1[0:r, 0:cols], t_q[0:r, 0:cols]),
                rd=(g1b, t_qb), wr=(stgb_b,))
        else:
            mk.op("pool", lambda e, r=r: e.tensor_copy(
                out=stgb[0:r, 0:cols], in_=stg[0:r, 0:cols]),
                rd=(stg_b, stgb_b), wr=(stgb_b,))
        if not transpose:
            mk.dma("sp", wout[j * 128:j * 128 + r, :], stgb[0:r, 0:cols],
                   rd=(stgb_b,), wr=(wbuf,))
        else:
            idap, idbuf = ident
            ptb_f, ptb_b = cp.pb[7]
            ptb = ptb_f[:].bitcast(BF16)
            tout, tout_b = cp.scratch("tr_out", (128, 128), BF16)
            for c in range(ceil_div(cols, 128)):
                cw = min(128, cols - c * 128)
                mk.op("pe", lambda e, c=c, cw=cw, r=r: e.transpose(
                    ptb[0:cw, 0:r], stgb[0:r, c * 128:c * 128 + cw],
                    idap[0:r, 0:r]),
                    rd=(stgb_b, idbuf, ptb_b), wr=(ptb_b,))
                mk.op("dve", lambda e, cw=cw, r=r: e.tensor_copy(
                    out=tout[0:cw, 0:r], in_=ptb[0:cw, 0:r]),
                    rd=(ptb_b,), wr=(tout_b,))
                mk.dma("sp", wout[c * 128:c * 128 + cw, j * 128:j * 128 + r],
                       tout[0:cw, 0:r], rd=(tout_b,), wr=(wbuf,))
    return wout, wbuf



def build_l1ffn(debug=False):
    cp = CP("l1ffn")
    cp.lnw = 520
    mk = cp.mk
    TQ = 512

    R = cp.din("R", (1024, 2048))
    W1 = cp.din("W1", (4096, 1024))
    W2 = cp.din("W2", (1024, 4096))
    b1 = cp.din("b1", (4096, 1))
    b2 = cp.din("b2", (1024, 1))
    ident = cp.din("ident", (128, 128), BF16)
    Rout = cp.dout("Rout", (1024, 2048))

    ones_bf = cp.sb("ones_bf", (128, 1), BF16)
    mk.op("dve", lambda e: e.memset(ones_bf[0][:], 1.0), wr=(ones_bf[1],))
    ones_f = cp.sb("ones_f", (1, 128), F32)
    mk.op("dve", lambda e: e.memset(ones_f[0][:], 1.0), wr=(ones_f[1],))

    id_t = cp.load_tiles(ident, 128, 128, dtype_out=BF16, name="id", direct=True)
    b1_t = cp.load_tiles(b1, 4096, 1, dtype_out=F32, name="b1t")
    b2_t = cp.load_tiles(b2, 1024, 1, dtype_out=F32, name="b2t")

    stg = cp.sb("stg", (128, 4096), F32)
    stgb = cp.sb("stgb", (128, 4096), BF16)

    def mean_abs(W, rows, cols, accname):
        acc, acc_b = cp.sb(accname, (128, 1), F32)
        nt = rows // 128
        parts, parts_b = cp.scratch("ma_parts", (128, 32), F32)
        lvl1, lvl1_b = cp.scratch("ma_lvl1", (128, 32), F32)
        g = cols // 128
        for j in range(nt):
            mk.dma("sp", stg[0][:, 0:cols], W[j * 128:(j + 1) * 128, :],
                   rd=(stg[1],), wr=(stg[1],))
            mk.op("dve", lambda e, g=g: e.tensor_reduce(
                out=lvl1[:, 0:g],
                in_=stg[0][:, 0:cols].rearrange("p (g c) -> p g c", c=128),
                axis=AX.X, op=ALU.add, apply_absolute_value=True),
                rd=(stg[1],), wr=(lvl1_b,))
            mk.op("dve", lambda e, j=j, g=g: e.tensor_reduce(
                out=parts[:, j:j + 1],
                in_=lvl1[:, 0:g].rearrange("p (a c) -> p a c", a=1),
                axis=AX.X, op=ALU.add), rd=(lvl1_b,), wr=(parts_b,))
        mk.op("dve", lambda e, nt=nt: e.tensor_reduce(
            out=lvl1[:, 0:4],
            in_=parts[:, 0:nt].rearrange("p (a c) -> p a c", a=4)
            if nt % 4 == 0 else parts[:, 0:nt].rearrange("p (a c) -> p a c", a=1),
            axis=AX.X, op=ALU.add), rd=(parts_b,), wr=(lvl1_b,))
        if nt % 4 == 0:
            mk.op("dve", lambda e: e.tensor_reduce(
                out=acc[:], in_=lvl1[:, 0:4].rearrange("p (a c) -> p a c", a=1),
                axis=AX.X, op=ALU.add), rd=(lvl1_b,), wr=(acc_b,))
        else:
            mk.op("dve", lambda e: e.tensor_copy(out=acc[:], in_=lvl1[:, 0:1]),
                  rd=(lvl1_b,), wr=(acc_b,))
        pscr = cp.nc.dram_tensor(accname + "scr", [128], F32, kind="Internal").ap()
        pscr_b = mk.buf(accname + "scr")
        mk.dma("sp", pscr, acc[:], rd=(acc_b,), wr=(pscr_b,),
               allow_slow_non_contiguous=True)
        arow, arow_b = cp.sb(accname + "r", (1, 128), F32)
        mk.dma("sp", arow[:], pscr.rearrange("(a c) -> a c", a=1),
               rd=(pscr_b,), wr=(arow_b,))
        a8, a8_b = cp.sb(accname + "r8", (1, 8), F32)
        mk.op("dve", lambda e: e.tensor_reduce(
            out=a8[:], in_=arow[:].rearrange("p (g c) -> p g c", c=16),
            axis=AX.X, op=ALU.add), rd=(arow_b,), wr=(a8_b,))
        g1, g1_b = cp.sb(accname + "g", (1, 1), F32)
        mk.op("dve", lambda e: e.tensor_reduce(
            out=g1[:], in_=a8[:].rearrange("p (g c) -> p g c", g=1),
            axis=AX.X, op=ALU.add), rd=(a8_b,), wr=(g1_b,))
        mk.op("dve", lambda e: e.tensor_scalar_mul(g1[:], g1[:],
                                                   1.0 / (rows * cols)),
              rd=(g1_b,), wr=(g1_b,))
        mk.op("dve", lambda e: e.tensor_scalar_add(g1[:], g1[:], 1e-5),
              rd=(g1_b,), wr=(g1_b,))
        mk.op("dve", lambda e: e.reciprocal(g1[:], g1[:]), rd=(g1_b,), wr=(g1_b,))
        pt1, pbx1 = cp.pb[1]
        mk.op("pe", lambda e: e.matmul(pt1[:, 0:1], ones_f[0][:], g1[:],
                                       start=True, stop=True),
              rd=(ones_f[1], g1_b, pbx1), wr=(pbx1,))
        inv, inv_b = cp.sb(accname + "i", (128, 1), F32)
        mk.op("dve", lambda e: e.tensor_copy(out=inv[:], in_=pt1[:, 0:1]),
              rd=(pbx1,), wr=(inv_b,))
        return inv, inv_b

    inv1 = mean_abs(W1, 4096, 1024, "ma1")
    inv2 = mean_abs(W2, 1024, 4096, "ma2")

    wq1t, wq1t_b = stage_weight_hbm(cp, W1, 4096, 1024, "wq1t", transpose=True,
                                    stage=(stg, stgb), ident=(id_t[0][0][:], id_t[0][1]),
                                    quant=(inv1[0][:], inv1[1]))
    wq2t, wq2t_b = stage_weight_hbm(cp, W2, 1024, 4096, "wq2t", transpose=True,
                                    stage=(stg, stgb), ident=(id_t[0][0][:], id_t[0][1]),
                                    quant=(inv2[0][:], inv2[1]))

    Rt = [cp.sb(f"Rt{j}", (128, TQ), F32) for j in range(8)]
    xn = [cp.sb(f"xn{j}", (128, TQ), BF16) for j in range(8)]
    hbuf = [cp.sb(f"hb{j}", (128, TQ), BF16) for j in range(32)]
    wstr = [cp.sb(f"wstr{i}", (128, 2048), BF16) for i in range(2)]
    tt1, tt1_b = cp.sb("tt1", (128, TQ), F32)
    routs, routs_b = cp.sb("routs", (128, TQ), F32)

    for q in range(4):
        e0 = q * TQ
        for j in range(8):
            mk.dma("sp", Rt[j][0][:], R[j * 128:(j + 1) * 128, e0:e0 + TQ],
                   rd=(Rt[j][1],), wr=(Rt[j][1],))
        arow, brow = emit_ln_stats(cp, Rt, TQ, (ones_bf[0][:], ones_bf[1]))
        emit_rowscale_apply(cp, Rt, xn, TQ, arow, brow, (ones_f[0][:], ones_f[1]))
        for ot in range(32):
            ws, ws_b = wstr[ot % 2]
            mk.dma("sp", ws[:, 0:1024].rearrange("p (k c) -> p k c", k=8),
                   wq1t.rearrange("(k p) o -> p k o", p=128)[:, :, ot * 128:(ot + 1) * 128],
                   rd=(ws_b, wq1t_b), wr=(ws_b,))
            lhs = [(ws[:, k * 128:(k + 1) * 128], ws_b) for k in range(8)]
            rhs = [(xn[k][0][:], xn[k][1]) for k in range(8)]
            pt, pbx = cp.mm_acc(2 + ot % 2, 128, TQ, lhs, rhs)
            bcol = b1_t[ot][0][:, 0:1]
            mk.op("dve", lambda e, pt=pt, bcol=bcol, ot=ot: e.tensor_scalar_add(
                tt1[:], pt[:, 0:TQ], bcol), rd=(pbx, b1_t[ot][1]), wr=(tt1_b,))
            mk.op("act", lambda e, ot=ot: e.activation(
                out=hbuf[ot][0][:], in_=tt1[:], func=AF.Relu),
                rd=(tt1_b,), wr=(hbuf[ot][1],))
        for ot in range(8):
            ws, ws_b = wstr[ot % 2]
            mk.dma("sp", ws[:, 0:2048].rearrange("p (g c) -> p g c", g=16),
                   wq2t.rearrange("(k p) o -> p k o", p=128)[:, 0:16, ot * 128:(ot + 1) * 128],
                   rd=(ws_b, wq2t_b), wr=(ws_b,))
            ws2, ws2_b = wstr[(ot + 1) % 2]
            mk.dma("sp", ws2[:, 0:2048].rearrange("p (k c) -> p k c", k=16),
                   wq2t.rearrange("(k p) o -> p k o", p=128)[:, 16:32, ot * 128:(ot + 1) * 128],
                   rd=(ws2_b, wq2t_b), wr=(ws2_b,))
            lhs = [(ws[:, k * 128:(k + 1) * 128], ws_b) for k in range(16)] + \
                  [(ws2[:, k * 128:(k + 1) * 128], ws2_b) for k in range(16)]
            rhs = [(hbuf[k][0][:], hbuf[k][1]) for k in range(32)]
            pt, pbx = cp.mm_acc(4 + ot % 4, 128, TQ, lhs, rhs)
            bcol = b2_t[ot][0][:, 0:1]
            mk.op("dve", lambda e, pt=pt, bcol=bcol, ot=ot: e.tensor_scalar_add(
                routs[:], pt[:, 0:TQ], bcol), rd=(pbx, b2_t[ot][1]), wr=(routs_b,))
            mk.op("dve", lambda e, ot=ot: e.tensor_add(
                routs[:], routs[:], Rt[ot][0][:]),
                rd=(routs_b, Rt[ot][1]), wr=(routs_b,))
            mk.dma("sp", Rout[ot * 128:(ot + 1) * 128, e0:e0 + TQ], routs[:],
                   rd=(routs_b,), wr=())

    if debug:
        dq1 = cp.dout("dq1", (1024, 4096), BF16)
        dq2 = cp.dout("dq2", (4096, 1024), BF16)
        dinv1 = cp.dout("dinv1", (128, 1))
        dinv2 = cp.dout("dinv2", (128, 1))
        dbg, dbg_b = cp.sb("dbgw", (128, 4096), BF16)
        mk.dma("sp", dinv1, inv1[0][:], rd=(inv1[1],), wr=())
        mk.dma("sp", dinv2, inv2[0][:], rd=(inv2[1],), wr=())
        for j in range(8):
            mk.dma("sp", dbg[:, 0:4096], wq1t[j * 128:(j + 1) * 128, :],
                   rd=(dbg_b, wq1t_b), wr=(dbg_b,))
            mk.dma("sp", dq1[j * 128:(j + 1) * 128, :], dbg[:, 0:4096],
                   rd=(dbg_b,), wr=(dbg_b,))
        for j in range(32):
            mk.dma("sp", dbg[:, 0:1024], wq2t[j * 128:(j + 1) * 128, :],
                   rd=(dbg_b, wq2t_b), wr=(dbg_b,))
            mk.dma("sp", dq2[j * 128:(j + 1) * 128, :], dbg[:, 0:1024],
                   rd=(dbg_b,), wr=(dbg_b,))

    _CONST_VALS["l1ffn"] = list(cp.const_vals)
    return cp.mk.finalize()


def _ident128():
    import ml_dtypes
    return np.eye(128, dtype=ml_dtypes.bfloat16)


def _sub_params(name, inputs):
    pref = {"l0a": "L0_attn_a_", "l0ffn": "L0_ffn_", "l0b": "L0_attn_b_",
            "l1a": "L1_attn_a_", "l1ffn": "L1_ffn_", "l1b": "L1_attn_b_"}[name]
    return {k[len(pref):]: np.asarray(v) for k, v in inputs.items()
            if k.startswith(pref) and k != "L0_attn_a_x"}


def _run_l1ffn(Rs, P):
    nc = _get_built("l1ffn", build_l1ffn)
    base = dict(W1=P["W1"], W2=P["W2"], b1=P["b1"].reshape(-1, 1),
                b2=P["b2"].reshape(-1, 1), ident=_ident128(),
                cconst=_cconst_arr("l1ffn"))
    base = {k: np.ascontiguousarray(v) for k, v in base.items()}
    maps = []
    for c in range(8):
        m = dict(base)
        m["R"] = np.ascontiguousarray(Rs[c])
        maps.append(m)
    res = run_bass_kernel_spmd(nc, maps, core_ids=list(range(8)))
    return [np.asarray(res.results[c]["Rout"], np.float32) for c in range(8)]


def build_l0b():
    cp = CP("l0b")
    cp.lnw = 520
    mk = cp.mk
    TQ = 512
    NT_D, NT_E = 8, 16
    MAIN0 = 3
    WQ = TQ + 3

    R = cp.din("R", (1024, 2112))
    skipw = cp.din("skipw", (2048, 1024))
    inw = cp.din("inw", (2048, 1024))
    wd1 = cp.din("wd1", (64, 2048))
    wd2 = cp.din("wd2", (2048, 64))
    wB = cp.din("wB", (16, 2048))
    wC = cp.din("wC", (16, 2048))
    outw = cp.din("outw", (1024, 2048))
    Alog = cp.din("Alog", (2048, 16))
    vecs = cp.din("vecs", (2048, 3))
    convw = cp.din("convw", (2048, 4))
    nrmw = cp.din("nrmw", (1024, 1))
    hinit = cp.din("hinit", (2048, 16))
    sel16 = cp.din("sel16", (16, 2048), BF16)
    ident = cp.din("ident", (128, 128), BF16)
    Rout = cp.dout("Rout", (1024, 2048))
    hout = cp.dout("hout", (2048, 16))

    ones_bf = cp.sb("ones_bf", (128, 1), BF16)
    mk.op("dve", lambda e: e.memset(ones_bf[0][:], 1.0), wr=(ones_bf[1],))
    ones_f = cp.sb("ones_f", (1, 128), F32)
    mk.op("dve", lambda e: e.memset(ones_f[0][:], 1.0), wr=(ones_f[1],))

    vec_t = cp.load_tiles(vecs, 2048, 3, dtype_out=F32, name="vec")
    cw_t = cp.load_tiles(convw, 2048, 4, dtype_out=F32, name="cw")
    nw_t = cp.load_tiles(nrmw, 1024, 1, dtype_out=F32, name="nw")
    alog_t = cp.load_tiles(Alog, 2048, 16, dtype_out=F32, name="alog")
    hcur = cp.load_tiles(hinit, 2048, 16, dtype_out=F32, name="hcur")
    sel_t = cp.load_tiles(sel16, 16, 2048, dtype_out=BF16, name="sel", direct=True)
    id_t = cp.load_tiles(ident, 128, 128, dtype_out=BF16, name="id", direct=True)

    aneg = []
    for j in range(NT_E):
        t, tb = cp.sb(f"aneg{j}", (128, 16), F32)
        src, srcb = alog_t[j]
        mk.op("act", lambda e, src=src, t=t: e.activation(
            out=t[:], in_=src[:], func=AF.Exp), rd=(srcb,), wr=(tb,))
        mk.op("dve", lambda e, t=t: e.tensor_scalar_mul(t[:], t[:], -1.0),
              rd=(tb,), wr=(tb,))
        aneg.append((t, tb))

    brep, brep_b = cp.sb("brep", (128, 16 * 512), BF16)
    crep, crep_b = cp.sb("crep", (128, 16 * 512), BF16)
    stg = crep[:].bitcast(F32)[:, 0:2048]
    stg_b = crep_b
    stgb = brep[:, 0:4096]
    stgb_b = brep_b

    ident_pair = (id_t[0][0][:], id_t[0][1])
    stage = ((stg, stg_b), (stgb, stgb_b))
    skipwT, skipwT_b = stage_weight_hbm(cp, skipw, 2048, 1024, "skipwT",
                                        transpose=True, stage=stage, ident=ident_pair)
    inwT, inwT_b = stage_weight_hbm(cp, inw, 2048, 1024, "inwT",
                                    transpose=True, stage=stage, ident=ident_pair)
    outwT, outwT_b = stage_weight_hbm(cp, outw, 1024, 2048, "outwT",
                                      transpose=True, stage=stage, ident=ident_pair)
    wd1T, wd1T_b = stage_weight_hbm(cp, wd1, 64, 2048, "wd1T",
                                    transpose=True, stage=stage, ident=ident_pair)
    wd2T, wd2T_b = stage_weight_hbm(cp, wd2, 2048, 64, "wd2T",
                                    transpose=True, stage=stage, ident=ident_pair)
    wBT, wBT_b = stage_weight_hbm(cp, wB, 16, 2048, "wBT",
                                  transpose=True, stage=stage, ident=ident_pair)
    wCT, wCT_b = stage_weight_hbm(cp, wC, 16, 2048, "wCT",
                                  transpose=True, stage=stage, ident=ident_pair)

    wd1s = cp.load_tiles(wd1T, 2048, 64, dtype_out=BF16, name="wd1s", direct=True)
    wd2s = cp.load_tiles(wd2T, 64, 2048, dtype_out=BF16, name="wd2s", direct=True)
    wbs = cp.load_tiles(wBT, 2048, 16, dtype_out=BF16, name="wbs", direct=True)
    wcs = cp.load_tiles(wCT, 2048, 16, dtype_out=BF16, name="wcs", direct=True)

    Rt = [cp.sb(f"Rt{j}", (128, WQ), F32) for j in range(NT_D)]
    xn = [cp.sb(f"xn{j}", (128, WQ), BF16) for j in range(NT_D)]
    xb = [cp.sb(f"xb{j}", (128, TQ), BF16) for j in range(NT_E)]
    xzf, xzf_b = cp.sb("xzf", (128, WQ), F32)
    wstr = [cp.sb(f"wstr{i}", (128, 2048), BF16) for i in range(2)]
    bsb, bsb_b = cp.sb("bsb", (16, TQ), BF16)
    csb, csb_b = cp.sb("csb", (16, TQ), BF16)
    rsb, rsb_b = cp.sb("rsb", (64, TQ), BF16)
    dtt, dtt_b = cp.sb("dtt", (128, TQ), F32)
    sps, sps_b = cp.sb("sps", (128, TQ), F32)
    spm, spm_b = cp.sb("spm", (128, TQ), F32)
    dtx, dtx_b = cp.sb("dtx", (128, TQ), BF16)
    a_scr = [cp.sb(f"ascr{i}", (128, TQ), F32) for i in range(2)]
    u_scr = [cp.sb(f"uscr{i}", (128, TQ), BF16) for i in range(2)]
    h_all, h_all_b = cp.sb("h_all", (128, 16 * TQ), BF16)
    yv, yv_b = cp.sb("yv", (128, TQ), F32)
    y2 = [cp.sb(f"y2{j}", (128, TQ), BF16) for j in range(NT_E)]
    zs, zs_b = cp.sb("zs", (128, TQ), BF16)
    t1s, t1s_b = cp.sb("t1s", (128, TQ), F32)
    rm, rm_b = cp.sb("rm", (128, TQ), F32)
    rout_s, rout_b = cp.sb("rout_s", (128, TQ), F32)

    for q in range(4):
        e0 = q * TQ
        for j in range(NT_D):
            mk.dma("sp", Rt[j][0][:], R[j * 128:(j + 1) * 128, e0:e0 + WQ],
                   rd=(Rt[j][1],), wr=(Rt[j][1],))
        arow0, brow0 = emit_ln_stats(cp, Rt, WQ, (ones_bf[0][:], ones_bf[1]))
        emit_rowscale_apply(cp, Rt, Rt, WQ, arow0, brow0,
                            (ones_f[0][:], ones_f[1]))
        arow, brow = emit_ln_stats(cp, Rt, WQ, (ones_bf[0][:], ones_bf[1]),
                                   mean_free=True)
        affine = [((nw_t[j][0][:, 0:1], nw_t[j][1]), cp.const_col(0.0))
                  for j in range(NT_D)]
        emit_rowscale_apply(cp, Rt, xn, WQ, arow, brow, (ones_f[0][:], ones_f[1]),
                            affine=affine)
        for ot in range(NT_E):
            ws, ws_b = wstr[ot % 2]
            mk.dma("sp", ws[:, 0:1024].rearrange("p (k c) -> p k c", k=8),
                   inwT.rearrange("(k p) o -> p k o", p=128)[:, :, ot * 128:(ot + 1) * 128],
                   rd=(ws_b, inwT_b), wr=(ws_b,))
            lhs = [(ws[:, k * 128:(k + 1) * 128], ws_b) for k in range(8)]
            rhs0 = [(xn[j][0][:, 0:512], xn[j][1]) for j in range(NT_D)]
            rhs1 = [(xn[j][0][:, 3:515], xn[j][1]) for j in range(NT_D)]
            cp.mm_acc(2, 128, 512, lhs, rhs0)
            cp.mm_acc(3, 128, 512, lhs, rhs1)
            pt2, pb2 = cp.pb[2]
            pt3, pb3 = cp.pb[3]
            mk.op("act", lambda e: e.activation(out=xzf[:, 0:512], in_=pt2[:, 0:512],
                                                func=AF.Copy), rd=(pb2,), wr=(xzf_b,))
            mk.op("act", lambda e: e.activation(out=xzf[:, 512:515], in_=pt3[:, 509:512],
                                                func=AF.Copy), rd=(pb3,), wr=(xzf_b,))
            cwj, cwj_b = cw_t[ot]
            cbj, cbj_b = vec_t[ot]
            xbj, xbj_b = xb[ot]
            mk.op("dve", lambda e, cwj=cwj, cbj=cbj: e.scalar_tensor_tensor(
                out=t1s[:], in0=xzf[:, 0:512], scalar=cwj[:, 0:1],
                in1=cbj[:, 0:1].broadcast_to([128, 512]), op0=ALU.mult, op1=ALU.add),
                rd=(xzf_b, cwj_b, cbj_b), wr=(t1s_b,))
            for kk in range(1, 4):
                mk.op("dve", lambda e, cwj=cwj, kk=kk: e.scalar_tensor_tensor(
                    out=t1s[:], in0=xzf[:, kk:kk + 512], scalar=cwj[:, kk:kk + 1],
                    in1=t1s[:], op0=ALU.mult, op1=ALU.add),
                    rd=(xzf_b, cwj_b, t1s_b), wr=(t1s_b,))
            mk.op("act", lambda e, xbj=xbj: e.activation(
                out=xbj[:], in_=t1s[:], func=AF.Silu), rd=(t1s_b,), wr=(xbj_b,))
        lhs = [(wd1s[k][0][:], wd1s[k][1]) for k in range(NT_E)]
        rhs = [(xb[k][0][:], xb[k][1]) for k in range(NT_E)]
        pt, pbx = cp.mm_acc(0, 64, 512, lhs, rhs)
        mk.op("dve", lambda e, pt=pt: e.tensor_copy(out=rsb[:], in_=pt[0:64, :]),
              rd=(pbx,), wr=(rsb_b,))
        for (wt, dst, dst_b) in ((wbs, bsb, bsb_b), (wcs, csb, csb_b)):
            lhs = [(wt[k][0][:], wt[k][1]) for k in range(NT_E)]
            pt, pbx = cp.mm_acc(0, 16, 512, lhs, rhs)
            mk.op("dve", lambda e, dst=dst, pt=pt: e.tensor_copy(
                out=dst[:], in_=pt[0:16, :]), rd=(pbx,), wr=(dst_b,))
        selap, sel_b = sel_t[0]
        for n in range(16):
            for (src, src_b, dst, dst_b2) in ((bsb, bsb_b, brep, brep_b),
                                              (csb, csb_b, crep, crep_b)):
                pt, pbx = cp.pb[1]
                mk.op("pe", lambda e, n=n, src=src, pt=pt: e.matmul(
                    pt[:, 0:512], selap[:, n * 128:(n + 1) * 128], src[:],
                    start=True, stop=True), rd=(sel_b, src_b, pbx), wr=(pbx,))
                mk.op("dve", lambda e, n=n, dst=dst, pt=pt: e.tensor_copy(
                    out=dst[:, n * TQ:(n + 1) * TQ], in_=pt[:, 0:512]),
                    rd=(pbx,), wr=(dst_b2,))
        for j in range(NT_E):
            wd2j = wd2s[0][0][:, j * 128:(j + 1) * 128]
            ptd, pbd = cp.pb[2]
            mk.op("pe", lambda e, wd2j=wd2j, ptd=ptd: e.matmul(
                ptd[:, 0:512], wd2j, rsb[:], start=True, stop=True),
                rd=(wd2s[0][1], rsb_b, pbd), wr=(pbd,))
            bdt_col = vec_t[j][0][:, 1:2]
            mk.op("dve", lambda e, bdt_col=bdt_col, ptd=ptd: e.tensor_scalar_add(
                sps[:], ptd[:, 0:512], bdt_col),
                rd=(pbd, vec_t[j][1]), wr=(sps_b,))
            mk.op("dve", lambda e: e.tensor_scalar_max(spm[:], sps[:], 0.0),
                  rd=(sps_b,), wr=(spm_b,))
            mk.op("dve", lambda e: e.scalar_tensor_tensor(
                out=sps[:], in0=sps[:], scalar=-1.0, in1=sps[:],
                op0=ALU.mult, op1=ALU.min), rd=(sps_b,), wr=(sps_b,))
            mk.op("act", lambda e: e.activation(out=sps[:], in_=sps[:], func=AF.Exp),
                  rd=(sps_b,), wr=(sps_b,))
            mk.op("act", lambda e: e.activation(out=sps[:], in_=sps[:], func=AF.Ln,
                                                bias=1.0), rd=(sps_b,), wr=(sps_b,))
            mk.op("dve", lambda e: e.tensor_add(dtt[:], spm[:], sps[:]),
                  rd=(spm_b, sps_b), wr=(dtt_b,))
            mk.op("dve", lambda e, j=j: e.tensor_mul(dtx[:], dtt[:], xb[j][0][:]),
                  rd=(dtt_b, xb[j][1]), wr=(dtx_b,))
            ws2, ws2_b = wstr[(j + 1) % 2]
            mk.dma("sp", ws2[:, 0:1024].rearrange("p (k c) -> p k c", k=8),
                   skipwT.rearrange("(k p) o -> p k o", p=128)[:, :, j * 128:(j + 1) * 128],
                   rd=(ws2_b, skipwT_b), wr=(ws2_b,))
            lhsz = [(ws2[:, k * 128:(k + 1) * 128], ws2_b) for k in range(8)]
            rhsz = [(xn[jj][0][:, MAIN0:MAIN0 + 512], xn[jj][1]) for jj in range(NT_D)]
            ptz, pbz = cp.mm_acc(3, 128, 512, lhsz, rhsz)
            mk.op("act", lambda e, ptz=ptz: e.activation(
                out=zs[:], in_=ptz[:, 0:512], func=AF.Silu), rd=(pbz,), wr=(zs_b,))
            for n in range(16):
                asc, asc_b = a_scr[n % 2]
                usc, usc_b = u_scr[n % 2]
                an_col = aneg[j][0][:, n:n + 1]
                mk.op("act", lambda e, an_col=an_col, asc=asc: e.activation(
                    out=asc[:], in_=dtt[:], func=AF.Exp, scale=an_col),
                    rd=(dtt_b, aneg[j][1]), wr=(asc_b,))
                mk.op("pool", lambda e, n=n, usc=usc: e.tensor_tensor(
                    out=usc[:], in0=dtx[:], in1=brep[:, n * TQ:(n + 1) * TQ],
                    op=ALU.mult), rd=(dtx_b, brep_b), wr=(usc_b,))
                hj, hj_b = hcur[j]
                mk.op("dve", lambda e, n=n, asc=asc, usc=usc, hj=hj: e.tensor_tensor_scan(
                    out=h_all[:, n * TQ:(n + 1) * TQ], data0=asc[:], data1=usc[:],
                    initial=hj[:, n:n + 1], op0=ALU.mult, op1=ALU.add),
                    rd=(asc_b, usc_b, hj_b), wr=(h_all_b,))
            hj, hj_b = hcur[j]
            for n in range(16):
                mk.op("dve", lambda e, hj=hj, n=n: e.tensor_copy(
                    out=hj[:, n:n + 1],
                    in_=h_all[:, (n + 1) * TQ - 1:(n + 1) * TQ]),
                    rd=(h_all_b,), wr=(hj_b,))
            mk.op("pool", lambda e: e.tensor_tensor(
                out=h_all[:], in0=h_all[:], in1=crep[:], op=ALU.mult),
                rd=(h_all_b, crep_b), wr=(h_all_b,))
            mk.op("dve", lambda e: e.tensor_reduce(
                out=yv[:], in_=h_all[:].rearrange("p (n t) -> p t n", n=16),
                axis=AX.X, op=ALU.add), rd=(h_all_b,), wr=(yv_b,))
            d_col = vec_t[j][0][:, 2:3]
            mk.op("dve", lambda e, j=j, d_col=d_col: e.scalar_tensor_tensor(
                out=t1s[:], in0=xb[j][0][:], scalar=d_col, in1=yv[:],
                op0=ALU.mult, op1=ALU.add), rd=(xb[j][1], vec_t[j][1], yv_b),
                wr=(t1s_b,))
            mk.op("dve", lambda e, j=j: e.tensor_mul(y2[j][0][:], t1s[:], zs[:]),
                  rd=(t1s_b, zs_b), wr=(y2[j][1],))
        for k in range(NT_E):
            wk, wk_b = wstr[k % 2]
            mk.dma("sp", wk[:, 0:1024], outwT[k * 128:(k + 1) * 128, :],
                   rd=(wk_b, outwT_b), wr=(wk_b,))
            for ot in range(8):
                pt, pbx = cp.pb[ot]
                mk.op("pe", lambda e, k=k, ot=ot, wk=wk, pt=pt: e.matmul(
                    pt[:], wk[:, ot * 128:(ot + 1) * 128], y2[k][0][:],
                    start=(k == 0), stop=(k == NT_E - 1)),
                    rd=(wk_b, y2[k][1]) + ((pbx,) if k == 0 else ()), wr=(pbx,))
        for ot in range(8):
            pt, pbx = cp.pb[ot]
            mk.dma("sp", rm[:], R[ot * 128:(ot + 1) * 128, MAIN0 + e0:MAIN0 + e0 + TQ],
                   rd=(rm_b,), wr=(rm_b,))
            mk.op("dve", lambda e, pt=pt: e.tensor_add(rout_s[:], pt[:], rm[:]),
                  rd=(pbx, rm_b), wr=(rout_b,))
            mk.op("dve", lambda e, ot=ot: e.tensor_add(
                rout_s[:], rout_s[:], Rt[ot][0][:, MAIN0:MAIN0 + TQ]),
                rd=(rout_b, Rt[ot][1]), wr=(rout_b,))
            mk.dma("sp", Rout[ot * 128:(ot + 1) * 128, e0:e0 + TQ], rout_s[:],
                   rd=(rout_b,), wr=())

    for j in range(NT_E):
        hj, hj_b = hcur[j]
        mk.dma("sp", hout[j * 128:(j + 1) * 128, :], hj[:], rd=(hj_b,), wr=())

    _CONST_VALS["l0b"] = list(cp.const_vals)
    return cp.mk.finalize()


def _run_l0b(Rs, P):
    nc = _get_built("l0b", build_l0b)
    vecs = np.stack([P["conv_b"], P["wd2_b"], P["W_D"]], axis=1).astype(np.float32)
    base = dict(skipw=P["skip_w"], inw=P["in_w"], wd1=P["wd1"], wd2=P["wd2"],
                wB=P["wB"], wC=P["wC"], outw=P["out_w"], Alog=P["A_log"],
                vecs=vecs, convw=P["conv_w"],
                nrmw=P["norm_w"].reshape(-1, 1), sel16=_sel16(),
                ident=_ident128(), cconst=_cconst_arr("l0b"))
    base = {k: np.ascontiguousarray(v) for k, v in base.items()}

    def mk_maps(houts):
        maps = []
        for c in range(8):
            h = c % 2
            Rext = np.zeros((1024, 2112), np.float32)
            Rext[:, 3:2051] = Rs[c]
            if h == 1:
                Rext[:, 0:3] = Rs[c - 1][:, -3:]
            hin = (houts[c - 1] if (h == 1 and houts is not None)
                   else np.zeros((2048, 16), np.float32))
            m = dict(base)
            m["R"] = Rext
            m["hinit"] = hin
            maps.append(m)
        return maps

    r1 = run_bass_kernel_spmd(nc, mk_maps(None), core_ids=list(range(8)))
    houts = [np.asarray(r1.results[c]["hout"], np.float32) for c in range(8)]
    r2 = run_bass_kernel_spmd(nc, mk_maps(houts), core_ids=list(range(8)))
    out = []
    for c in range(8):
        res = r1 if c % 2 == 0 else r2
        out.append(np.asarray(res.results[c]["Rout"], np.float32))
    return out


def build_l1a():
    cp = CP("l1a")
    cp.lnw = 520
    mk = cp.mk
    TQ = 512
    NT_D, NT_E = 8, 16
    MAIN0 = 3
    WQ = TQ + 3

    R = cp.din("R", (1024, 2112))
    Win = cp.din("Win", (4096, 1024))
    Wx = cp.din("Wx", (96, 2048))
    Wdt = cp.din("Wdt", (2048, 64))
    Wout = cp.din("Wout", (1024, 2048))
    Alog = cp.din("Alog", (2048, 16))
    vecs = cp.din("vecs", (2048, 3))
    convw = cp.din("convw", (2048, 4))
    hinit = cp.din("hinit", (2048, 16))
    sel16 = cp.din("sel16", (16, 2048), BF16)
    ident = cp.din("ident", (128, 128), BF16)
    Rout = cp.dout("Rout", (1024, 2048))
    hout = cp.dout("hout", (2048, 16))

    ones_bf = cp.sb("ones_bf", (128, 1), BF16)
    mk.op("dve", lambda e: e.memset(ones_bf[0][:], 1.0), wr=(ones_bf[1],))
    ones_f = cp.sb("ones_f", (1, 128), F32)
    mk.op("dve", lambda e: e.memset(ones_f[0][:], 1.0), wr=(ones_f[1],))

    vec_t = cp.load_tiles(vecs, 2048, 3, dtype_out=F32, name="vec")
    cw_t = cp.load_tiles(convw, 2048, 4, dtype_out=F32, name="cw")
    alog_t = cp.load_tiles(Alog, 2048, 16, dtype_out=F32, name="alog")
    hcur = cp.load_tiles(hinit, 2048, 16, dtype_out=F32, name="hcur")
    sel_t = cp.load_tiles(sel16, 16, 2048, dtype_out=BF16, name="sel", direct=True)
    id_t = cp.load_tiles(ident, 128, 128, dtype_out=BF16, name="id", direct=True)

    aneg = []
    for j in range(NT_E):
        t, tb = cp.sb(f"aneg{j}", (128, 16), F32)
        src, srcb = alog_t[j]
        mk.op("act", lambda e, src=src, t=t: e.activation(
            out=t[:], in_=src[:], func=AF.Exp), rd=(srcb,), wr=(tb,))
        mk.op("dve", lambda e, t=t: e.tensor_scalar_mul(t[:], t[:], -1.0),
              rd=(tb,), wr=(tb,))
        aneg.append((t, tb))

    brep, brep_b = cp.sb("brep", (128, 16 * 512), BF16)
    crep, crep_b = cp.sb("crep", (128, 16 * 512), BF16)
    stg = crep[:].bitcast(F32)[:, 0:2048]
    stg_b = crep_b
    stgb = brep[:, 0:4096]
    stgb_b = brep_b
    ident_pair = (id_t[0][0][:], id_t[0][1])
    stage = ((stg, stg_b), (stgb, stgb_b))
    winT, winT_b = stage_weight_hbm(cp, Win, 4096, 1024, "winT",
                                    transpose=True, stage=stage, ident=ident_pair)
    wxT, wxT_b = stage_weight_hbm(cp, Wx, 96, 2048, "wxT",
                                  transpose=True, stage=stage, ident=ident_pair)
    wdtT, wdtT_b = stage_weight_hbm(cp, Wdt, 2048, 64, "wdtT",
                                    transpose=True, stage=stage, ident=ident_pair)
    woutT, woutT_b = stage_weight_hbm(cp, Wout, 1024, 2048, "woutT",
                                      transpose=True, stage=stage, ident=ident_pair)

    wxs = cp.load_tiles(wxT, 2048, 96, dtype_out=BF16, name="wxs", direct=True)
    wdts = cp.load_tiles(wdtT, 64, 2048, dtype_out=BF16, name="wdts", direct=True)

    Rt = [cp.sb(f"Rt{j}", (128, WQ), F32) for j in range(NT_D)]
    xn = [cp.sb(f"xn{j}", (128, WQ), BF16) for j in range(NT_D)]
    xb = [cp.sb(f"xb{j}", (128, TQ), BF16) for j in range(NT_E)]
    xzf, xzf_b = cp.sb("xzf", (128, WQ), F32)
    wstr = [cp.sb(f"wstr{i}", (128, 2048), BF16) for i in range(2)]
    bsb, bsb_b = cp.sb("bsb", (16, TQ), BF16)
    csb, csb_b = cp.sb("csb", (16, TQ), BF16)
    rsb, rsb_b = cp.sb("rsb", (64, TQ), BF16)
    bcs, bcs_b = cp.sb("bcs", (32, TQ), BF16)
    dtt, dtt_b = cp.sb("dtt", (128, TQ), F32)
    sps, sps_b = cp.sb("sps", (128, TQ), F32)
    spm, spm_b = cp.sb("spm", (128, TQ), F32)
    dtx, dtx_b = cp.sb("dtx", (128, TQ), BF16)
    a_scr = [cp.sb(f"ascr{i}", (128, TQ), F32) for i in range(2)]
    u_scr = [cp.sb(f"uscr{i}", (128, TQ), BF16) for i in range(2)]
    h_all, h_all_b = cp.sb("h_all", (128, 16 * TQ), BF16)
    yv, yv_b = cp.sb("yv", (128, TQ), F32)
    y2 = [cp.sb(f"y2{j}", (128, TQ), BF16) for j in range(NT_E)]
    zs, zs_b = cp.sb("zs", (128, TQ), BF16)
    t1s, t1s_b = cp.sb("t1s", (128, TQ), F32)
    rm, rm_b = cp.sb("rm", (128, TQ), F32)
    rout_s, rout_b = cp.sb("rout_s", (128, TQ), F32)

    for q in range(4):
        e0 = q * TQ
        for j in range(NT_D):
            mk.dma("sp", Rt[j][0][:], R[j * 128:(j + 1) * 128, e0:e0 + WQ],
                   rd=(Rt[j][1],), wr=(Rt[j][1],))
        arow0, brow0 = emit_ln_stats(cp, Rt, WQ, (ones_bf[0][:], ones_bf[1]))
        emit_rowscale_apply(cp, Rt, xn, WQ, arow0, brow0,
                            (ones_f[0][:], ones_f[1]))
        for ot in range(NT_E):
            ws, ws_b = wstr[ot % 2]
            mk.dma("sp", ws[:, 0:1024].rearrange("p (k c) -> p k c", k=8),
                   winT.rearrange("(k p) o -> p k o", p=128)[:, :, ot * 128:(ot + 1) * 128],
                   rd=(ws_b, winT_b), wr=(ws_b,))
            lhs = [(ws[:, k * 128:(k + 1) * 128], ws_b) for k in range(8)]
            rhs0 = [(xn[j][0][:, 0:512], xn[j][1]) for j in range(NT_D)]
            rhs1 = [(xn[j][0][:, 3:515], xn[j][1]) for j in range(NT_D)]
            cp.mm_acc(2, 128, 512, lhs, rhs0)
            cp.mm_acc(3, 128, 512, lhs, rhs1)
            pt2, pb2 = cp.pb[2]
            pt3, pb3 = cp.pb[3]
            mk.op("act", lambda e: e.activation(out=xzf[:, 0:512], in_=pt2[:, 0:512],
                                                func=AF.Copy), rd=(pb2,), wr=(xzf_b,))
            mk.op("act", lambda e: e.activation(out=xzf[:, 512:515], in_=pt3[:, 509:512],
                                                func=AF.Copy), rd=(pb3,), wr=(xzf_b,))
            cwj, cwj_b = cw_t[ot]
            cbj, cbj_b = vec_t[ot]
            xbj, xbj_b = xb[ot]
            mk.op("dve", lambda e, cwj=cwj, cbj=cbj: e.scalar_tensor_tensor(
                out=t1s[:], in0=xzf[:, 0:512], scalar=cwj[:, 0:1],
                in1=cbj[:, 0:1].broadcast_to([128, 512]), op0=ALU.mult, op1=ALU.add),
                rd=(xzf_b, cwj_b, cbj_b), wr=(t1s_b,))
            for kk in range(1, 4):
                mk.op("dve", lambda e, cwj=cwj, kk=kk: e.scalar_tensor_tensor(
                    out=t1s[:], in0=xzf[:, kk:kk + 512], scalar=cwj[:, kk:kk + 1],
                    in1=t1s[:], op0=ALU.mult, op1=ALU.add),
                    rd=(xzf_b, cwj_b, t1s_b), wr=(t1s_b,))
            mk.op("act", lambda e, xbj=xbj: e.activation(
                out=xbj[:], in_=t1s[:], func=AF.Silu), rd=(t1s_b,), wr=(xbj_b,))
        lhs = [(wxs[k][0][:], wxs[k][1]) for k in range(NT_E)]
        rhs = [(xb[k][0][:], xb[k][1]) for k in range(NT_E)]
        pt, pbx = cp.mm_acc(0, 96, 512, lhs, rhs)
        mk.op("dve", lambda e, pt=pt: e.tensor_copy(out=rsb[:], in_=pt[0:64, :]),
              rd=(pbx,), wr=(rsb_b,))
        mk.op("dve", lambda e, pt=pt: e.tensor_copy(out=bcs[:], in_=pt[64:96, :]),
              rd=(pbx,), wr=(bcs_b,))
        mk.dma("sp", bsb[:], bcs[0:16, :], rd=(bcs_b,), wr=(bsb_b,))
        mk.dma("sp", csb[:], bcs[16:32, :], rd=(bcs_b,), wr=(csb_b,))
        selap, sel_b = sel_t[0]
        for n in range(16):
            for (src, src_b, dst, dst_b2) in ((bsb, bsb_b, brep, brep_b),
                                              (csb, csb_b, crep, crep_b)):
                pt, pbx = cp.pb[1]
                mk.op("pe", lambda e, n=n, src=src, pt=pt: e.matmul(
                    pt[:, 0:512], selap[:, n * 128:(n + 1) * 128], src[:],
                    start=True, stop=True), rd=(sel_b, src_b, pbx), wr=(pbx,))
                mk.op("dve", lambda e, n=n, dst=dst, pt=pt: e.tensor_copy(
                    out=dst[:, n * TQ:(n + 1) * TQ], in_=pt[:, 0:512]),
                    rd=(pbx,), wr=(dst_b2,))
        for j in range(NT_E):
            wdtj = wdts[0][0][:, j * 128:(j + 1) * 128]
            ptd, pbd = cp.pb[2]
            mk.op("pe", lambda e, wdtj=wdtj, ptd=ptd: e.matmul(
                ptd[:, 0:512], wdtj, rsb[:], start=True, stop=True),
                rd=(wdts[0][1], rsb_b, pbd), wr=(pbd,))
            bdt_col = vec_t[j][0][:, 1:2]
            mk.op("dve", lambda e, bdt_col=bdt_col, ptd=ptd: e.tensor_scalar_add(
                sps[:], ptd[:, 0:512], bdt_col),
                rd=(pbd, vec_t[j][1]), wr=(sps_b,))
            mk.op("dve", lambda e: e.tensor_scalar_max(spm[:], sps[:], 0.0),
                  rd=(sps_b,), wr=(spm_b,))
            mk.op("dve", lambda e: e.scalar_tensor_tensor(
                out=sps[:], in0=sps[:], scalar=-1.0, in1=sps[:],
                op0=ALU.mult, op1=ALU.min), rd=(sps_b,), wr=(sps_b,))
            mk.op("act", lambda e: e.activation(out=sps[:], in_=sps[:], func=AF.Exp),
                  rd=(sps_b,), wr=(sps_b,))
            mk.op("act", lambda e: e.activation(out=sps[:], in_=sps[:], func=AF.Ln,
                                                bias=1.0), rd=(sps_b,), wr=(sps_b,))
            mk.op("dve", lambda e: e.tensor_add(dtt[:], spm[:], sps[:]),
                  rd=(spm_b, sps_b), wr=(dtt_b,))
            mk.op("dve", lambda e, j=j: e.tensor_mul(dtx[:], dtt[:], xb[j][0][:]),
                  rd=(dtt_b, xb[j][1]), wr=(dtx_b,))
            ws2, ws2_b = wstr[(j + 1) % 2]
            mk.dma("sp", ws2[:, 0:1024].rearrange("p (k c) -> p k c", k=8),
                   winT.rearrange("(k p) o -> p k o", p=128)[:, :, 2048 + j * 128:2048 + (j + 1) * 128],
                   rd=(ws2_b, winT_b), wr=(ws2_b,))
            lhsz = [(ws2[:, k * 128:(k + 1) * 128], ws2_b) for k in range(8)]
            rhsz = [(xn[jj][0][:, MAIN0:MAIN0 + 512], xn[jj][1]) for jj in range(NT_D)]
            ptz, pbz = cp.mm_acc(3, 128, 512, lhsz, rhsz)
            mk.op("act", lambda e, ptz=ptz: e.activation(
                out=zs[:], in_=ptz[:, 0:512], func=AF.Silu), rd=(pbz,), wr=(zs_b,))
            for n in range(16):
                asc, asc_b = a_scr[n % 2]
                usc, usc_b = u_scr[n % 2]
                an_col = aneg[j][0][:, n:n + 1]
                mk.op("act", lambda e, an_col=an_col, asc=asc: e.activation(
                    out=asc[:], in_=dtt[:], func=AF.Exp, scale=an_col),
                    rd=(dtt_b, aneg[j][1]), wr=(asc_b,))
                mk.op("pool", lambda e, n=n, usc=usc: e.tensor_tensor(
                    out=usc[:], in0=dtx[:], in1=brep[:, n * TQ:(n + 1) * TQ],
                    op=ALU.mult), rd=(dtx_b, brep_b), wr=(usc_b,))
                hj, hj_b = hcur[j]
                mk.op("dve", lambda e, n=n, asc=asc, usc=usc, hj=hj: e.tensor_tensor_scan(
                    out=h_all[:, n * TQ:(n + 1) * TQ], data0=asc[:], data1=usc[:],
                    initial=hj[:, n:n + 1], op0=ALU.mult, op1=ALU.add),
                    rd=(asc_b, usc_b, hj_b), wr=(h_all_b,))
            hj, hj_b = hcur[j]
            for n in range(16):
                mk.op("dve", lambda e, hj=hj, n=n: e.tensor_copy(
                    out=hj[:, n:n + 1],
                    in_=h_all[:, (n + 1) * TQ - 1:(n + 1) * TQ]),
                    rd=(h_all_b,), wr=(hj_b,))
            mk.op("pool", lambda e: e.tensor_tensor(
                out=h_all[:], in0=h_all[:], in1=crep[:], op=ALU.mult),
                rd=(h_all_b, crep_b), wr=(h_all_b,))
            mk.op("dve", lambda e: e.tensor_reduce(
                out=yv[:], in_=h_all[:].rearrange("p (n t) -> p t n", n=16),
                axis=AX.X, op=ALU.add), rd=(h_all_b,), wr=(yv_b,))
            d_col = vec_t[j][0][:, 2:3]
            mk.op("dve", lambda e: e.tensor_mul(t1s[:], yv[:], zs[:]),
                  rd=(yv_b, zs_b), wr=(t1s_b,))
            mk.op("dve", lambda e, j=j, d_col=d_col: e.scalar_tensor_tensor(
                out=y2[j][0][:], in0=xb[j][0][:], scalar=d_col, in1=t1s[:],
                op0=ALU.mult, op1=ALU.add), rd=(xb[j][1], vec_t[j][1], t1s_b),
                wr=(y2[j][1],))
        for k in range(NT_E):
            wk, wk_b = wstr[k % 2]
            mk.dma("sp", wk[:, 0:1024], woutT[k * 128:(k + 1) * 128, :],
                   rd=(wk_b, woutT_b), wr=(wk_b,))
            for ot in range(8):
                pt, pbx = cp.pb[ot]
                mk.op("pe", lambda e, k=k, ot=ot, wk=wk, pt=pt: e.matmul(
                    pt[:], wk[:, ot * 128:(ot + 1) * 128], y2[k][0][:],
                    start=(k == 0), stop=(k == NT_E - 1)),
                    rd=(wk_b, y2[k][1]) + ((pbx,) if k == 0 else ()), wr=(pbx,))
        for ot in range(8):
            pt, pbx = cp.pb[ot]
            mk.dma("sp", rm[:], R[ot * 128:(ot + 1) * 128, MAIN0 + e0:MAIN0 + e0 + TQ],
                   rd=(rm_b,), wr=(rm_b,))
            mk.op("dve", lambda e, pt=pt: e.tensor_add(rout_s[:], pt[:], rm[:]),
                  rd=(pbx, rm_b), wr=(rout_b,))
            mk.dma("sp", Rout[ot * 128:(ot + 1) * 128, e0:e0 + TQ], rout_s[:],
                   rd=(rout_b,), wr=())

    for j in range(NT_E):
        hj, hj_b = hcur[j]
        mk.dma("sp", hout[j * 128:(j + 1) * 128, :], hj[:], rd=(hj_b,), wr=())

    _CONST_VALS["l1a"] = list(cp.const_vals)
    return cp.mk.finalize()


def _run_l1a(Rs, P):
    nc = _get_built("l1a", build_l1a)
    vecs = np.stack([P["conv_b"], P["b_dt"], P["D"]], axis=1).astype(np.float32)
    base = dict(Win=P["W_in"], Wx=P["W_x"], Wdt=P["W_dt"], Wout=P["W_out"],
                Alog=P["A_log"], vecs=vecs,
                convw=P["conv_w"].reshape(2048, 4), sel16=_sel16(),
                ident=_ident128(), cconst=_cconst_arr("l1a"))
    base = {k: np.ascontiguousarray(v) for k, v in base.items()}

    def mk_maps(houts):
        maps = []
        for c in range(8):
            h = c % 2
            Rext = np.zeros((1024, 2112), np.float32)
            Rext[:, 3:2051] = Rs[c]
            if h == 1:
                Rext[:, 0:3] = Rs[c - 1][:, -3:]
            hin = (houts[c - 1] if (h == 1 and houts is not None)
                   else np.zeros((2048, 16), np.float32))
            m = dict(base)
            m["R"] = Rext
            m["hinit"] = hin
            maps.append(m)
        return maps

    r1 = run_bass_kernel_spmd(nc, mk_maps(None), core_ids=list(range(8)))
    houts = [np.asarray(r1.results[c]["hout"], np.float32) for c in range(8)]
    r2 = run_bass_kernel_spmd(nc, mk_maps(houts), core_ids=list(range(8)))
    out = []
    for c in range(8):
        res = r1 if c % 2 == 0 else r2
        out.append(np.asarray(res.results[c]["Rout"], np.float32))
    return out


def build_l1b(debug=False):
    cp = CP("l1b")
    cp.lnw = 520
    mk = cp.mk
    TQ = 512
    NT_D = 8
    T = 2048
    C = 64
    NCH = 32
    PI = math.pi

    R = cp.din("R", (1024, 2048))
    w_in = cp.din("w_in", (128, 1024))
    w_out = cp.din("w_out", (1024, 128))
    w_gate = cp.din("w_gate", (1024, 1024))
    w_mlp1 = cp.din("w_mlp1", (4096, 1024))
    w_mlp2 = cp.din("w_mlp2", (1024, 4096))
    coupT = cp.din("coupT", (64, 16), BF16)
    ldec = cp.din("ldec", (64, 1))
    freq = cp.din("freq", (64, 1))
    bgate = cp.din("bgate", (1024, 1))
    bmlp1 = cp.din("bmlp1", (4096, 1))
    bmlp2 = cp.din("bmlp2", (1024, 1))
    lng = cp.din("lng", (1024, 4))
    jramp = cp.din("jramp", (64, 64))
    carry = cp.din("carry", (64, 2), BF16)
    ident = cp.din("ident", (128, 128), BF16)
    Rout = cp.dout("Rout", (1024, 2048))
    cout = cp.dout("cout", (64, 2), BF16)

    ones_bf = cp.sb("ones_bf", (128, 1), BF16)
    mk.op("dve", lambda e: e.memset(ones_bf[0][:], 1.0), wr=(ones_bf[1],))
    ones_f = cp.sb("ones_f", (1, 128), F32)
    mk.op("dve", lambda e: e.memset(ones_f[0][:], 1.0), wr=(ones_f[1],))

    id_t = cp.load_tiles(ident, 128, 128, dtype_out=BF16, name="id", direct=True)
    ld_t = cp.load_tiles(ldec, 64, 1, dtype_out=F32, name="ld")
    fq_t = cp.load_tiles(freq, 64, 1, dtype_out=F32, name="fq")
    bg_t = cp.load_tiles(bgate, 1024, 1, dtype_out=F32, name="bg")
    bm1_t = cp.load_tiles(bmlp1, 4096, 1, dtype_out=F32, name="bm1")
    bm2_t = cp.load_tiles(bmlp2, 1024, 1, dtype_out=F32, name="bm2")
    lng_t = cp.load_tiles(lng, 1024, 4, dtype_out=F32, name="lng")
    jr_t = cp.load_tiles(jramp, 64, 64, dtype_out=F32, name="jr")
    cy_t = cp.load_tiles(carry, 64, 2, dtype_out=BF16, name="cy", direct=True)
    cpT_t = cp.load_tiles(coupT, 64, 16, dtype_out=BF16, name="cpT", direct=True)

    brep, brep_b = cp.sb("bigb", (128, 8192), BF16)
    stg = brep[:].bitcast(F32)
    stg_b = brep_b
    stgb2, stgb2_b = cp.sb("stgb2", (128, 4096), BF16)
    ident_pair = (id_t[0][0][:], id_t[0][1])
    stage = ((stg, stg_b), (stgb2, stgb2_b))
    winT, winT_b = stage_weight_hbm(cp, w_in, 128, 1024, "winT",
                                    transpose=True, stage=stage, ident=ident_pair)
    woutT, woutT_b = stage_weight_hbm(cp, w_out, 1024, 128, "woutT",
                                      transpose=True, stage=stage, ident=ident_pair)
    wgT, wgT_b = stage_weight_hbm(cp, w_gate, 1024, 1024, "wgT",
                                  transpose=True, stage=stage, ident=ident_pair)
    wm1T, wm1T_b = stage_weight_hbm(cp, w_mlp1, 4096, 1024, "wm1T",
                                    transpose=True, stage=stage, ident=ident_pair)
    wm2T, wm2T_b = stage_weight_hbm(cp, w_mlp2, 1024, 4096, "wm2T",
                                    transpose=True, stage=stage, ident=ident_pair)
    wins = cp.load_tiles(winT, 1024, 128, dtype_out=BF16, name="wins", direct=True)
    wouts = cp.load_tiles(woutT, 128, 1024, dtype_out=BF16, name="wouts", direct=True)

    mcol, mcol_b = cp.sb("mcol", (64, 1), F32)
    lnm, lnm_b = cp.sb("lnm", (64, 1), F32)
    arc, arc_b = cp.sb("arc", (64, 1), F32)
    naic, naic_b = cp.sb("naic", (64, 1), F32)
    aic, aic_b = cp.sb("aic", (64, 1), F32)
    tv, tv_b = cp.sb("tv", (64, 1), F32)
    tw, tw_b = cp.sb("tw", (64, 1), F32)
    ti32, ti32_b = cp.sb("ti32", (64, 1), mybir.dt.int32)

    def sin_reduced(dst, dst_b, src_ap, src_b, shift):
        mk.op("dve", lambda e: e.tensor_scalar_add(tv[:], src_ap, shift),
              rd=(src_b,), wr=(tv_b,))
        mk.op("dve", lambda e: e.tensor_scalar_mul(tv[:], tv[:], 1.0 / (2 * PI)),
              rd=(tv_b,), wr=(tv_b,))
        mk.op("dve", lambda e: e.tensor_copy(out=ti32[:], in_=tv[:]),
              rd=(tv_b,), wr=(ti32_b,))
        mk.op("dve", lambda e: e.tensor_copy(out=tw[:], in_=ti32[:]),
              rd=(ti32_b,), wr=(tw_b,))
        mk.op("dve", lambda e: e.tensor_sub(tv[:], tv[:], tw[:]),
              rd=(tv_b, tw_b), wr=(tv_b,))
        mk.op("dve", lambda e: e.tensor_scalar_mul(tv[:], tv[:], 2 * PI),
              rd=(tv_b,), wr=(tv_b,))
        mk.op("dve", lambda e: e.tensor_single_scalar(
            out=tw[:], in_=tv[:], scalar=PI, op=ALU.is_ge),
            rd=(tv_b,), wr=(tw_b,))
        mk.op("dve", lambda e: e.scalar_tensor_tensor(
            out=tv[:], in0=tw[:], scalar=-2 * PI, in1=tv[:],
            op0=ALU.mult, op1=ALU.add), rd=(tw_b, tv_b), wr=(tv_b,))
        mk.op("dve", lambda e: e.tensor_single_scalar(
            out=tw[:], in_=tv[:], scalar=-PI, op=ALU.is_le),
            rd=(tv_b,), wr=(tw_b,))
        mk.op("dve", lambda e: e.scalar_tensor_tensor(
            out=tv[:], in0=tw[:], scalar=2 * PI, in1=tv[:],
            op0=ALU.mult, op1=ALU.add), rd=(tw_b, tv_b), wr=(tv_b,))
        mk.op("act", lambda e: e.activation(out=dst, in_=tv[:], func=AF.Sin),
              rd=(tv_b,), wr=(dst_b,))

    ldap = ld_t[0][0][:, 0:1]
    fqap = fq_t[0][0][:, 0:1]
    mk.op("act", lambda e: e.activation(out=mcol[:], in_=ldap, func=AF.Sigmoid),
          rd=(ld_t[0][1],), wr=(mcol_b,))
    sin_reduced(arc[:], arc_b, fqap, fq_t[0][1], PI / 2)
    mk.op("dve", lambda e: e.tensor_mul(arc[:], arc[:], mcol[:]),
          rd=(arc_b, mcol_b), wr=(arc_b,))
    sin_reduced(aic[:], aic_b, fqap, fq_t[0][1], 0.0)
    mk.op("dve", lambda e: e.tensor_mul(aic[:], aic[:], mcol[:]),
          rd=(aic_b, mcol_b), wr=(aic_b,))
    mk.op("dve", lambda e: e.tensor_scalar_mul(naic[:], aic[:], -1.0),
          rd=(aic_b,), wr=(naic_b,))

    Rt = [cp.sb(f"Rt{j}", (128, TQ), F32) for j in range(NT_D)]
    x1n = [cp.sb(f"x1n{j}", (128, T), BF16) for j in range(NT_D)]
    x1q = [cp.sb(f"x1q{j}", (128, TQ), F32) for j in range(NT_D)]
    x2n = [cp.sb(f"x2n{j}", (128, TQ), BF16) for j in range(NT_D)]
    mbuf = [cp.sb(f"mb{j}", (128, TQ), BF16) for j in range(24)]
    mbuf = mbuf + [(stgb2[:, (j - 24) * TQ:(j - 23) * TQ], stgb2_b)
                   for j in range(24, 32)]
    wstr = [cp.sb(f"wstr{i}", (128, 2048), BF16) for i in range(2)]
    tt1, tt1_b = cp.sb("tt1", (128, TQ), F32)
    gs, gs_b = cp.sb("gs", (128, TQ), BF16)
    h2s, h2s_b = cp.sb("h2s", (128, TQ), F32)
    rout_s, rout_b = cp.sb("rout_s", (128, TQ), F32)
    rm_l1b = cp.sb("rm_l1b", (128, TQ), F32)

    betar, betar_b = cp.sb("betar", (64, T), BF16)
    betai, betai_b = cp.sb("betai", (64, T), BF16)
    intr, intr_b = cp.sb("intr", (64, T), F32)
    inti, inti_b = cp.sb("inti", (64, T), F32)
    t1e, t1e_b = cp.sb("t1e", (64, NCH), F32)
    t3e, t3e_b = cp.sb("t3e", (64, NCH), F32)
    orv, orv_b = betar, betar_b
    oiv, oiv_b = betai, betai_b
    catb, catb_b = cp.sb("catb", (128, T), BF16)

    BD, BD_b = cp.sb("BD", (64, 64), BF16)
    mk.op("dve", lambda e: e.memset(BD[:], 0.0), wr=(BD_b,))
    for h in range(4):
        mk.dma("sp", BD[h * 16:(h + 1) * 16, h * 16:(h + 1) * 16],
               coupT[h * 16:(h + 1) * 16, :], rd=(cpT_t[0][1], BD_b), wr=(BD_b,))

    for q in range(4):
        e0 = q * TQ
        for j in range(NT_D):
            mk.dma("sp", Rt[j][0][:], R[j * 128:(j + 1) * 128, e0:e0 + TQ],
                   rd=(Rt[j][1],), wr=(Rt[j][1],))
        arow0, brow0 = emit_ln_stats(cp, Rt, TQ, (ones_bf[0][:], ones_bf[1]))
        emit_rowscale_apply(cp, Rt, Rt, TQ, arow0, brow0,
                            (ones_f[0][:], ones_f[1]))
        arow, brow = emit_ln_stats(cp, Rt, TQ, (ones_bf[0][:], ones_bf[1]))
        affine = [((lng_t[j][0][:, 0:1], lng_t[j][1]),
                   (lng_t[j][0][:, 1:2], lng_t[j][1])) for j in range(NT_D)]
        x1n_q = [(x1n[j][0][:, e0:e0 + TQ], x1n[j][1]) for j in range(NT_D)]
        emit_rowscale_apply(cp, Rt, x1n_q, TQ, arow, brow,
                            (ones_f[0][:], ones_f[1]), affine=affine)
        for (colr, dst, dst_b) in ((0, betar, betar_b), (64, betai, betai_b)):
            lhs = [(wins[k][0][:, colr:colr + 64], wins[k][1]) for k in range(8)]
            rhs = [(x1n[j][0][:, e0:e0 + TQ], x1n[j][1]) for j in range(NT_D)]
            pt, pbx = cp.mm_acc(0, 64, 512, lhs, rhs)
            mk.op("dve", lambda e, dst=dst, pt=pt, e0=e0: e.tensor_copy(
                out=dst[:, e0:e0 + TQ], in_=pt[0:64, 0:512]),
                rd=(pbx,), wr=(dst_b,))

    mk.op("dve", lambda e: e.memset(intr[:], 0.0), wr=(intr_b,))
    mk.op("dve", lambda e: e.memset(inti[:], 0.0), wr=(inti_b,))
    zc = cp.const_col(0.0)
    for j in range(C - 1, -1, -1):
        if j == C - 1:
            s_ar, s_ai, s_nai = zc[0][0:64, :], zc[0][0:64, :], zc[0][0:64, :]
            ab = (zc[1],)
        else:
            s_ar, s_ai, s_nai = arc[:], aic[:], naic[:]
            ab = (arc_b, aic_b, naic_b)
        pr = (j + 1) % C
        rv = intr[:].rearrange("p (c j) -> p c j", j=C)
        iv = inti[:].rearrange("p (c j) -> p c j", j=C)
        brj = betar[:].rearrange("p (c j) -> p c j", j=C)[:, :, j]
        bij = betai[:].rearrange("p (c j) -> p c j", j=C)[:, :, j]
        mk.op("dve", lambda e, s_ai=s_ai, rv=rv, pr=pr, bij=bij: e.scalar_tensor_tensor(
            out=t3e[:], in0=rv[:, :, pr], scalar=s_ai, in1=bij,
            op0=ALU.mult, op1=ALU.add), rd=(intr_b, betai_b) + ab, wr=(t3e_b,))
        mk.op("dve", lambda e, s_ar=s_ar, rv=rv, pr=pr, brj=brj: e.scalar_tensor_tensor(
            out=t1e[:], in0=rv[:, :, pr], scalar=s_ar, in1=brj,
            op0=ALU.mult, op1=ALU.add), rd=(intr_b, betar_b) + ab, wr=(t1e_b,))
        mk.op("dve", lambda e, s_nai=s_nai, iv=iv, rv=rv, pr=pr, j=j: e.scalar_tensor_tensor(
            out=rv[:, :, j], in0=iv[:, :, pr], scalar=s_nai, in1=t1e[:],
            op0=ALU.mult, op1=ALU.add), rd=(inti_b, t1e_b) + ab, wr=(intr_b,))
        mk.op("dve", lambda e, s_ar=s_ar, iv=iv, pr=pr, j=j: e.scalar_tensor_tensor(
            out=iv[:, :, j], in0=iv[:, :, pr], scalar=s_ar, in1=t3e[:],
            op0=ALU.mult, op1=ALU.add), rd=(inti_b, t3e_b) + ab, wr=(inti_b,))

    mk.dma("sp", cout[0:64, 0:1], betar[:, T - 1:T], rd=(betar_b,), wr=(),
           allow_slow_non_contiguous=True)
    mk.dma("sp", cout[0:64, 1:2], betai[:, T - 1:T], rd=(betai_b,), wr=(),
           allow_slow_non_contiguous=True)
    rv = intr[:].rearrange("p (c j) -> p c j", j=C)
    iv = inti[:].rearrange("p (c j) -> p c j", j=C)
    brv = betar[:].rearrange("p (c j) -> p c j", j=C)
    biv = betai[:].rearrange("p (c j) -> p c j", j=C)
    mk.op("dve", lambda e: e.tensor_add(rv[:, 1:NCH, 0], rv[:, 1:NCH, 0],
                                        brv[:, 0:NCH - 1, C - 1]),
          rd=(intr_b, betar_b), wr=(intr_b,))
    mk.op("dve", lambda e: e.tensor_add(iv[:, 1:NCH, 0], iv[:, 1:NCH, 0],
                                        biv[:, 0:NCH - 1, C - 1]),
          rd=(inti_b, betai_b), wr=(inti_b,))
    mk.op("dve", lambda e: e.tensor_add(intr[:, 0:1], intr[:, 0:1],
                                        cy_t[0][0][:, 0:1]),
          rd=(intr_b, cy_t[0][1]), wr=(intr_b,))
    mk.op("dve", lambda e: e.tensor_add(inti[:, 0:1], inti[:, 0:1],
                                        cy_t[0][0][:, 1:2]),
          rd=(inti_b, cy_t[0][1]), wr=(inti_b,))
    mk.op("dve", lambda e: e.tensor_copy(out=orv[:], in_=intr[:]),
          rd=(intr_b,), wr=(orv_b,))
    mk.op("dve", lambda e: e.tensor_copy(out=oiv[:], in_=inti[:]),
          rd=(inti_b,), wr=(oiv_b,))

    for q in range(4):
        e0 = q * TQ
        pt, pbx = cp.pb[0]
        mk.op("pe", lambda e, e0=e0, pt=pt: e.matmul(
            pt[0:64, 0:512], BD[:], orv[:, e0:e0 + TQ], start=True, stop=True),
            rd=(BD_b, orv_b, pbx), wr=(pbx,))
        mk.op("dve", lambda e, e0=e0, pt=pt: e.tensor_copy(
            out=catb[0:64, e0:e0 + TQ], in_=pt[0:64, 0:512]),
            rd=(pbx,), wr=(catb_b,))
        pt1, pbx1 = cp.pb[1]
        mk.op("pe", lambda e, e0=e0, pt1=pt1: e.matmul(
            pt1[64:128, 0:512], BD[:], oiv[:, e0:e0 + TQ], start=True, stop=True),
            rd=(BD_b, oiv_b, pbx1), wr=(pbx1,))
        mk.op("dve", lambda e, e0=e0, pt1=pt1: e.tensor_copy(
            out=catb[64:128, e0:e0 + TQ], in_=pt1[64:128, 0:512]),
            rd=(pbx1,), wr=(catb_b,))

    for q in range(4):
        e0 = q * TQ
        for j in range(NT_D):
            mk.dma("sp", Rt[j][0][:], R[j * 128:(j + 1) * 128, e0:e0 + TQ],
                   rd=(Rt[j][1],), wr=(Rt[j][1],))
        arow0, brow0 = emit_ln_stats(cp, Rt, TQ, (ones_bf[0][:], ones_bf[1]))
        emit_rowscale_apply(cp, Rt, Rt, TQ, arow0, brow0,
                            (ones_f[0][:], ones_f[1]))
        for ot in range(NT_D):
            ws, ws_b = wstr[ot % 2]
            mk.dma("sp", ws[:, 0:1024].rearrange("p (k c) -> p k c", k=8),
                   wgT.rearrange("(k p) o -> p k o", p=128)[:, :, ot * 128:(ot + 1) * 128],
                   rd=(ws_b, wgT_b), wr=(ws_b,))
            lhs = [(ws[:, k * 128:(k + 1) * 128], ws_b) for k in range(8)]
            rhs = [(x1n[j][0][:, e0:e0 + TQ], x1n[j][1]) for j in range(NT_D)]
            ptg, pbg = cp.mm_acc(2, 128, 512, lhs, rhs)
            bgc = bg_t[ot][0][:, 0:1]
            mk.op("dve", lambda e, ptg=ptg, bgc=bgc, ot=ot: e.tensor_scalar_add(
                tt1[:], ptg[:, 0:512], bgc), rd=(pbg, bg_t[ot][1]), wr=(tt1_b,))
            mk.op("act", lambda e: e.activation(out=gs[:], in_=tt1[:],
                                                func=AF.Sigmoid),
                  rd=(tt1_b,), wr=(gs_b,))
            wo = wouts[0][0][:, ot * 128:(ot + 1) * 128]
            pth, pbh = cp.pb[3]
            mk.op("pe", lambda e, wo=wo, e0=e0, pth=pth: e.matmul(
                pth[:, 0:512], wo, catb[:, e0:e0 + TQ], start=True, stop=True),
                rd=(wouts[0][1], catb_b, pbh), wr=(pbh,))
            mk.op("dve", lambda e, pth=pth: e.tensor_copy(out=h2s[:], in_=pth[:, 0:512]),
                  rd=(pbh,), wr=(h2s_b,))
            mk.op("dve", lambda e, ot=ot: e.tensor_mul(h2s[:], h2s[:], gs[:]),
                  rd=(h2s_b, gs_b), wr=(h2s_b,))
            mk.op("dve", lambda e, ot=ot: e.tensor_add(
                x1q[ot][0][:], Rt[ot][0][:], h2s[:]),
                rd=(Rt[ot][1], h2s_b), wr=(x1q[ot][1],))
        arow2, brow2 = emit_ln_stats(cp, x1q, TQ, (ones_bf[0][:], ones_bf[1]))
        affine2 = [((lng_t[j][0][:, 2:3], lng_t[j][1]),
                    (lng_t[j][0][:, 3:4], lng_t[j][1])) for j in range(NT_D)]
        emit_rowscale_apply(cp, x1q, x2n, TQ, arow2, brow2,
                            (ones_f[0][:], ones_f[1]), affine=affine2)
        for ot in range(32):
            ws, ws_b = wstr[ot % 2]
            mk.dma("sp", ws[:, 0:1024].rearrange("p (k c) -> p k c", k=8),
                   wm1T.rearrange("(k p) o -> p k o", p=128)[:, :, ot * 128:(ot + 1) * 128],
                   rd=(ws_b, wm1T_b), wr=(ws_b,))
            lhs = [(ws[:, k * 128:(k + 1) * 128], ws_b) for k in range(8)]
            rhs = [(x2n[j][0][:], x2n[j][1]) for j in range(NT_D)]
            pt, pbx = cp.mm_acc(4 + ot % 2, 128, TQ, lhs, rhs)
            bcol = bm1_t[ot][0][:, 0:1]
            mk.op("dve", lambda e, pt=pt, bcol=bcol, ot=ot: e.tensor_scalar_add(
                tt1[:], pt[:, 0:TQ], bcol), rd=(pbx, bm1_t[ot][1]), wr=(tt1_b,))
            mk.op("act", lambda e, ot=ot: e.activation(
                out=mbuf[ot][0][:, 0:TQ], in_=tt1[:], func=AF.Silu),
                rd=(tt1_b,), wr=(mbuf[ot][1],))
        for ot in range(NT_D):
            ws, ws_b = wstr[ot % 2]
            mk.dma("sp", ws[:, 0:2048].rearrange("p (k c) -> p k c", k=16),
                   wm2T.rearrange("(k p) o -> p k o", p=128)[:, 0:16, ot * 128:(ot + 1) * 128],
                   rd=(ws_b, wm2T_b), wr=(ws_b,))
            ws2, ws2_b = wstr[(ot + 1) % 2]
            mk.dma("sp", ws2[:, 0:2048].rearrange("p (k c) -> p k c", k=16),
                   wm2T.rearrange("(k p) o -> p k o", p=128)[:, 16:32, ot * 128:(ot + 1) * 128],
                   rd=(ws2_b, wm2T_b), wr=(ws2_b,))
            lhs = [(ws[:, k * 128:(k + 1) * 128], ws_b) for k in range(16)] + \
                  [(ws2[:, k * 128:(k + 1) * 128], ws2_b) for k in range(16)]
            rhs = [(mbuf[k][0][:, 0:TQ], mbuf[k][1]) for k in range(32)]
            pt, pbx = cp.mm_acc(6 + ot % 2, 128, TQ, lhs, rhs)
            bcol = bm2_t[ot][0][:, 0:1]
            mk.op("dve", lambda e, pt=pt, bcol=bcol, ot=ot: e.tensor_scalar_add(
                rout_s[:], pt[:, 0:TQ], bcol), rd=(pbx, bm2_t[ot][1]), wr=(rout_b,))
            mk.op("dve", lambda e, ot=ot: e.tensor_add(
                rout_s[:], rout_s[:], x1q[ot][0][:]),
                rd=(rout_b, x1q[ot][1]), wr=(rout_b,))
            mk.dma("sp", rm_l1b[0][:], R[ot * 128:(ot + 1) * 128, e0:e0 + TQ],
                   rd=(rm_l1b[1],), wr=(rm_l1b[1],))
            mk.op("dve", lambda e: e.tensor_add(rout_s[:], rout_s[:], rm_l1b[0][:]),
                  rd=(rout_b, rm_l1b[1]), wr=(rout_b,))
            mk.dma("sp", Rout[ot * 128:(ot + 1) * 128, e0:e0 + TQ], rout_s[:],
                   rd=(rout_b,), wr=())

    _CONST_VALS["l1b"] = list(cp.const_vals)
    return cp.mk.finalize()


def _run_l1b(Rs, P):
    nc = _get_built("l1b", build_l1b)
    import ml_dtypes
    coupT = np.zeros((64, 16), ml_dtypes.bfloat16)
    cp_in = np.asarray(P["coupling"], np.float32)
    for h in range(4):
        coupT[h * 16:(h + 1) * 16, :] = cp_in[h].T.astype(ml_dtypes.bfloat16)
    jramp = np.broadcast_to(np.arange(64, dtype=np.float32)[None, :], (64, 64)).copy()
    lng = np.stack([P["ln1_g"], P["ln1_b"], P["ln2_g"], P["ln2_b"]], axis=1).astype(np.float32)
    base = dict(w_in=P["w_in"], w_out=P["w_out"], w_gate=P["w_gate"],
                w_mlp1=P["w_mlp1"], w_mlp2=P["w_mlp2"], coupT=coupT,
                ldec=P["log_decay"].reshape(-1, 1), freq=P["frequency"].reshape(-1, 1),
                bgate=P["b_gate"].reshape(-1, 1), bmlp1=P["b_mlp1"].reshape(-1, 1),
                bmlp2=P["b_mlp2"].reshape(-1, 1), lng=lng, jramp=jramp,
                ident=_ident128(), cconst=_cconst_arr("l1b"))
    base = {k: np.ascontiguousarray(v) for k, v in base.items()}

    def mk_maps(couts):
        maps = []
        for c in range(8):
            h = c % 2
            import ml_dtypes
            cin = (couts[c - 1] if (h == 1 and couts is not None)
                   else np.zeros((64, 2), ml_dtypes.bfloat16))
            m = dict(base)
            m["R"] = np.ascontiguousarray(Rs[c])
            m["carry"] = np.ascontiguousarray(cin)
            maps.append(m)
        return maps

    r1 = run_bass_kernel_spmd(nc, mk_maps(None), core_ids=list(range(8)))
    couts = [np.asarray(r1.results[c]["cout"]) for c in range(8)]
    r2 = run_bass_kernel_spmd(nc, mk_maps(couts), core_ids=list(range(8)))
    out = []
    for c in range(8):
        res = r1 if c % 2 == 0 else r2
        out.append(np.asarray(res.results[c]["Rout"], np.float32))
    return out


def build_l0ffn():
    cp = CP("l0ffn")
    cp.lnw = 512
    mk = cp.mk
    W = 512
    NT_D = 8
    NW = 7
    EXTW = 1 + NW * W

    R = cp.din("R", (1024, 3588))
    wqkv = cp.din("wqkv", (2048, 1536))
    wo = cp.din("wo", (1024, 1024))
    w1 = cp.din("w1", (2048, 5460))
    w2 = cp.din("w2", (5460, 1024))
    qk_sc = cp.din("qk_sc", (64, 4))
    lnab = cp.din("lnab", (1024, 4))
    b1v = cp.din("b1v", (5460, 2))
    b2v = cp.din("b2v", (1024, 2))
    tabs = cp.din("tabs", (64, 4 * 3588), BF16)
    masks = cp.din("masks", (128, 4 * 1024), BF16)
    flags = cp.din("flags", (128, 8))
    rotm = cp.din("rotm", (64, 64), BF16)
    ident = cp.din("ident", (128, 128), BF16)
    Rout = cp.dout("Rout", (1024, 2048))

    ones_bf = cp.sb("ones_bf", (128, 1), BF16)
    mk.op("dve", lambda e: e.memset(ones_bf[0][:], 1.0), wr=(ones_bf[1],))
    ones_f = cp.sb("ones_f", (1, 128), F32)
    mk.op("dve", lambda e: e.memset(ones_f[0][:], 1.0), wr=(ones_f[1],))

    qs_t = cp.load_tiles(qk_sc, 64, 4, dtype_out=F32, name="qs")
    ln_t = cp.load_tiles(lnab, 1024, 4, dtype_out=F32, name="lnv")
    b1_t = cp.load_tiles(b1v, 5460, 2, dtype_out=F32, name="b1v")
    b2_t = cp.load_tiles(b2v, 1024, 2, dtype_out=F32, name="b2v")
    mask_t = cp.load_tiles(masks, 128, 4096, dtype_out=BF16, name="mask", direct=True)
    flag_t = cp.load_tiles(flags, 128, 8, dtype_out=F32, name="flag")
    rot_t = cp.load_tiles(rotm, 64, 64, dtype_out=BF16, name="rot", direct=True)
    id_t = cp.load_tiles(ident, 128, 128, dtype_out=BF16, name="id", direct=True)

    agpool, agpool_b = cp.sb("agpool", (128, 11264), BF16)
    stgf = agpool[:].bitcast(F32)[:, 0:1536]
    stgf_b = agpool_b
    stgbq = agpool[:, 3072:4608]
    stgbq_b = agpool_b
    wqkvb = cp.nc.dram_tensor("wqkvb", [2048, 1536], BF16, kind="Internal").ap()
    wob = cp.nc.dram_tensor("wob", [1024, 1024], BF16, kind="Internal").ap()
    wqkvb_buf = mk.buf("wqkvb")
    wob_buf = mk.buf("wob")
    for k in range(16):
        mk.dma("sp", stgf[:, 0:1536], wqkv[k * 128:(k + 1) * 128, :],
               rd=(stgf_b,), wr=(stgf_b,))
        mk.op("pool", lambda e: e.tensor_copy(out=stgbq[:, 0:1536],
                                              in_=stgf[:, 0:1536]),
              rd=(stgf_b, stgbq_b), wr=(stgbq_b,))
        mk.dma("sp", wqkvb[k * 128:(k + 1) * 128, :], stgbq[:, 0:1536],
               rd=(stgbq_b,), wr=(wqkvb_buf,))
    for k in range(8):
        mk.dma("sp", stgf[:, 0:1024], wo[k * 128:(k + 1) * 128, :],
               rd=(stgf_b,), wr=(stgf_b,))
        mk.op("pool", lambda e: e.tensor_copy(out=stgbq[:, 0:1024],
                                              in_=stgf[:, 0:1024]),
              rd=(stgf_b, stgbq_b), wr=(stgbq_b,))
        mk.dma("sp", wob[k * 128:(k + 1) * 128, :], stgbq[:, 0:1024],
               rd=(stgbq_b,), wr=(wob_buf,))

    x0c = [cp.sb(f"x0c{j}", (128, 1), F32) for j in range(NT_D)]
    y1w = [cp.sb(f"y1w{j}", (128, W), F32) for j in range(NT_D)]
    y1c = [cp.sb(f"y1c{j}", (128, 1), F32) for j in range(NT_D)]
    y2w = [cp.sb(f"y2w{j}", (128, W), F32) for j in range(NT_D)]
    pcol = [[cp.sb(f"pc{l}_{j}", (128, 2), F32) for j in range(NT_D)]
            for l in range(2)]
    Rt = [cp.sb(f"Rt{j}", (128, W), F32) for j in range(NT_D)]
    sxw = [cp.sb(f"sxw{j}", (128, W), BF16) for j in range(NT_D)]
    kf = [[[cp.sb(f"kf{l}_{h}_{p}", (64, W), BF16) for p in range(2)]
           for h in range(8)] for l in range(2)]
    vt = [[[cp.sb(f"vt{l}_{h}_{p}", (128, 4 * 64), BF16) for p in range(2)]
           for h in range(8)] for l in range(2)]
    tabw = [cp.sb(f"tab{i}", (64, W), BF16) for i in range(4)]
    qh, qh_b = cp.sb("qh", (64, W), F32)
    gq, gq_b = cp.sb("gq", (64, W), BF16)
    q_all = [cp.sb(f"qall{h}", (64, W), BF16) for h in range(8)]
    sS = agpool[:].bitcast(F32)[:, 2816:2816 + 2 * W]
    sS_b = agpool_b
    mcol, mcol_b = cp.sb("mS", (128, 1), F32)
    scol, scol_b = cp.sb("sScol", (128, 1), F32)
    Pb2 = [cp.sb(f"Pb{i}", (128, 2 * W), BF16) for i in range(2)]
    Ptt2 = [cp.sb(f"Ptt{i}", (128, 128), BF16) for i in range(2)]
    oh = [cp.sb(f"oh{hp}", (128, W), BF16) for hp in range(4)]
    ag = [(agpool[:, i * W:(i + 1) * W], agpool_b) for i in range(22)]
    agf, agf_b = cp.sb("agf", (128, W), F32)
    ggf, ggf_b = cp.sb("ggf", (128, W), BF16)
    wstr = [cp.sb(f"wstr{i}", (128, 1024), BF16) for i in range(2)]
    wstf, wstf_b = cp.sb("wstf", (128, 128), F32)
    t1s, t1s_b = cp.sb("t1s", (128, W), F32)
    t2s, t2s_b = cp.sb("t2s", (128, W), F32)
    routs, routs_b = t2s, t2s_b

    def shift_into_sxw(xin, col_src, col_idx):
        for j in range(NT_D):
            xj, xj_b = xin[j]
            sj, sj_b = sxw[j]
            if j < 4:
                mk.op("act", lambda e, xj=xj, sj=sj: e.activation(
                    out=sj[:], in_=xj[:], func=AF.Copy), rd=(xj_b,), wr=(sj_b,))
            else:
                cj, cj_b = col_src[j]
                mk.op("act", lambda e, xj=xj, sj=sj: e.activation(
                    out=sj[:, 1:W], in_=xj[:, 0:W - 1], func=AF.Copy),
                    rd=(xj_b,), wr=(sj_b,))
                mk.op("act", lambda e, cj=cj, sj=sj, ci=col_idx: e.activation(
                    out=sj[:, 0:1], in_=cj[:, ci:ci + 1], func=AF.Copy),
                    rd=(cj_b,), wr=(sj_b,))

    def emit_layer(l, ew, xin, xin_col, xin_col_idx, yout, kv_only,
                   attn_only=False):
        par = ew % 2
        shift_into_sxw(xin, xin_col, xin_col_idx)
        for i in range(4):
            mk.dma("sp", tabw[i][0][:],
                   tabs[:, i * 3588 + 1 + ew * W:i * 3588 + 1 + (ew + 1) * W],
                   rd=(tabw[i][1],), wr=(tabw[i][1],))
        for ot in range(12):
            ws, ws_b = wstr[ot % 2]
            mk.dma("sp", ws[:, 0:1024].rearrange("p (k c) -> p k c", k=8),
                   wqkvb.rearrange("(k p) o -> p k o", p=128)[:, l * 8:(l + 1) * 8,
                                                             ot * 128:(ot + 1) * 128],
                   rd=(ws_b, wqkvb_buf), wr=(ws_b,))
            lhs = [(ws[:, k * 128:(k + 1) * 128], ws_b) for k in range(8)]
            rhs = [(sxw[j][0][:], sxw[j][1]) for j in range(NT_D)]
            pt, pbx = cp.mm_acc(ot % 2, 128, W, lhs, rhs)
            for half in range(2):
                h = (ot % 4) * 2 + half
                if ot < 8:
                    is_q = ot < 4
                    mk.op("dve", lambda e, pt=pt, half=half: e.tensor_copy(
                        out=qh[:], in_=pt[half * 64:(half + 1) * 64, 0:W]),
                        rd=(pbx,), wr=(qh_b,))
                    mk.op("act", lambda e: e.activation(out=gq[:], in_=qh[:],
                                                        func=AF.Square),
                          rd=(qh_b,), wr=(gq_b,))
                    ptn, pbn = cp.pb[2]
                    mk.op("pe", lambda e, ptn=ptn: e.matmul(
                        ptn[0:1, 0:W], ones_bf[0][0:64, :], gq[:],
                        start=True, stop=True), rd=(ones_bf[1], gq_b, pbn), wr=(pbn,))
                    nrow, nrow_b = cp.scratch("l2row", (1, W), F32)
                    mk.op("dve", lambda e, ptn=ptn: e.tensor_scalar_max(
                        nrow[:], ptn[0:1, 0:W], 1e-24), rd=(pbn,), wr=(nrow_b,))
                    mk.op("act", lambda e: e.activation(out=nrow[:], in_=nrow[:],
                                                        func=AF.Ln),
                          rd=(nrow_b,), wr=(nrow_b,))
                    mh, mh_b = cp.const_col(-0.5)
                    mk.op("act", lambda e, mh=mh: e.activation(
                        out=nrow[:], in_=nrow[:], func=AF.Exp, scale=mh[0:1, :]),
                        rd=(nrow_b, mh_b), wr=(nrow_b,))
                    ptr, pbr = cp.pb[3]
                    mk.op("pe", lambda e, ptr=ptr: e.matmul(
                        ptr[0:64, 0:W], ones_f[0][:, 0:64], nrow[:],
                        start=True, stop=True), rd=(ones_f[1], nrow_b, pbr), wr=(pbr,))
                    mk.op("dve", lambda e, ptr=ptr: e.tensor_mul(
                        qh[:], qh[:], ptr[0:64, 0:W]), rd=(qh_b, pbr), wr=(qh_b,))
                    sc_col = qs_t[0][0][:, (0 if is_q else 1) + 2 * l:
                                        (1 if is_q else 2) + 2 * l]
                    mk.op("dve", lambda e, sc_col=sc_col: e.tensor_scalar(
                        out=gq[:], in0=qh[:], scalar1=sc_col, scalar2=None,
                        op0=ALU.mult), rd=(qh_b, qs_t[0][1]), wr=(gq_b,))
                    ptm, pbm = cp.pb[2]
                    mk.op("pe", lambda e, ptm=ptm: e.matmul(
                        ptm[0:64, 0:W], rot_t[0][0][:], gq[:], start=True,
                        stop=True), rd=(rot_t[0][1], gq_b, pbm), wr=(pbm,))
                    ci = 0 if is_q else 2
                    mk.op("dve", lambda e, ci=ci: e.tensor_mul(
                        t1s[0:64, :], gq[:], tabw[ci][0][:]),
                        rd=(gq_b, tabw[ci][1]), wr=(t1s_b,))
                    mk.op("dve", lambda e, ci=ci, ptm=ptm: e.tensor_mul(
                        t2s[0:64, :], ptm[0:64, 0:W], tabw[ci + 1][0][:]),
                        rd=(pbm, tabw[ci + 1][1]), wr=(t2s_b,))
                    dst, dst_b = ((q_all[h][0], q_all[h][1]) if is_q
                                  else (kf[l][h][par][0], kf[l][h][par][1]))
                    mk.op("dve", lambda e, dst=dst: e.tensor_add(
                        dst[:], t1s[0:64, :], t2s[0:64, :]),
                        rd=(t1s_b, t2s_b), wr=(dst_b,))
                else:
                    mk.op("dve", lambda e, pt=pt, half=half: e.tensor_copy(
                        out=gq[:], in_=pt[half * 64:(half + 1) * 64, 0:W]),
                        rd=(pbx,), wr=(gq_b,))
                    vtt, vtt_b = vt[l][h][par]
                    for c in range(4):
                        ptb_f, ptb_b2 = cp.pb[2]
                        ptb = ptb_f[:].bitcast(BF16)
                        mk.op("pe", lambda e, c=c, ptb=ptb: e.transpose(
                            ptb[0:128, 0:64], gq[:, c * 128:(c + 1) * 128],
                            id_t[0][0][0:64, 0:64]),
                            rd=(gq_b, id_t[0][1], ptb_b2), wr=(ptb_b2,))
                        mk.op("dve", lambda e, c=c, vtt=vtt, ptb=ptb: e.tensor_copy(
                            out=vtt[:, c * 64:(c + 1) * 64], in_=ptb[0:128, 0:64]),
                            rd=(ptb_b2,), wr=(vtt_b,))
        if kv_only:
            for j in range(NT_D):
                cj, cj_b = xin_col[j]
                mk.op("act", lambda e, j=j, cj=cj: e.activation(
                    out=cj[:, 0:1], in_=xin[j][0][:, W - 1:W], func=AF.Copy),
                    rd=(xin[j][1],), wr=(cj_b,))
            return
        for h in range(8):
            for qs in range(4):
                pp = (h * 4 + qs) % 2
                Pb, Pb_b = Pb2[pp]
                Ptt, Ptt_b = Ptt2[pp]
                sb0, sb1 = (4, 5) if pp == 0 else (0, 1)
                pvb = 6 if pp == 0 else 2
                trb = 7 if pp == 0 else 3
                qsl = q_all[h][0][:, qs * 128:(qs + 1) * 128]
                for wi in range(2):
                    src, src_b = kf[l][h][(ew - 1 + wi) % 2]
                    pt, pbx = cp.pb[(sb0, sb1)[wi]]
                    mk.op("pe", lambda e, qsl=qsl, src=src, pt=pt: e.matmul(
                        pt[:, 0:W], qsl, src[:], start=True, stop=True),
                        rd=(q_all[h][1], src_b, pbx), wr=(pbx,))
                pt4, pb4 = cp.pb[sb0]
                pt5, pb5 = cp.pb[sb1]
                mk.op("dve", lambda e, qs=qs, pt4=pt4: e.tensor_add(
                    sS[:, 0:W], pt4[:, 0:W],
                    mask_t[0][0][:, qs * 1024:qs * 1024 + W]),
                    rd=(pb4, mask_t[0][1]), wr=(sS_b,))
                mk.op("dve", lambda e, qs=qs, pt5=pt5: e.tensor_add(
                    sS[:, W:2 * W], pt5[:, 0:W],
                    mask_t[0][0][:, qs * 1024 + W:qs * 1024 + 2 * W]),
                    rd=(pb5, mask_t[0][1]), wr=(sS_b,))
                fcol = flag_t[0][0][:, ew:ew + 1]
                mk.op("dve", lambda e, fcol=fcol: e.tensor_scalar_add(
                    sS[:, 0:W], sS[:, 0:W], fcol), rd=(sS_b, flag_t[0][1]),
                    wr=(sS_b,))
                mk.op("dve", lambda e: e.reduce_max(mcol[:], sS[:], axis=AX.X),
                      rd=(sS_b,), wr=(mcol_b,))
                mk.op("dve", lambda e: e.tensor_scalar_mul(mcol[:], mcol[:], -1.0),
                      rd=(mcol_b,), wr=(mcol_b,))
                mk.op("act", lambda e, Pb=Pb: e.activation(out=Pb[:], in_=sS[:],
                                                           func=AF.Exp,
                                                           bias=mcol[:]),
                      rd=(sS_b, mcol_b), wr=(Pb_b,))
                mk.op("dve", lambda e, Pb=Pb: e.tensor_reduce(
                    scol[:], Pb[:], axis=AX.X, op=ALU.add),
                    rd=(Pb_b,), wr=(scol_b,))
                mk.op("dve", lambda e: e.reciprocal(scol[:], scol[:]),
                      rd=(scol_b,), wr=(scol_b,))
                mk.op("dve", lambda e, Pb=Pb: e.tensor_scalar(
                    out=Pb[:], in0=Pb[:], scalar1=scol[:], scalar2=None,
                    op0=ALU.mult), rd=(Pb_b, scol_b), wr=(Pb_b,))
                pto, pbo = cp.pb[pvb]
                for kt in range(8):
                    ptb_f, ptb_b2 = cp.pb[trb]
                    ptb = ptb_f[:].bitcast(BF16)
                    mk.op("pe", lambda e, kt=kt, ptb=ptb, Pb=Pb: e.transpose(
                        ptb[0:128, 0:128], Pb[:, kt * 128:(kt + 1) * 128],
                        id_t[0][0][:]), rd=(Pb_b, id_t[0][1], ptb_b2), wr=(ptb_b2,))
                    mk.op("act", lambda e, ptb=ptb, Ptt=Ptt: e.activation(
                        out=Ptt[:], in_=ptb[0:128, 0:128], func=AF.Copy),
                        rd=(ptb_b2,), wr=(Ptt_b,))
                    wi = kt // 4
                    vtt, vtt_b = vt[l][h][(ew - 1 + wi) % 2]
                    c = kt % 4
                    mk.op("pe", lambda e, c=c, vtt=vtt, pto=pto, kt=kt,
                          Ptt=Ptt: e.matmul(
                        pto[0:64, 0:128], vtt[:, c * 64:(c + 1) * 64], Ptt[:],
                        start=(kt == 0), stop=(kt == 7)),
                        rd=(vtt_b, Ptt_b) + ((pbo,) if kt == 0 else ()), wr=(pbo,))
                hp = h // 2
                off = (h % 2) * 64
                mk.op("dve", lambda e, hp=hp, off=off, qs=qs, pto=pto: e.tensor_copy(
                    out=oh[hp][0][off:off + 64, qs * 128:(qs + 1) * 128],
                    in_=pto[0:64, 0:128]), rd=(pbo,), wr=(oh[hp][1],))
        for ot in range(NT_D):
            ws, ws_b = wstr[ot % 2]
            mk.dma("sp", ws[:, 0:512].rearrange("p (k c) -> p k c", k=4),
                   wob.rearrange("(k p) o -> p k o", p=128)[:, l * 4:(l + 1) * 4,
                                                           ot * 128:(ot + 1) * 128],
                   rd=(ws_b, wob_buf), wr=(ws_b,))
            lhs = [(ws[:, k * 128:(k + 1) * 128], ws_b) for k in range(4)]
            rhs = [(oh[k][0][:], oh[k][1]) for k in range(4)]
            pt, pbx = cp.mm_acc(ot % 2, 128, W, lhs, rhs)
            mk.op("dve", lambda e, pt=pt, ot=ot: e.tensor_add(
                yout[ot][0][:], pt[:, 0:W], xin[ot][0][:]),
                rd=(pbx, xin[ot][1]), wr=(yout[ot][1],))
        for j in range(NT_D):
            mk.op("act", lambda e, j=j: e.activation(
                out=pcol[l][j][0][:, par:par + 1], in_=yout[j][0][:, W - 1:W],
                func=AF.Copy), rd=(yout[j][1],), wr=(pcol[l][j][1],))
        for j in range(NT_D):
            cj, cj_b = xin_col[j]
            mk.op("act", lambda e, j=j, cj=cj: e.activation(
                out=cj[:, 0:1], in_=xin[j][0][:, W - 1:W], func=AF.Copy),
                rd=(xin[j][1],), wr=(cj_b,))
        if attn_only:
            return
        shift_into_sxw(yout, [pcol[l][j] for j in range(NT_D)], 1 - par)
        sxf = Rt
        for j in range(NT_D):
            mk.op("dve", lambda e, j=j: e.tensor_copy(out=sxf[j][0][:],
                                                      in_=sxw[j][0][:]),
                  rd=(sxw[j][1],), wr=(sxf[j][1],))
        arow, brow = emit_ln_stats(cp, sxf, W, (ones_bf[0][:], ones_bf[1]))
        affine = [((ln_t[j][0][:, 2 * l:2 * l + 1], ln_t[j][1]),
                   (ln_t[j][0][:, 2 * l + 1:2 * l + 2], ln_t[j][1]))
                  for j in range(NT_D)]
        emit_rowscale_apply(cp, sxf, sxw, W, arow, brow,
                            (ones_f[0][:], ones_f[1]), affine=affine)
        for i in range(22):
            r = min(128, 2730 - i * 128)
            for (coff, ws_i) in ((i * 128, 0), (2730 + i * 128, 1)):
                ws, ws_b = wstr[ws_i]
                for k in range(8):
                    mk.dma("sp", wstf[:, 0:r],
                           w1[l * 1024 + k * 128:l * 1024 + (k + 1) * 128,
                              coff:coff + r],
                           rd=(wstf_b,), wr=(wstf_b,))
                    mk.op("pool", lambda e, k=k, r=r, ws=ws: e.tensor_copy(
                        out=ws[:, k * 128:k * 128 + r], in_=wstf[:, 0:r]),
                        rd=(wstf_b, ws_b), wr=(ws_b,))
            rhs = [(sxw[j][0][:], sxw[j][1]) for j in range(NT_D)]
            lhs = [(wstr[0][0][:, k * 128:k * 128 + r], wstr[0][1]) for k in range(8)]
            pta, pba = cp.mm_acc(2, r, W, lhs, rhs)
            abias, abias_b = cp.scratch("abias", (128, 1), F32)
            mk.dma("sp", abias[0:r, :], b1v[i * 128:i * 128 + r, l:l + 1],
                   rd=(abias_b,), wr=(abias_b,), allow_slow_non_contiguous=True)
            mk.op("dve", lambda e, pta=pta, r=r: e.tensor_scalar_add(
                agf[0:r, :], pta[0:r, 0:W], abias[0:r, 0:1]),
                rd=(pba, abias_b), wr=(agf_b,))
            lhs2 = [(wstr[1][0][:, k * 128:k * 128 + r], wstr[1][1]) for k in range(8)]
            ptg, pbg = cp.mm_acc(3, r, W, lhs2, rhs)
            gbias, gbias_b = cp.scratch("gbias", (128, 1), F32)
            mk.dma("sp", gbias[0:r, :], b1v[2730 + i * 128:2730 + i * 128 + r, l:l + 1],
                   rd=(gbias_b,), wr=(gbias_b,), allow_slow_non_contiguous=True)
            mk.op("dve", lambda e, ptg=ptg, r=r: e.tensor_scalar_add(
                t1s[0:r, :], ptg[0:r, 0:W], gbias[0:r, 0:1]),
                rd=(pbg, gbias_b), wr=(t1s_b,))
            mk.op("act", lambda e, r=r: e.activation(out=ggf[0:r, :],
                                                     in_=t1s[0:r, :],
                                                     func=AF.Gelu),
                  rd=(t1s_b,), wr=(ggf_b,))
            mk.op("dve", lambda e, i=i, r=r: e.tensor_mul(
                ag[i][0][0:r, 0:W], agf[0:r, :], ggf[0:r, :]),
                rd=(agf_b, ggf_b), wr=(ag[i][1],))
        for ot in range(NT_D):
            pt, pbx = cp.pb[2 + ot % 2]
            for k in range(22):
                r = min(128, 2730 - k * 128)
                mk.dma("sp", wstf[0:r, 0:128],
                       w2[l * 2730 + k * 128:l * 2730 + k * 128 + r,
                          ot * 128:(ot + 1) * 128],
                       rd=(wstf_b,), wr=(wstf_b,))
                wsb2, wsb2_b = cp.scratch(("w2b", k % 2), (128, 128), BF16)
                mk.op("pool", lambda e, r=r, wsb2=wsb2: e.tensor_copy(
                    out=wsb2[0:r, :], in_=wstf[0:r, 0:128]),
                    rd=(wstf_b, wsb2_b), wr=(wsb2_b,))
                mk.op("pe", lambda e, k=k, r=r, wsb2=wsb2, pt=pt: e.matmul(
                    pt[:, 0:W], wsb2[0:r, 0:128], ag[k][0][0:r, 0:W],
                    start=(k == 0), stop=(k == 21)),
                    rd=(wsb2_b, ag[k][1]) + ((pbx,) if k == 0 else ()), wr=(pbx,))
            b2c = b2_t[ot][0][:, l:l + 1]
            mk.op("dve", lambda e, pt=pt, b2c=b2c, ot=ot: e.tensor_scalar_add(
                t2s[:], pt[:, 0:W], b2c), rd=(pbx, b2_t[ot][1]), wr=(t2s_b,))
            mk.op("dve", lambda e, ot=ot: e.tensor_add(
                yout[ot][0][:], yout[ot][0][:], t2s[:]),
                rd=(yout[ot][1], t2s_b), wr=(yout[ot][1],))

    for j in range(NT_D):
        mk.op("dve", lambda e, j=j: e.memset(x0c[j][0][:], 0.0), wr=(x0c[j][1],))
        mk.op("dve", lambda e, j=j: e.memset(y1c[j][0][:], 0.0), wr=(y1c[j][1],))
        mk.op("dve", lambda e, j=j, l=0: e.memset(pcol[0][j][0][:], 0.0),
              wr=(pcol[0][j][1],))
        mk.op("dve", lambda e, j=j: e.memset(pcol[1][j][0][:], 0.0),
              wr=(pcol[1][j][1],))

    for ew in range(NW):
        for j in range(NT_D):
            mk.dma("sp", Rt[j][0][:],
                   R[j * 128:(j + 1) * 128, 1 + ew * W:1 + (ew + 1) * W],
                   rd=(Rt[j][1],), wr=(Rt[j][1],))
        arow, brow = emit_ln_stats(cp, Rt, W, (ones_bf[0][:], ones_bf[1]))
        emit_rowscale_apply(cp, Rt, Rt, W, arow, brow, (ones_f[0][:], ones_f[1]))
        emit_layer(0, ew, Rt, x0c, 0, y1w, kv_only=(ew == 0))
        if ew == 1:
            emit_layer(1, 1, y1w, y1c, 0, y2w, kv_only=True)
        elif ew == 2:
            emit_layer(1, 2, y1w, y1c, 0, y2w, kv_only=False, attn_only=True)
        elif ew >= 3:
            emit_layer(1, ew, y1w, y1c, 0, y2w, kv_only=False)
        if ew >= 3:
            for ot in range(NT_D):
                mk.dma("sp", Rt[ot][0][:],
                       R[ot * 128:(ot + 1) * 128, 1 + ew * W:1 + (ew + 1) * W],
                       rd=(Rt[ot][1],), wr=(Rt[ot][1],))
                mk.op("dve", lambda e, ot=ot: e.tensor_add(
                    routs[:], Rt[ot][0][:], y2w[ot][0][:]),
                    rd=(Rt[ot][1], y2w[ot][1]), wr=(routs_b,))
                mk.dma("sp", Rout[ot * 128:(ot + 1) * 128, (ew - 3) * W:(ew - 2) * W],
                       routs[:], rd=(routs_b,), wr=())

    _CONST_VALS["l0ffn"] = list(cp.const_vals)
    return cp.mk.finalize()


def _l0ffn_tables(t0):
    import ml_dtypes
    DIM_HEAD, WIN = 64, 512
    cols = 3588
    g = t0 - 1537 + np.arange(cols, dtype=np.float64)
    half = np.arange(0, DIM_HEAD, 2, dtype=np.float64)
    inv = 1.0 / 10000.0 ** (half / DIM_HEAD)
    base = (half + 0.4 * DIM_HEAD) / (1.4 * DIM_HEAD)
    power = (g - 2048.0) / (WIN // 2)
    fr = g[None, :] * inv[:, None]
    sc = base[:, None] ** power[None, :]
    cs = np.cos(fr); sn = np.sin(fr)
    csq = np.concatenate([cs * sc, cs * sc], 0) * 8.0
    ssq = np.concatenate([sn * sc, sn * sc], 0) * 8.0
    csk = np.concatenate([cs / sc, cs / sc], 0)
    ssk = np.concatenate([sn / sc, sn / sc], 0)
    tabs = np.concatenate([csq, ssq, csk, ssk], 1).astype(ml_dtypes.bfloat16)
    return np.ascontiguousarray(tabs)


def _l0ffn_masks():
    import ml_dtypes
    m = np.zeros((128, 4 * 1024), np.float32)
    for qs in range(4):
        r = np.arange(128)[:, None]
        c = np.arange(1024)[None, :]
        qq = qs * 128 + r
        bad = (c < qq) | (c > qq + 512)
        m[:, qs * 1024:(qs + 1) * 1024] = np.where(bad, -30000.0, 0.0)
    return np.ascontiguousarray(m.astype(ml_dtypes.bfloat16))


def _rotm():
    import ml_dtypes
    m = np.zeros((64, 64), np.float32)
    for i in range(32):
        m[i + 32, i] = -1.0
    for i in range(32, 64):
        m[i - 32, i] = 1.0
    return np.ascontiguousarray(m.astype(ml_dtypes.bfloat16))


def _run_l0ffn(Rs, P):
    nc = _get_built("l0ffn", build_l0ffn)
    qk = np.stack([P["q_scale"][0], P["k_scale"][0],
                   P["q_scale"][1], P["k_scale"][1]], axis=1).astype(np.float32)
    lnab = np.stack([P["ln_g"][0], P["ln_b"][0], P["ln_g"][1], P["ln_b"][1]],
                    axis=1).astype(np.float32)
    b1v = np.stack([P["b1"][0], P["b1"][1]], axis=1).astype(np.float32)
    b2v = np.stack([P["b2"][0], P["b2"][1]], axis=1).astype(np.float32)
    base = dict(wqkv=P["wqkv"].reshape(2048, 1536), wo=P["wo"].reshape(1024, 1024),
                w1=P["w1"].reshape(2048, 5460), w2=P["w2"].reshape(5460, 1024),
                qk_sc=qk, lnab=lnab, b1v=b1v, b2v=b2v, masks=_l0ffn_masks(),
                rotm=_rotm(), ident=_ident128(), cconst=_cconst_arr("l0ffn"))
    base = {k: np.ascontiguousarray(v) for k, v in base.items()}
    maps = []
    for c in range(8):
        h = c % 2
        t0 = h * 2048
        Rext = np.zeros((1024, 3588), np.float32)
        if h == 0:
            Rext[:, 1537:3585] = Rs[c]
        else:
            Rext[:, 1:1537] = Rs[c - 1][:, 512:2048]
            Rext[:, 1537:3585] = Rs[c]
            Rext[:, 0] = Rs[c - 1][:, 511]
        flags = np.zeros((128, 8), np.float32)
        if h == 0:
            flags[:, 3] = -30000.0
        m = dict(base)
        m["R"] = Rext
        m["tabs"] = _l0ffn_tables(t0)
        m["flags"] = np.ascontiguousarray(flags)
        maps.append(m)
    res = run_bass_kernel_spmd(nc, maps, core_ids=list(range(8)))
    return [np.asarray(res.results[c]["Rout"], np.float32) for c in range(8)]


def kernel(**inputs):
    Rs = _feat_major_cores(np.asarray(inputs["L0_attn_a_x"], np.float32))
    Rs = _run_l0a(Rs, _sub_params("l0a", inputs))
    pf = _sub_params("l0ffn", inputs)
    Rs = _run_l0ffn(Rs, pf)
    Rs = _run_l0b(Rs, _sub_params("l0b", inputs))
    Rs = _run_l0ffn(Rs, pf)
    Rs = _run_l1a(Rs, _sub_params("l1a", inputs))
    pl1f = _sub_params("l1ffn", inputs)
    Rs = _run_l1ffn(Rs, pl1f)
    Rs = _run_l1b(Rs, _sub_params("l1b", inputs))
    Rs = _run_l1ffn(Rs, pl1f)
    return _cores_to_full(Rs)


# version 06 (final)
# speedup vs baseline: 1.0657x; 1.0657x over previous
import contextlib
import math
import numpy as np

import concourse.bass as bass
import concourse.mybir as mybir
from concourse.bass_utils import run_bass_kernel_spmd

F32 = mybir.dt.float32
BF16 = mybir.dt.bfloat16
AF = mybir.ActivationFunctionType
ALU = mybir.AluOpType
AX = mybir.AxisListType

D_MODEL = 1024
T_CORE = 2048
N_CORES = 8


ENGINES = ("sp", "act", "dve", "pool", "pe")


class Buf:
    __slots__ = ("name", "last_w", "readers")

    def __init__(self, name):
        self.name = name
        self.last_w = None
        self.readers = []


class Step:
    __slots__ = ("engine", "fn", "rd", "wr", "is_dma", "seq", "eseq",
                 "deps", "tagged", "tagidx", "vclock", "coll", "_waits")

    def __init__(self, engine, fn, rd, wr, is_dma, coll=False):
        self.engine = engine
        self.fn = fn
        self.rd = rd
        self.wr = wr
        self.is_dma = is_dma
        self.coll = coll
        self.deps = []
        self.tagged = False
        self.tagidx = 0
        self.vclock = {}


PARANOID = [1]


class MK:
    def __init__(self, name="mk"):
        self.nc = bass.Bass("TRN2", target_bir_lowering=False)
        self.steps = []
        self.stack = contextlib.ExitStack()
        self._nbuf = 0
        self.sb_used = 0
        self._finalized = False
        self.paranoid = PARANOID[0]

    def sbuf(self, name, shape, dtype):
        t = self.stack.enter_context(self.nc.sbuf_tensor(name, list(shape), dtype))
        nb = int(np.prod(shape[1:])) * (2 if dtype == BF16 else 4)
        self.sb_used += nb
        return t, Buf(name)

    def psum(self, name, shape=(128, 512), dtype=F32):
        t = self.stack.enter_context(self.nc.psum_tensor(name, list(shape), dtype))
        return t, Buf(name)

    def buf(self, name=None):
        self._nbuf += 1
        return Buf(name or f"b{self._nbuf}")

    def step(self, engine, fn, rd=(), wr=(), is_dma=False, coll=False):
        s = Step(engine, fn, list(rd), list(wr), is_dma, coll)
        s.seq = len(self.steps)
        self.steps.append(s)
        return s

    def op(self, engine, fn, rd=(), wr=()):
        return self.step(engine, fn, rd, wr, False)

    def dma(self, queue, out_ap, in_ap, rd=(), wr=(), **kw):
        return self.step(queue, lambda e: e.dma_start(out=out_ap, in_=in_ap, **kw),
                         rd, wr, True)

    def resolve(self):
        for s in self.steps:
            def add_dep(t):
                if t is None or t is s:
                    return
                if (not self.paranoid) and t.engine == s.engine \
                        and not t.is_dma and not s.is_dma:
                    return
                if t.seq < s.seq and t not in s.deps:
                    s.deps.append(t)
            for b in s.rd:
                add_dep(b.last_w)
            for b in s.wr:
                add_dep(b.last_w)
                for r in b.readers:
                    add_dep(r)
            for b in s.rd:
                b.readers.append(s)
            for b in s.wr:
                b.last_w = s
                b.readers = []
        for s in self.steps:
            if s.is_dma:
                s.tagged = True
            for d in s.deps:
                d.tagged = True
        counts = {}
        for s in self.steps:
            key = (s.engine, s.is_dma)
            if s.tagged:
                inc = 1 if s.coll else (16 if s.is_dma else 1)
                counts[key] = counts.get(key, 0) + inc
                s.tagidx = counts[key]

    def finalize(self):
        assert not self._finalized
        self._finalized = True
        self.resolve()
        nc = self.nc
        semkeys = set()
        for s in self.steps:
            if s.tagged:
                semkeys.add((s.engine, s.is_dma))
        sems = {}
        for key in sorted(semkeys):
            sems[key] = self.stack.enter_context(
                nc.semaphore(f"s_{key[0]}_{'d' if key[1] else 'c'}"))
        per_engine = {e: [] for e in ENGINES}
        for s in self.steps:
            per_engine[s.engine].append(s)
        known = {}
        last_on_engine = {e: None for e in ENGINES}
        for s in self.steps:
            vc = {}
            prev = last_on_engine[s.engine]
            if prev is not None:
                vc.update(known[prev])
            waits = []
            for d in s.deps:
                key = (d.engine, d.is_dma)
                need = d.tagidx
                if self.paranoid and key == (s.engine, s.is_dma):
                    if (key, need) not in waits:
                        waits.append((key, need))
                    if vc.get(key, 0) < need:
                        vc[key] = need
                    continue
                if vc.get(key, 0) < need:
                    waits.append((key, need))
                    vc[key] = need
                    if not self.paranoid:
                        for k2, v2 in known[d].items():
                            if vc.get(k2, 0) < v2:
                                vc[k2] = v2
            s._waits = waits
            k = dict(vc)
            if s.tagged:
                k[(s.engine, s.is_dma)] = s.tagidx
            known[s] = k
            last_on_engine[s.engine] = s

        block = self.stack.enter_context(nc.Block())

        max_wait = {}
        for s in self.steps:
            for (key, need) in s._waits:
                if key[1]:
                    max_wait[key] = max(max_wait.get(key, 0), need)

        def run_engine(eng_obj, engine_name):
            for s in per_engine[engine_name]:
                for (key, need) in s._waits:
                    eng_obj.wait_ge(sems[key], need)
                inst = s.fn(eng_obj)
                if s.tagged:
                    key = (s.engine, s.is_dma)
                    inc = 1 if s.coll else (16 if s.is_dma else 1)
                    if inst is None:
                        inst = eng_obj.nop()
                    inst.then_inc(sems[key], inc)
                    if s.is_dma and s.tagidx <= max_wait.get(key, 0):
                        eng_obj.wait_ge(sems[key], s.tagidx)

        finals = [(sems[key], counts_total) for key, counts_total in
                  [(k, max((s.tagidx for s in self.steps
                            if s.tagged and (s.engine, s.is_dma) == k), default=0))
                   for k in sems]
                  if counts_total > 0]

        @block.sync
        def _(sync):
            run_engine(sync, "sp")
            for sem, tot in finals:
                sync.wait_ge(sem, tot)

        @block.scalar
        def _(scalar):
            run_engine(scalar, "act")

        @block.vector
        def _(vector):
            run_engine(vector, "dve")

        @block.gpsimd
        def _(gpsimd):
            run_engine(gpsimd, "pool")

        @block.tensor
        def _(tensor):
            run_engine(tensor, "pe")

        self.stack.close()
        return nc



def ceil_div(a, b):
    return (a + b - 1) // b


def chunks_of(total, size):
    out = []
    o = 0
    while o < total:
        c = min(size, total - o)
        out.append((o, c))
        o += c
    return out


def chunks_ov(total, size=512):
    if total <= size:
        return [(0, total)]
    out = []
    o = 0
    while o + size < total:
        out.append((o, size))
        o += size
    out.append((total - size, size))
    return out



class CP:

    def __init__(self, name):
        self.mk = MK(name)
        self.nc = self.mk.nc
        self.pb = []
        for i in range(8):
            self.pb.append(self.mk.psum(f"pb{i}", (128, 512), F32))
        self._scr = {}
        self._uid = 0
        self._lnslot = 0
        self.lnw = 2112
        self.const_vals = []
        self._cconst_in = self.nc.dram_tensor(
            "cconst", [128, 16], F32, kind="ExternalInput").ap()
        self._cconst = self.sb("cconst_sb", (128, 16), F32)
        self.mk.dma("sp", self._cconst[0][:], self._cconst_in,
                    rd=(), wr=(self._cconst[1],))

    def const_col(self, val):
        val = float(val)
        if val not in self.const_vals:
            assert len(self.const_vals) < 16
            self.const_vals.append(val)
        i = self.const_vals.index(val)
        t, b = self._cconst
        return t[:, i:i + 1], b

    def uid(self):
        self._uid += 1
        return self._uid

    def sb(self, name, shape, dtype):
        return self.mk.sbuf(name, shape, dtype)

    def scratch(self, key, shape, dtype):
        k = (key, tuple(shape), str(dtype))
        if k not in self._scr:
            if dtype == "psum_bf16":
                self._scr[k] = self.mk.psum(f"scr_{key}", shape, BF16)
            else:
                self._scr[k] = self.sb(f"scr_{key}", shape, dtype)
        return self._scr[k]

    def din(self, name, shape, dtype=F32):
        return self.nc.dram_tensor(name, list(shape), dtype, kind="ExternalInput").ap()

    def dout(self, name, shape, dtype=F32):
        return self.nc.dram_tensor(name, list(shape), dtype, kind="ExternalOutput").ap()

    def load_tiles(self, hbm_ap, rows, cols, dtype_out=BF16, name=None,
                   conv_engine="pool", direct=False):
        name = name or f"w{self.uid()}"
        tiles = []
        nt = ceil_div(rows, 128)
        for j in range(nt):
            r = min(128, rows - j * 128)
            stage, sbuf_b = self.sb(f"{name}_s{j}", (r, cols), F32) if (dtype_out != F32 and not direct) else (None, None)
            dst, dst_b = self.sb(f"{name}_{j}", (r, cols), dtype_out)
            if dtype_out == F32 or direct:
                self.mk.dma("sp", dst[:], hbm_ap[j * 128:j * 128 + r, :],
                            rd=(), wr=(dst_b,))
            else:
                self.mk.dma("sp", stage[:], hbm_ap[j * 128:j * 128 + r, :],
                            rd=(), wr=(sbuf_b,))
                self.mk.op(conv_engine,
                           lambda e, o=dst, i=stage: e.tensor_copy(out=o[:], in_=i[:]),
                           rd=(sbuf_b,), wr=(dst_b,))
            tiles.append((dst, dst_b))
        return tiles

    def load_tiles_shared_stage(self, hbm_ap, rows, cols, name=None,
                                conv_engine="pool", stage=None):
        name = name or f"w{self.uid()}"
        if stage is None:
            stage = self.sb(f"{name}_stage", (128, cols), F32)
        st, st_b = stage
        tiles = []
        nt = rows // 128
        for j in range(nt):
            dst, dst_b = self.sb(f"{name}_{j}", (128, cols), BF16)
            self.mk.dma("sp", st[:, 0:cols], hbm_ap[j * 128:(j + 1) * 128, :],
                        rd=(st_b,), wr=(st_b,))
            self.mk.op(conv_engine,
                       lambda e, o=dst, i=st, c=cols: e.tensor_copy(out=o[:], in_=i[:, 0:c]),
                       rd=(st_b,), wr=(dst_b,))
            tiles.append((dst, dst_b))
        return tiles

    def mm_acc(self, bank, m, n_len, lhs_list, rhs_list, bank_row0=0):
        pt, pbuf = self.pb[bank]
        nk = len(lhs_list)
        for ki in range(nk):
            lap, lbuf = lhs_list[ki]
            rap, rbuf = rhs_list[ki]
            self.mk.op("pe",
                       lambda e, lap=lap, rap=rap, ki=ki, nk=nk: e.matmul(
                           pt[bank_row0:bank_row0 + m, 0:n_len], lap, rap,
                           start=(ki == 0), stop=(ki == nk - 1)),
                       rd=(lbuf, rbuf) + ((pbuf,) if ki == 0 else ()),
                       wr=(pbuf,))
        return pt, pbuf

    def replicate_row(self, bank, row_ap, row_buf, n_len, ones):
        pt, pbuf = self.pb[bank]
        ones_ap, ones_buf = ones
        self.mk.op("pe",
                   lambda e: e.matmul(pt[:, 0:n_len], ones_ap, row_ap,
                                      start=True, stop=True),
                   rd=(ones_buf, row_buf, pbuf), wr=(pbuf,))
        return pt, pbuf


def emit_ln_stats(cp, tiles, width, ones_bf, banks=(0, 1, 2, 3), eps=1e-5,
                  affine=None, mean_free=False):
    mk = cp.mk
    nt = len(tiles)
    D = nt * 128
    onesap, ones_buf = ones_bf
    nch = chunks_ov(width, 512)
    srow, srow_b = cp.scratch("ln_srow", (1, cp.lnw), F32)
    qrow, qrow_b = cp.scratch("ln_qrow", (1, cp.lnw), F32)
    casts = []
    for j, (tap, tbuf) in enumerate(tiles):
        cj, cjb = cp.scratch(("lncast", j), (128, cp.lnw), BF16)
        qj, qjb = cp.scratch(("lnsq", j), (128, cp.lnw), BF16)
        if not mean_free:
            mk.op("act", lambda e, i=tap, o=cj, w=width: e.activation(
                out=o[:, 0:w], in_=i[:, 0:w], func=AF.Copy), rd=(tbuf,), wr=(cjb,))
        mk.op("act", lambda e, i=tap, o=qj, w=width: e.activation(
            out=o[:, 0:w], in_=i[:, 0:w], func=AF.Square), rd=(tbuf,), wr=(qjb,))
        casts.append(((cj, cjb), (qj, qjb)))
    for ci, (off, ln) in enumerate(nch):
        pt_q, pb_q = cp.pb[banks[2 + ci % 2]]
        if not mean_free:
            pt_s, pb_s = cp.pb[banks[ci % 2]]
            for j in range(nt):
                cj, cjb = casts[j][0]
                mk.op("pe", lambda e, off=off, ln=ln, j=j, cj=cj, pt_s=pt_s: e.matmul(
                    pt_s[0:1, 0:ln], onesap, cj[:, off:off + ln],
                    start=(j == 0), stop=(j == nt - 1)),
                    rd=(ones_buf, cjb) + ((pb_s,) if j == 0 else ()), wr=(pb_s,))
            mk.op("dve", lambda e, off=off, ln=ln, pt_s=pt_s: e.tensor_scalar_mul(
                srow[0:1, off:off + ln], pt_s[0:1, 0:ln], 1.0 / D),
                rd=(pb_s,), wr=(srow_b,))
        for j in range(nt):
            qj, qjb = casts[j][1]
            mk.op("pe", lambda e, off=off, ln=ln, j=j, qj=qj, pt_q=pt_q: e.matmul(
                pt_q[0:1, 0:ln], onesap, qj[:, off:off + ln],
                start=(j == 0), stop=(j == nt - 1)),
                rd=(ones_buf, qjb) + ((pb_q,) if j == 0 else ()), wr=(pb_q,))
        mk.op("dve", lambda e, off=off, ln=ln, pt_q=pt_q: e.tensor_scalar_mul(
            qrow[0:1, off:off + ln], pt_q[0:1, 0:ln], 1.0 / D),
            rd=(pb_q,), wr=(qrow_b,))
    arow, arow_b = cp.scratch(("ln_a", cp._lnslot), (1, cp.lnw), F32)
    brow, brow_b = cp.scratch(("ln_b", cp._lnslot), (1, cp.lnw), F32)
    cp._lnslot ^= 1
    tr, tr_b = cp.scratch("ln_t", (1, cp.lnw), F32)
    if mean_free:
        ep_ap, ep_b = cp.const_col(eps)
        mh_ap, mh_b = cp.const_col(-0.5)
        mk.op("act", lambda e: e.activation(out=tr[0:1, 0:width], in_=qrow[0:1, 0:width], func=AF.Ln,
                                            bias=ep_ap[0:1, :]), rd=(qrow_b, ep_b), wr=(tr_b,))
        mk.op("act", lambda e: e.activation(out=arow[0:1, 0:width], in_=tr[0:1, 0:width], func=AF.Exp,
                                            scale=mh_ap[0:1, :]), rd=(tr_b, mh_b), wr=(arow_b,))
        mk.op("dve", lambda e: e.memset(brow[0:1, 0:width], 0.0), rd=(), wr=(brow_b,))
    else:
        mk.op("act", lambda e: e.activation(out=tr[0:1, 0:width], in_=srow[0:1, 0:width], func=AF.Square),
              rd=(srow_b,), wr=(tr_b,))
        mk.op("dve", lambda e: e.tensor_sub(tr[0:1, 0:width], qrow[0:1, 0:width], tr[0:1, 0:width]),
              rd=(qrow_b, tr_b), wr=(tr_b,))
        ep_ap, ep_b = cp.const_col(eps)
        mh_ap, mh_b = cp.const_col(-0.5)
        mk.op("act", lambda e: e.activation(out=tr[0:1, 0:width], in_=tr[0:1, 0:width], func=AF.Ln,
                                            bias=ep_ap[0:1, :]), rd=(tr_b, ep_b), wr=(tr_b,))
        mk.op("act", lambda e: e.activation(out=arow[0:1, 0:width], in_=tr[0:1, 0:width], func=AF.Exp,
                                            scale=mh_ap[0:1, :]), rd=(tr_b, mh_b), wr=(arow_b,))
        mk.op("dve", lambda e: e.scalar_tensor_tensor(
            out=brow[0:1, 0:width], in0=srow[0:1, 0:width], scalar=-1.0, in1=arow[0:1, 0:width],
            op0=ALU.mult, op1=ALU.mult), rd=(srow_b, arow_b), wr=(brow_b,))
    return (arow, arow_b), (brow, brow_b)


def emit_rowscale_apply(cp, tiles_in, tiles_out, width, arow, brow, ones_f32,
                        bank_a=4, bank_b=5, affine=None, out_dtype=None):
    mk = cp.mk
    a_ap, a_b = arow
    b_ap, b_b = brow
    nch = chunks_ov(width, 512)
    tmp, tmp_b = cp.scratch("rowscale_tmp", (128, 512), F32)
    applied = 0
    for ci, (off, ln) in enumerate(nch):
        pa, pa_b = cp.replicate_row(bank_a, a_ap[0:1, off:off + ln], a_b, ln,
                                    ones_f32)
        pbk, pb_b = cp.replicate_row(bank_b, b_ap[0:1, off:off + ln], b_b, ln,
                                     ones_f32)
        s = max(off, applied)
        e_ = off + ln
        applied = e_
        w = e_ - s
        so = s - off
        for j, (tin, tin_b) in enumerate(tiles_in):
            tout, tout_b = tiles_out[j]
            if affine is None:
                mk.op("dve", lambda e, tin=tin, s=s, w=w, so=so, pa=pa: e.tensor_mul(
                    tmp[:, 0:w], tin[:, s:s + w], pa[:, so:so + w]),
                    rd=(tin_b, pa_b), wr=(tmp_b,))
                mk.op("dve", lambda e, tout=tout, s=s, w=w, so=so,
                      pbk=pbk: e.tensor_add(
                          tout[:, s:s + w], tmp[:, 0:w], pbk[:, so:so + w]),
                    rd=(tmp_b, pb_b), wr=(tout_b,))
            else:
                (g_ap, g_b), (be_ap, be_b) = affine[j]
                mk.op("dve", lambda e, tin=tin, s=s, w=w, so=so, pa=pa: e.tensor_mul(
                    tmp[:, 0:w], tin[:, s:s + w], pa[:, so:so + w]),
                    rd=(tin_b, pa_b), wr=(tmp_b,))
                mk.op("dve", lambda e, w=w, so=so, pbk=pbk: e.tensor_add(
                    tmp[:, 0:w], tmp[:, 0:w], pbk[:, so:so + w]),
                    rd=(tmp_b, pb_b), wr=(tmp_b,))
                mk.op("dve", lambda e, tout=tout, s=s, w=w,
                      g_ap=g_ap, be_ap=be_ap: e.scalar_tensor_tensor(
                          out=tout[:, s:s + w], in0=tmp[:, 0:w],
                          scalar=g_ap, in1=be_ap.broadcast_to([128, w]),
                          op0=ALU.mult, op1=ALU.add),
                    rd=(tmp_b, g_b, be_b), wr=(tout_b,))





def build_l0a(debug=False):
    cp = CP("l0a")
    cp.lnw = 520
    mk = cp.mk
    TQ = 512
    NT_D = 8
    NT_E = 16
    MAIN0 = 1

    R = cp.din("R", (1024, 2112))
    Win = cp.din("Win", (1024, 4096))
    Wdt = cp.din("Wdt", (2048, 2048))
    WB = cp.din("WB", (2048, 16))
    WC = cp.din("WC", (2048, 16))
    Wout = cp.din("Wout", (2048, 1024))
    Alog = cp.din("Alog", (2048, 16))
    vecs = cp.din("vecs", (2048, 3))
    convw = cp.din("convw", (2048, 3))
    nsb = cp.din("nsb", (1024, 2))
    hinit = cp.din("hinit", (2048, 16))
    sel16 = cp.din("sel16", (16, 2048), BF16)
    Rout = cp.dout("Rout", (1024, 2048))
    hout = cp.dout("hout", (2048, 16))
    if debug:
        d_xn = cp.dout("d_xn", (1024, 514), BF16)
        d_xb = cp.dout("d_xb", (2048, 512), BF16)
        d_dt0 = cp.dout("d_dt0", (128, 512))
        d_B = cp.dout("d_B", (16, 512), BF16)
        d_C = cp.dout("d_C", (16, 512), BF16)
        d_z0 = cp.dout("d_z0", (128, 512), BF16)
        d_y0 = cp.dout("d_y0", (128, 512))
        d_y20 = cp.dout("d_y20", (128, 512), BF16)
        d_y2a = cp.dout("d_y2a", (2048, 512), BF16)
        d_xn0 = cp.dout("d_xn0", (1024, 514))
        d_rows = cp.dout("d_rows", (4, 520))
        d_dta = cp.dout("d_dta", (2048, 512))
        d_za = cp.dout("d_za", (2048, 512), BF16)

    winb = cp.nc.dram_tensor("winb", [1024, 4096], BF16, kind="Internal").ap()
    wdtb = cp.nc.dram_tensor("wdtb", [2048, 2048], BF16, kind="Internal").ap()
    woutbh = cp.nc.dram_tensor("woutbh", [2048, 1024], BF16, kind="Internal").ap()
    winb_buf = cp.mk.buf("winb")
    wdtb_buf = cp.mk.buf("wdtb")
    woutbh_buf = cp.mk.buf("woutbh")

    ones_bf = cp.sb("ones_bf", (128, 1), BF16)
    mk.op("dve", lambda e: e.memset(ones_bf[0][:], 1.0), wr=(ones_bf[1],))
    ones_f = cp.sb("ones_f", (1, 128), F32)
    mk.op("dve", lambda e: e.memset(ones_f[0][:], 1.0), wr=(ones_f[1],))

    vec_t = cp.load_tiles(vecs, 2048, 3, dtype_out=F32, name="vec")
    cw_t = cp.load_tiles(convw, 2048, 3, dtype_out=F32, name="cw")
    nsb_t = cp.load_tiles(nsb, 1024, 2, dtype_out=F32, name="nsb")
    alog_t = cp.load_tiles(Alog, 2048, 16, dtype_out=F32, name="alog")
    hcur = cp.load_tiles(hinit, 2048, 16, dtype_out=F32, name="hcur")
    sel_t = cp.load_tiles(sel16, 16, 2048, dtype_out=BF16, name="sel", direct=True)
    wb_t = cp.load_tiles(WB, 2048, 16, dtype_out=BF16, name="wbt")
    wc_t = cp.load_tiles(WC, 2048, 16, dtype_out=BF16, name="wct")

    aneg = []
    for j in range(NT_E):
        t, tb = cp.sb(f"aneg{j}", (128, 16), F32)
        src, srcb = alog_t[j]
        mk.op("act", lambda e, src=src, t=t: e.activation(
            out=t[:], in_=src[:], func=AF.Exp), rd=(srcb,), wr=(tb,))
        mk.op("dve", lambda e, t=t: e.tensor_scalar_mul(t[:], t[:], -1.0),
              rd=(tb,), wr=(tb,))
        aneg.append((t, tb))

    brep, brep_b = cp.sb("brep", (128, 16 * 512), BF16)
    crep, crep_b = cp.sb("crep", (128, 16 * 512), BF16)
    stg = crep[:].bitcast(F32)[:, 0:2048]
    stg_b = crep_b
    stgb = brep[:, 0:4096]
    stgb_b = brep_b
    for j in range(8):
        for hh in range(2):
            mk.dma("sp", stg[:, 0:2048], Win[j * 128:(j + 1) * 128, hh * 2048:(hh + 1) * 2048],
                   rd=(stg_b,), wr=(stg_b,))
            mk.op("pool", lambda e, hh=hh: e.tensor_copy(
                out=stgb[:, hh * 2048:(hh + 1) * 2048], in_=stg[:, 0:2048]),
                rd=(stg_b, stgb_b), wr=(stgb_b,))
        mk.dma("sp", winb[j * 128:(j + 1) * 128, :], stgb[:],
               rd=(stgb_b,), wr=(winb_buf,))

    for j in range(16):
        mk.dma("sp", stg[:, 0:1024], Wout[j * 128:(j + 1) * 128, :],
               rd=(stg_b,), wr=(stg_b,))
        mk.op("pool", lambda e: e.tensor_copy(out=stgb[:, 0:1024], in_=stg[:, 0:1024]),
              rd=(stg_b, stgb_b), wr=(stgb_b,))
        mk.dma("sp", woutbh[j * 128:(j + 1) * 128, :], stgb[:, 0:1024],
               rd=(stgb_b,), wr=(woutbh_buf,))
    for j in range(16):
        mk.dma("sp", stg[:, 0:2048], Wdt[j * 128:(j + 1) * 128, :],
               rd=(stg_b,), wr=(stg_b,))
        mk.op("pool", lambda e: e.tensor_copy(out=stgb[:, 0:2048], in_=stg[:, 0:2048]),
              rd=(stg_b, stgb_b), wr=(stgb_b,))
        mk.dma("sp", wdtb[j * 128:(j + 1) * 128, :], stgb[:, 0:2048],
               rd=(stgb_b,), wr=(wdtb_buf,))

    WQ = TQ + 2
    Rt = [cp.sb(f"Rt{j}", (128, WQ), F32) for j in range(NT_D)]
    xn = [cp.sb(f"xn{j}", (128, WQ), BF16) for j in range(NT_D)]
    xb = [cp.sb(f"xb{j}", (128, TQ), BF16) for j in range(NT_E)]
    xzf, xzf_b = cp.sb("xzf", (128, WQ), F32)
    wstr = [cp.sb(f"wstr{i}", (128, 2048), BF16) for i in range(2)]
    bsb, bsb_b = cp.sb("bsb", (16, TQ), BF16)
    csb, csb_b = cp.sb("csb", (16, TQ), BF16)
    dtt, dtt_b = cp.sb("dtt", (128, TQ), F32)
    sps, sps_b = cp.sb("sps", (128, TQ), F32)
    spm, spm_b = cp.sb("spm", (128, TQ), F32)
    dtx, dtx_b = cp.sb("dtx", (128, TQ), BF16)
    a_scr = [cp.sb(f"ascr{i}", (128, TQ), F32) for i in range(2)]
    u_scr = [cp.sb(f"uscr{i}", (128, TQ), BF16) for i in range(2)]
    h_all, h_all_b = cp.sb("h_all", (128, 16 * TQ), BF16)
    yv, yv_b = cp.sb("yv", (128, TQ), F32)
    y2 = [cp.sb(f"y2{j}", (128, TQ), BF16) for j in range(NT_E)]
    zs, zs_b = cp.sb("zs", (128, TQ), BF16)
    t1s, t1s_b = cp.sb("t1s", (128, TQ), F32)
    rm, rm_b = cp.sb("rm", (128, TQ), F32)
    rout_s, rout_b = cp.sb("rout_s", (128, TQ), F32)

    for q in range(4):
        e0 = q * TQ
        for j in range(NT_D):
            mk.dma("sp", Rt[j][0][:], R[j * 128:(j + 1) * 128, e0:e0 + WQ],
                   rd=(Rt[j][1],), wr=(Rt[j][1],))
        arow, brow = emit_ln_stats(cp, Rt, WQ, (ones_bf[0][:], ones_bf[1]))
        if debug and q == 0:
            sr, sr_b = cp.scratch("ln_srow", (1, cp.lnw), F32)
            qr, qr_b = cp.scratch("ln_qrow", (1, cp.lnw), F32)
            mk.dma("sp", d_rows[0:1, 0:514], sr[0:1, 0:514], rd=(sr_b,), wr=())
            mk.dma("sp", d_rows[1:2, 0:514], qr[0:1, 0:514], rd=(qr_b,), wr=())
            mk.dma("sp", d_rows[2:3, 0:514], arow[0][0:1, 0:514], rd=(arow[1],), wr=())
            mk.dma("sp", d_rows[3:4, 0:514], brow[0][0:1, 0:514], rd=(brow[1],), wr=())
        emit_rowscale_apply(cp, Rt, Rt, WQ, arow, brow, (ones_f[0][:], ones_f[1]))
        arow2, brow2 = emit_ln_stats(cp, Rt, WQ, (ones_bf[0][:], ones_bf[1]))
        affine = [((nsb_t[j][0][:, 0:1], nsb_t[j][1]),
                   (nsb_t[j][0][:, 1:2], nsb_t[j][1])) for j in range(NT_D)]
        emit_rowscale_apply(cp, Rt, xn, WQ, arow2, brow2, (ones_f[0][:], ones_f[1]),
                            affine=affine)
        if debug and q == 0:
            for j in range(NT_D):
                mk.dma("sp", d_xn[j * 128:(j + 1) * 128, :], xn[j][0][:],
                       rd=(xn[j][1],), wr=())
                mk.dma("sp", d_xn0[j * 128:(j + 1) * 128, :], Rt[j][0][:],
                       rd=(Rt[j][1],), wr=())

        for ot in range(NT_E):
            ws, ws_b = wstr[ot % 2]
            mk.dma("sp", ws[:, 0:1024].rearrange("p (k c) -> p k c", k=8),
                   winb.rearrange("(k p) o -> p k o", p=128)[:, :, ot * 128:(ot + 1) * 128],
                   rd=(ws_b,), wr=(ws_b,))
            lhs = [(ws[:, k * 128:(k + 1) * 128], ws_b) for k in range(8)]
            rhs0 = [(xn[j][0][:, 0:512], xn[j][1]) for j in range(NT_D)]
            rhs1 = [(xn[j][0][:, 2:514], xn[j][1]) for j in range(NT_D)]
            cp.mm_acc(2, 128, 512, lhs, rhs0)
            cp.mm_acc(3, 128, 512, lhs, rhs1)
            pt2, pb2 = cp.pb[2]
            pt3, pb3 = cp.pb[3]
            mk.op("act", lambda e: e.activation(out=xzf[:, 0:512], in_=pt2[:, 0:512],
                                                func=AF.Copy), rd=(pb2,), wr=(xzf_b,))
            mk.op("act", lambda e: e.activation(out=xzf[:, 512:514], in_=pt3[:, 510:512],
                                                func=AF.Copy), rd=(pb3,), wr=(xzf_b,))
            cwj, cwj_b = cw_t[ot]
            cbj, cbj_b = vec_t[ot]
            xbj, xbj_b = xb[ot]
            mk.op("dve", lambda e, cwj=cwj, cbj=cbj: e.scalar_tensor_tensor(
                out=t1s[:], in0=xzf[:, 0:512], scalar=cwj[:, 0:1],
                in1=cbj[:, 0:1].broadcast_to([128, 512]), op0=ALU.mult, op1=ALU.add),
                rd=(xzf_b, cwj_b, cbj_b), wr=(t1s_b,))
            mk.op("dve", lambda e, cwj=cwj: e.scalar_tensor_tensor(
                out=t1s[:], in0=xzf[:, 1:513], scalar=cwj[:, 1:2],
                in1=t1s[:], op0=ALU.mult, op1=ALU.add),
                rd=(xzf_b, cwj_b, t1s_b), wr=(t1s_b,))
            mk.op("dve", lambda e, cwj=cwj: e.scalar_tensor_tensor(
                out=t1s[:], in0=xzf[:, 2:514], scalar=cwj[:, 2:3],
                in1=t1s[:], op0=ALU.mult, op1=ALU.add),
                rd=(xzf_b, cwj_b, t1s_b), wr=(t1s_b,))
            mk.op("act", lambda e, xbj=xbj: e.activation(
                out=xbj[:], in_=t1s[:], func=AF.Silu), rd=(t1s_b,), wr=(xbj_b,))
            if debug and q == 0:
                mk.dma("sp", d_xb[ot * 128:(ot + 1) * 128, :], xbj[:],
                       rd=(xbj_b,), wr=())

        for (wt, dst, dst_b) in ((wb_t, bsb, bsb_b), (wc_t, csb, csb_b)):
            lhs = [(wt[k][0][:], wt[k][1]) for k in range(NT_E)]
            rhs = [(xb[k][0][:], xb[k][1]) for k in range(NT_E)]
            pt, pbx = cp.mm_acc(0, 16, 512, lhs, rhs)
            mk.op("dve", lambda e, dst=dst, pt=pt: e.tensor_copy(
                out=dst[:], in_=pt[0:16, :]), rd=(pbx,), wr=(dst_b,))
            if debug and q == 0:
                mk.dma("sp", d_B if dst is bsb else d_C, dst[:],
                       rd=(dst_b,), wr=())
        selap, sel_b = sel_t[0]
        for n in range(16):
            for (src, src_b, dst, dst_b2) in ((bsb, bsb_b, brep, brep_b),
                                              (csb, csb_b, crep, crep_b)):
                pt, pbx = cp.pb[1]
                mk.op("pe", lambda e, n=n, src=src, pt=pt: e.matmul(
                    pt[:, 0:512], selap[:, n * 128:(n + 1) * 128], src[:],
                    start=True, stop=True), rd=(sel_b, src_b, pbx), wr=(pbx,))
                mk.op("dve", lambda e, n=n, dst=dst, pt=pt: e.tensor_copy(
                    out=dst[:, n * TQ:(n + 1) * TQ], in_=pt[:, 0:512]),
                    rd=(pbx,), wr=(dst_b2,))

        for j in range(NT_E):
            ws, ws_b = wstr[j % 2]
            mk.dma("sp", ws[:, 0:2048].rearrange("p (k c) -> p k c", k=16),
                   wdtb.rearrange("(k p) o -> p k o", p=128)[:, :, j * 128:(j + 1) * 128],
                   rd=(ws_b,), wr=(ws_b,))
            lhs = [(ws[:, k * 128:(k + 1) * 128], ws_b) for k in range(16)]
            rhs = [(xb[k][0][:], xb[k][1]) for k in range(NT_E)]
            ptd, pbd = cp.mm_acc(2, 128, 512, lhs, rhs)
            bdt_col = vec_t[j][0][:, 1:2]
            mk.op("dve", lambda e, bdt_col=bdt_col, ptd=ptd: e.tensor_scalar_add(
                sps[:], ptd[:, 0:512], bdt_col),
                rd=(pbd, vec_t[j][1]), wr=(sps_b,))
            mk.op("dve", lambda e: e.tensor_scalar_max(spm[:], sps[:], 0.0),
                  rd=(sps_b,), wr=(spm_b,))
            mk.op("dve", lambda e: e.scalar_tensor_tensor(
                out=sps[:], in0=sps[:], scalar=-1.0, in1=sps[:],
                op0=ALU.mult, op1=ALU.min), rd=(sps_b,), wr=(sps_b,))
            mk.op("act", lambda e: e.activation(out=sps[:], in_=sps[:], func=AF.Exp),
                  rd=(sps_b,), wr=(sps_b,))
            mk.op("act", lambda e: e.activation(out=sps[:], in_=sps[:], func=AF.Ln,
                                                bias=1.0), rd=(sps_b,), wr=(sps_b,))
            mk.op("dve", lambda e: e.tensor_add(dtt[:], spm[:], sps[:]),
                  rd=(spm_b, sps_b), wr=(dtt_b,))
            mk.op("dve", lambda e, j=j: e.tensor_mul(dtx[:], dtt[:], xb[j][0][:]),
                  rd=(dtt_b, xb[j][1]), wr=(dtx_b,))
            if debug and q == 0 and j == 0:
                mk.dma("sp", d_dt0, dtt[:], rd=(dtt_b,), wr=())
            if debug and q == 0:
                mk.dma("sp", d_dta[j * 128:(j + 1) * 128, :], dtt[:],
                       rd=(dtt_b,), wr=())
            ws2, ws2_b = wstr[(j + 1) % 2]
            mk.dma("sp", ws2[:, 0:1024].rearrange("p (k c) -> p k c", k=8),
                   winb.rearrange("(k p) o -> p k o", p=128)[:, :, 2048 + j * 128:2048 + (j + 1) * 128],
                   rd=(ws2_b,), wr=(ws2_b,))
            lhsz = [(ws2[:, k * 128:(k + 1) * 128], ws2_b) for k in range(8)]
            rhsz = [(xn[jj][0][:, MAIN0:MAIN0 + 512], xn[jj][1]) for jj in range(NT_D)]
            ptz, pbz = cp.mm_acc(3, 128, 512, lhsz, rhsz)
            mk.op("act", lambda e: e.activation(out=zs[:], in_=ptz[:, 0:512],
                                                func=AF.Silu), rd=(pbz,), wr=(zs_b,))
            if debug and q == 0 and j == 0:
                mk.dma("sp", d_z0, zs[:], rd=(zs_b,), wr=())
            if debug and q == 0:
                mk.dma("sp", d_za[j * 128:(j + 1) * 128, :], zs[:],
                       rd=(zs_b,), wr=())
            for n in range(16):
                asc, asc_b = a_scr[n % 2]
                usc, usc_b = u_scr[n % 2]
                an_col = aneg[j][0][:, n:n + 1]
                mk.op("act", lambda e, an_col=an_col, asc=asc: e.activation(
                    out=asc[:], in_=dtt[:], func=AF.Exp, scale=an_col),
                    rd=(dtt_b, aneg[j][1]), wr=(asc_b,))
                mk.op("pool", lambda e, n=n, usc=usc: e.tensor_tensor(
                    out=usc[:], in0=dtx[:], in1=brep[:, n * TQ:(n + 1) * TQ],
                    op=ALU.mult), rd=(dtx_b, brep_b), wr=(usc_b,))
                hj, hj_b = hcur[j]
                mk.op("dve", lambda e, n=n, asc=asc, usc=usc, hj=hj: e.tensor_tensor_scan(
                    out=h_all[:, n * TQ:(n + 1) * TQ], data0=asc[:], data1=usc[:],
                    initial=hj[:, n:n + 1], op0=ALU.mult, op1=ALU.add),
                    rd=(asc_b, usc_b, hj_b), wr=(h_all_b,))
            hj, hj_b = hcur[j]
            for n in range(16):
                mk.op("dve", lambda e, hj=hj, n=n: e.tensor_copy(
                    out=hj[:, n:n + 1],
                    in_=h_all[:, (n + 1) * TQ - 1:(n + 1) * TQ]),
                    rd=(h_all_b,), wr=(hj_b,))
            mk.op("pool", lambda e: e.tensor_tensor(
                out=h_all[:], in0=h_all[:], in1=crep[:], op=ALU.mult),
                rd=(h_all_b, crep_b), wr=(h_all_b,))
            mk.op("dve", lambda e: e.tensor_reduce(
                out=yv[:], in_=h_all[:].rearrange("p (n t) -> p t n", n=16),
                axis=AX.X, op=ALU.add), rd=(h_all_b,), wr=(yv_b,))
            if debug and q == 0 and j == 0:
                mk.dma("sp", d_y0, yv[:], rd=(yv_b,), wr=())
            d_col = vec_t[j][0][:, 2:3]
            mk.op("dve", lambda e, j=j, d_col=d_col: e.scalar_tensor_tensor(
                out=t1s[:], in0=xb[j][0][:], scalar=d_col, in1=yv[:],
                op0=ALU.mult, op1=ALU.add), rd=(xb[j][1], vec_t[j][1], yv_b),
                wr=(t1s_b,))
            mk.op("dve", lambda e, j=j: e.tensor_mul(y2[j][0][:], t1s[:], zs[:]),
                  rd=(t1s_b, zs_b), wr=(y2[j][1],))
            if debug and q == 0 and j == 0:
                mk.dma("sp", d_y20, y2[j][0][:], rd=(y2[j][1],), wr=())
            if debug and q == 0:
                mk.dma("sp", d_y2a[j * 128:(j + 1) * 128, :], y2[j][0][:],
                       rd=(y2[j][1],), wr=())

        for k in range(NT_E):
            wk, wk_b = wstr[k % 2]
            mk.dma("sp", wk[:, 0:1024], woutbh[k * 128:(k + 1) * 128, :],
                   rd=(wk_b, woutbh_buf), wr=(wk_b,))
            for ot in range(8):
                pt, pbx = cp.pb[ot]
                mk.op("pe", lambda e, k=k, ot=ot, wk=wk, pt=pt: e.matmul(
                    pt[:], wk[:, ot * 128:(ot + 1) * 128], y2[k][0][:],
                    start=(k == 0), stop=(k == NT_E - 1)),
                    rd=(wk_b, y2[k][1]) + ((pbx,) if k == 0 else ()), wr=(pbx,))
        for ot in range(8):
            pt, pbx = cp.pb[ot]
            mk.dma("sp", rm[:], R[ot * 128:(ot + 1) * 128, MAIN0 + e0:MAIN0 + e0 + TQ],
                   rd=(rm_b,), wr=(rm_b,))
            mk.op("dve", lambda e, pt=pt: e.tensor_add(rout_s[:], pt[:], rm[:]),
                  rd=(pbx, rm_b), wr=(rout_b,))
            mk.op("dve", lambda e, ot=ot: e.tensor_add(
                rout_s[:], rout_s[:], Rt[ot][0][:, 1:1 + TQ]),
                rd=(rout_b, Rt[ot][1]), wr=(rout_b,))
            mk.dma("sp", Rout[ot * 128:(ot + 1) * 128, e0:e0 + TQ], rout_s[:],
                   rd=(rout_b,), wr=())

    for j in range(NT_E):
        hj, hj_b = hcur[j]
        mk.dma("sp", hout[j * 128:(j + 1) * 128, :], hj[:], rd=(hj_b,), wr=())

    _CONST_VALS["l0a"] = list(cp.const_vals)
    return cp.mk.finalize()



_BUILD_CACHE = {}
_CONST_VALS = {}


def _get_built(name, builder):
    if name not in _BUILD_CACHE:
        _BUILD_CACHE[name] = builder()
    return _BUILD_CACHE[name]


def _cconst_arr(name):
    vals = _CONST_VALS.get(name, [])
    a = np.zeros((128, 16), np.float32)
    for i, v in enumerate(vals):
        a[:, i] = v
    return a


def _sel16():
    import ml_dtypes
    sel = np.zeros((16, 2048), ml_dtypes.bfloat16)
    for n in range(16):
        sel[n, n * 128:(n + 1) * 128] = 1.0
    return sel


def _feat_major_cores(x):
    out = []
    for b in range(4):
        for h in range(2):
            sl = x[b, h * 2048:(h + 1) * 2048, :]
            out.append(np.ascontiguousarray(sl.T.astype(np.float32)))
    return out


def _cores_to_full(Rs):
    x = np.zeros((4, 4096, 1024), np.float32)
    for b in range(4):
        for h in range(2):
            x[b, h * 2048:(h + 1) * 2048, :] = Rs[b * 2 + h].T
    return x


def _run_l0a(Rs, P):
    nc = _get_built("l0a", build_l0a)
    vecs = np.stack([P["conv_b"], P["b_dt"], P["D_param"]], axis=1).astype(np.float32)
    nsb = np.stack([P["norm_scale"], P["norm_bias"]], axis=1).astype(np.float32)
    base = dict(Win=P["W_in"], Wdt=P["W_dt"], WB=P["W_B"], WC=P["W_C"],
                Wout=P["W_out"], Alog=P["A_log"], vecs=vecs,
                convw=P["conv_w"], nsb=nsb, sel16=_sel16(),
                cconst=_cconst_arr("l0a"))
    base = {k: np.ascontiguousarray(v) for k, v in base.items()}

    def mk_maps(houts):
        maps = []
        for c in range(8):
            h = c % 2
            Rext = np.zeros((1024, 2112), np.float32)
            Rext[:, 1:2049] = Rs[c]
            if h == 1:
                Rext[:, 0] = Rs[c - 1][:, -1]
            if h == 0:
                Rext[:, 2049] = Rs[c + 1][:, 0]
            hin = (houts[c - 1] if (h == 1 and houts is not None)
                   else np.zeros((2048, 16), np.float32))
            m = dict(base)
            m["R"] = Rext
            m["hinit"] = hin
            maps.append(m)
        return maps

    r1 = run_bass_kernel_spmd(nc, mk_maps(None), core_ids=list(range(8)))
    houts = [r1.results[c]["hout"] for c in range(8)]
    r2 = run_bass_kernel_spmd(nc, mk_maps(houts), core_ids=list(range(8)))
    out = []
    for c in range(8):
        res = r1 if c % 2 == 0 else r2
        out.append(np.asarray(res.results[c]["Rout"], np.float32))
    return out



def stage_weight_hbm(cp, hbm_in, rows, cols, name, transpose=False,
                     stage=None, ident=None, quant=None):
    mk = cp.mk
    nc = cp.nc
    shape = [cols, rows] if transpose else [rows, cols]
    wout = nc.dram_tensor(name, shape, BF16, kind="Internal").ap()
    wbuf = mk.buf(name)
    (stg, stg_b), (stgb, stgb_b) = stage
    nt = ceil_div(rows, 128)
    for j in range(nt):
        r = min(128, rows - j * 128)
        mk.dma("sp", stg[0:r, 0:cols], hbm_in[j * 128:j * 128 + r, :],
               rd=(stg_b,), wr=(stg_b,))
        if quant is not None:
            inv_ap, inv_b = quant
            t_q, t_qb = cp.scratch("quant_t", (128, 4096), F32)
            g1, g1b = cp.scratch("quant_g1", (128, 4096), F32)
            mk.op("dve", lambda e, r=r: e.tensor_scalar_mul(
                t_q[0:r, 0:cols], stg[0:r, 0:cols], inv_ap),
                rd=(stg_b, inv_b), wr=(t_qb,))
            mk.op("dve", lambda e, r=r: e.tensor_single_scalar(
                out=g1[0:r, 0:cols], in_=t_q[0:r, 0:cols], scalar=0.5,
                op=ALU.is_ge), rd=(t_qb,), wr=(g1b,))
            mk.op("dve", lambda e, r=r: e.tensor_single_scalar(
                out=t_q[0:r, 0:cols], in_=t_q[0:r, 0:cols], scalar=-0.5,
                op=ALU.is_le), rd=(t_qb,), wr=(t_qb,))
            mk.op("dve", lambda e, r=r: e.tensor_sub(
                stgb[0:r, 0:cols], g1[0:r, 0:cols], t_q[0:r, 0:cols]),
                rd=(g1b, t_qb), wr=(stgb_b,))
        else:
            mk.op("pool", lambda e, r=r: e.tensor_copy(
                out=stgb[0:r, 0:cols], in_=stg[0:r, 0:cols]),
                rd=(stg_b, stgb_b), wr=(stgb_b,))
        if not transpose:
            mk.dma("sp", wout[j * 128:j * 128 + r, :], stgb[0:r, 0:cols],
                   rd=(stgb_b,), wr=(wbuf,))
        else:
            idap, idbuf = ident
            ptb_f, ptb_b = cp.pb[7]
            ptb = ptb_f[:].bitcast(BF16)
            tout, tout_b = cp.scratch("tr_out", (128, 128), BF16)
            for c in range(ceil_div(cols, 128)):
                cw = min(128, cols - c * 128)
                mk.op("pe", lambda e, c=c, cw=cw, r=r: e.transpose(
                    ptb[0:cw, 0:r], stgb[0:r, c * 128:c * 128 + cw],
                    idap[0:r, 0:r]),
                    rd=(stgb_b, idbuf, ptb_b), wr=(ptb_b,))
                mk.op("dve", lambda e, cw=cw, r=r: e.tensor_copy(
                    out=tout[0:cw, 0:r], in_=ptb[0:cw, 0:r]),
                    rd=(ptb_b,), wr=(tout_b,))
                mk.dma("sp", wout[c * 128:c * 128 + cw, j * 128:j * 128 + r],
                       tout[0:cw, 0:r], rd=(tout_b,), wr=(wbuf,))
    return wout, wbuf



def build_l1ffn(debug=False):
    cp = CP("l1ffn")
    cp.lnw = 520
    mk = cp.mk
    TQ = 512

    R = cp.din("R", (1024, 2048))
    W1 = cp.din("W1", (4096, 1024))
    W2 = cp.din("W2", (1024, 4096))
    b1 = cp.din("b1", (4096, 1))
    b2 = cp.din("b2", (1024, 1))
    ident = cp.din("ident", (128, 128), BF16)
    Rout = cp.dout("Rout", (1024, 2048))

    ones_bf = cp.sb("ones_bf", (128, 1), BF16)
    mk.op("dve", lambda e: e.memset(ones_bf[0][:], 1.0), wr=(ones_bf[1],))
    ones_f = cp.sb("ones_f", (1, 128), F32)
    mk.op("dve", lambda e: e.memset(ones_f[0][:], 1.0), wr=(ones_f[1],))

    id_t = cp.load_tiles(ident, 128, 128, dtype_out=BF16, name="id", direct=True)
    b1_t = cp.load_tiles(b1, 4096, 1, dtype_out=F32, name="b1t")
    b2_t = cp.load_tiles(b2, 1024, 1, dtype_out=F32, name="b2t")

    stg = cp.sb("stg", (128, 4096), F32)
    stgb = cp.sb("stgb", (128, 4096), BF16)

    def mean_abs(W, rows, cols, accname):
        acc, acc_b = cp.sb(accname, (128, 1), F32)
        nt = rows // 128
        parts, parts_b = cp.scratch("ma_parts", (128, 32), F32)
        lvl1, lvl1_b = cp.scratch("ma_lvl1", (128, 32), F32)
        g = cols // 128
        for j in range(nt):
            mk.dma("sp", stg[0][:, 0:cols], W[j * 128:(j + 1) * 128, :],
                   rd=(stg[1],), wr=(stg[1],))
            mk.op("dve", lambda e, g=g: e.tensor_reduce(
                out=lvl1[:, 0:g],
                in_=stg[0][:, 0:cols].rearrange("p (g c) -> p g c", c=128),
                axis=AX.X, op=ALU.add, apply_absolute_value=True),
                rd=(stg[1],), wr=(lvl1_b,))
            mk.op("dve", lambda e, j=j, g=g: e.tensor_reduce(
                out=parts[:, j:j + 1],
                in_=lvl1[:, 0:g].rearrange("p (a c) -> p a c", a=1),
                axis=AX.X, op=ALU.add), rd=(lvl1_b,), wr=(parts_b,))
        mk.op("dve", lambda e, nt=nt: e.tensor_reduce(
            out=lvl1[:, 0:4],
            in_=parts[:, 0:nt].rearrange("p (a c) -> p a c", a=4)
            if nt % 4 == 0 else parts[:, 0:nt].rearrange("p (a c) -> p a c", a=1),
            axis=AX.X, op=ALU.add), rd=(parts_b,), wr=(lvl1_b,))
        if nt % 4 == 0:
            mk.op("dve", lambda e: e.tensor_reduce(
                out=acc[:], in_=lvl1[:, 0:4].rearrange("p (a c) -> p a c", a=1),
                axis=AX.X, op=ALU.add), rd=(lvl1_b,), wr=(acc_b,))
        else:
            mk.op("dve", lambda e: e.tensor_copy(out=acc[:], in_=lvl1[:, 0:1]),
                  rd=(lvl1_b,), wr=(acc_b,))
        pscr = cp.nc.dram_tensor(accname + "scr", [128], F32, kind="Internal").ap()
        pscr_b = mk.buf(accname + "scr")
        mk.dma("sp", pscr, acc[:], rd=(acc_b,), wr=(pscr_b,),
               allow_slow_non_contiguous=True)
        arow, arow_b = cp.sb(accname + "r", (1, 128), F32)
        mk.dma("sp", arow[:], pscr.rearrange("(a c) -> a c", a=1),
               rd=(pscr_b,), wr=(arow_b,))
        a8, a8_b = cp.sb(accname + "r8", (1, 8), F32)
        mk.op("dve", lambda e: e.tensor_reduce(
            out=a8[:], in_=arow[:].rearrange("p (g c) -> p g c", c=16),
            axis=AX.X, op=ALU.add), rd=(arow_b,), wr=(a8_b,))
        g1, g1_b = cp.sb(accname + "g", (1, 1), F32)
        mk.op("dve", lambda e: e.tensor_reduce(
            out=g1[:], in_=a8[:].rearrange("p (g c) -> p g c", g=1),
            axis=AX.X, op=ALU.add), rd=(a8_b,), wr=(g1_b,))
        mk.op("dve", lambda e: e.tensor_scalar_mul(g1[:], g1[:],
                                                   1.0 / (rows * cols)),
              rd=(g1_b,), wr=(g1_b,))
        mk.op("dve", lambda e: e.tensor_scalar_add(g1[:], g1[:], 1e-5),
              rd=(g1_b,), wr=(g1_b,))
        mk.op("dve", lambda e: e.reciprocal(g1[:], g1[:]), rd=(g1_b,), wr=(g1_b,))
        pt1, pbx1 = cp.pb[1]
        mk.op("pe", lambda e: e.matmul(pt1[:, 0:1], ones_f[0][:], g1[:],
                                       start=True, stop=True),
              rd=(ones_f[1], g1_b, pbx1), wr=(pbx1,))
        inv, inv_b = cp.sb(accname + "i", (128, 1), F32)
        mk.op("dve", lambda e: e.tensor_copy(out=inv[:], in_=pt1[:, 0:1]),
              rd=(pbx1,), wr=(inv_b,))
        return inv, inv_b

    inv1 = mean_abs(W1, 4096, 1024, "ma1")
    inv2 = mean_abs(W2, 1024, 4096, "ma2")

    wq1t, wq1t_b = stage_weight_hbm(cp, W1, 4096, 1024, "wq1t", transpose=True,
                                    stage=(stg, stgb), ident=(id_t[0][0][:], id_t[0][1]),
                                    quant=(inv1[0][:], inv1[1]))
    wq2t, wq2t_b = stage_weight_hbm(cp, W2, 1024, 4096, "wq2t", transpose=True,
                                    stage=(stg, stgb), ident=(id_t[0][0][:], id_t[0][1]),
                                    quant=(inv2[0][:], inv2[1]))

    Rt = [cp.sb(f"Rt{j}", (128, TQ), F32) for j in range(8)]
    xn = [cp.sb(f"xn{j}", (128, TQ), BF16) for j in range(8)]
    hbuf = [cp.sb(f"hb{j}", (128, TQ), BF16) for j in range(32)]
    wstr = [cp.sb(f"wstr{i}", (128, 2048), BF16) for i in range(2)]
    tt1, tt1_b = cp.sb("tt1", (128, TQ), F32)
    routs, routs_b = cp.sb("routs", (128, TQ), F32)

    for q in range(4):
        e0 = q * TQ
        for j in range(8):
            mk.dma("sp", Rt[j][0][:], R[j * 128:(j + 1) * 128, e0:e0 + TQ],
                   rd=(Rt[j][1],), wr=(Rt[j][1],))
        arow, brow = emit_ln_stats(cp, Rt, TQ, (ones_bf[0][:], ones_bf[1]))
        emit_rowscale_apply(cp, Rt, xn, TQ, arow, brow, (ones_f[0][:], ones_f[1]))
        for ot in range(32):
            ws, ws_b = wstr[ot % 2]
            mk.dma("sp", ws[:, 0:1024].rearrange("p (k c) -> p k c", k=8),
                   wq1t.rearrange("(k p) o -> p k o", p=128)[:, :, ot * 128:(ot + 1) * 128],
                   rd=(ws_b, wq1t_b), wr=(ws_b,))
            lhs = [(ws[:, k * 128:(k + 1) * 128], ws_b) for k in range(8)]
            rhs = [(xn[k][0][:], xn[k][1]) for k in range(8)]
            pt, pbx = cp.mm_acc(2 + ot % 2, 128, TQ, lhs, rhs)
            bcol = b1_t[ot][0][:, 0:1]
            mk.op("dve", lambda e, pt=pt, bcol=bcol, ot=ot: e.tensor_scalar_add(
                tt1[:], pt[:, 0:TQ], bcol), rd=(pbx, b1_t[ot][1]), wr=(tt1_b,))
            mk.op("act", lambda e, ot=ot: e.activation(
                out=hbuf[ot][0][:], in_=tt1[:], func=AF.Relu),
                rd=(tt1_b,), wr=(hbuf[ot][1],))
        for ot in range(8):
            ws, ws_b = wstr[ot % 2]
            mk.dma("sp", ws[:, 0:2048].rearrange("p (g c) -> p g c", g=16),
                   wq2t.rearrange("(k p) o -> p k o", p=128)[:, 0:16, ot * 128:(ot + 1) * 128],
                   rd=(ws_b, wq2t_b), wr=(ws_b,))
            ws2, ws2_b = wstr[(ot + 1) % 2]
            mk.dma("sp", ws2[:, 0:2048].rearrange("p (k c) -> p k c", k=16),
                   wq2t.rearrange("(k p) o -> p k o", p=128)[:, 16:32, ot * 128:(ot + 1) * 128],
                   rd=(ws2_b, wq2t_b), wr=(ws2_b,))
            lhs = [(ws[:, k * 128:(k + 1) * 128], ws_b) for k in range(16)] + \
                  [(ws2[:, k * 128:(k + 1) * 128], ws2_b) for k in range(16)]
            rhs = [(hbuf[k][0][:], hbuf[k][1]) for k in range(32)]
            pt, pbx = cp.mm_acc(4 + ot % 4, 128, TQ, lhs, rhs)
            bcol = b2_t[ot][0][:, 0:1]
            mk.op("dve", lambda e, pt=pt, bcol=bcol, ot=ot: e.tensor_scalar_add(
                routs[:], pt[:, 0:TQ], bcol), rd=(pbx, b2_t[ot][1]), wr=(routs_b,))
            mk.op("dve", lambda e, ot=ot: e.tensor_add(
                routs[:], routs[:], Rt[ot][0][:]),
                rd=(routs_b, Rt[ot][1]), wr=(routs_b,))
            mk.dma("sp", Rout[ot * 128:(ot + 1) * 128, e0:e0 + TQ], routs[:],
                   rd=(routs_b,), wr=())

    if debug:
        dq1 = cp.dout("dq1", (1024, 4096), BF16)
        dq2 = cp.dout("dq2", (4096, 1024), BF16)
        dinv1 = cp.dout("dinv1", (128, 1))
        dinv2 = cp.dout("dinv2", (128, 1))
        dbg, dbg_b = cp.sb("dbgw", (128, 4096), BF16)
        mk.dma("sp", dinv1, inv1[0][:], rd=(inv1[1],), wr=())
        mk.dma("sp", dinv2, inv2[0][:], rd=(inv2[1],), wr=())
        for j in range(8):
            mk.dma("sp", dbg[:, 0:4096], wq1t[j * 128:(j + 1) * 128, :],
                   rd=(dbg_b, wq1t_b), wr=(dbg_b,))
            mk.dma("sp", dq1[j * 128:(j + 1) * 128, :], dbg[:, 0:4096],
                   rd=(dbg_b,), wr=(dbg_b,))
        for j in range(32):
            mk.dma("sp", dbg[:, 0:1024], wq2t[j * 128:(j + 1) * 128, :],
                   rd=(dbg_b, wq2t_b), wr=(dbg_b,))
            mk.dma("sp", dq2[j * 128:(j + 1) * 128, :], dbg[:, 0:1024],
                   rd=(dbg_b,), wr=(dbg_b,))

    _CONST_VALS["l1ffn"] = list(cp.const_vals)
    return cp.mk.finalize()


def _ident128():
    import ml_dtypes
    return np.eye(128, dtype=ml_dtypes.bfloat16)


def _sub_params(name, inputs):
    pref = {"l0a": "L0_attn_a_", "l0ffn": "L0_ffn_", "l0b": "L0_attn_b_",
            "l1a": "L1_attn_a_", "l1ffn": "L1_ffn_", "l1b": "L1_attn_b_"}[name]
    return {k[len(pref):]: np.asarray(v) for k, v in inputs.items()
            if k.startswith(pref) and k != "L0_attn_a_x"}


def _run_l1ffn(Rs, P):
    nc = _get_built("l1ffn", build_l1ffn)
    base = dict(W1=P["W1"], W2=P["W2"], b1=P["b1"].reshape(-1, 1),
                b2=P["b2"].reshape(-1, 1), ident=_ident128(),
                cconst=_cconst_arr("l1ffn"))
    base = {k: np.ascontiguousarray(v) for k, v in base.items()}
    maps = []
    for c in range(8):
        m = dict(base)
        m["R"] = np.ascontiguousarray(Rs[c])
        maps.append(m)
    res = run_bass_kernel_spmd(nc, maps, core_ids=list(range(8)))
    return [np.asarray(res.results[c]["Rout"], np.float32) for c in range(8)]


def build_l0b():
    cp = CP("l0b")
    cp.lnw = 520
    mk = cp.mk
    TQ = 512
    NT_D, NT_E = 8, 16
    MAIN0 = 3
    WQ = TQ + 3

    R = cp.din("R", (1024, 2112))
    skipw = cp.din("skipw", (2048, 1024))
    inw = cp.din("inw", (2048, 1024))
    wd1 = cp.din("wd1", (64, 2048))
    wd2 = cp.din("wd2", (2048, 64))
    wB = cp.din("wB", (16, 2048))
    wC = cp.din("wC", (16, 2048))
    outw = cp.din("outw", (1024, 2048))
    Alog = cp.din("Alog", (2048, 16))
    vecs = cp.din("vecs", (2048, 3))
    convw = cp.din("convw", (2048, 4))
    nrmw = cp.din("nrmw", (1024, 1))
    hinit = cp.din("hinit", (2048, 16))
    sel16 = cp.din("sel16", (16, 2048), BF16)
    ident = cp.din("ident", (128, 128), BF16)
    Rout = cp.dout("Rout", (1024, 2048))
    hout = cp.dout("hout", (2048, 16))

    ones_bf = cp.sb("ones_bf", (128, 1), BF16)
    mk.op("dve", lambda e: e.memset(ones_bf[0][:], 1.0), wr=(ones_bf[1],))
    ones_f = cp.sb("ones_f", (1, 128), F32)
    mk.op("dve", lambda e: e.memset(ones_f[0][:], 1.0), wr=(ones_f[1],))

    vec_t = cp.load_tiles(vecs, 2048, 3, dtype_out=F32, name="vec")
    cw_t = cp.load_tiles(convw, 2048, 4, dtype_out=F32, name="cw")
    nw_t = cp.load_tiles(nrmw, 1024, 1, dtype_out=F32, name="nw")
    alog_t = cp.load_tiles(Alog, 2048, 16, dtype_out=F32, name="alog")
    hcur = cp.load_tiles(hinit, 2048, 16, dtype_out=F32, name="hcur")
    sel_t = cp.load_tiles(sel16, 16, 2048, dtype_out=BF16, name="sel", direct=True)
    id_t = cp.load_tiles(ident, 128, 128, dtype_out=BF16, name="id", direct=True)

    aneg = []
    for j in range(NT_E):
        t, tb = cp.sb(f"aneg{j}", (128, 16), F32)
        src, srcb = alog_t[j]
        mk.op("act", lambda e, src=src, t=t: e.activation(
            out=t[:], in_=src[:], func=AF.Exp), rd=(srcb,), wr=(tb,))
        mk.op("dve", lambda e, t=t: e.tensor_scalar_mul(t[:], t[:], -1.0),
              rd=(tb,), wr=(tb,))
        aneg.append((t, tb))

    brep, brep_b = cp.sb("brep", (128, 16 * 512), BF16)
    crep, crep_b = cp.sb("crep", (128, 16 * 512), BF16)
    stg = crep[:].bitcast(F32)[:, 0:2048]
    stg_b = crep_b
    stgb = brep[:, 0:4096]
    stgb_b = brep_b

    ident_pair = (id_t[0][0][:], id_t[0][1])
    stage = ((stg, stg_b), (stgb, stgb_b))
    skipwT, skipwT_b = stage_weight_hbm(cp, skipw, 2048, 1024, "skipwT",
                                        transpose=True, stage=stage, ident=ident_pair)
    inwT, inwT_b = stage_weight_hbm(cp, inw, 2048, 1024, "inwT",
                                    transpose=True, stage=stage, ident=ident_pair)
    outwT, outwT_b = stage_weight_hbm(cp, outw, 1024, 2048, "outwT",
                                      transpose=True, stage=stage, ident=ident_pair)
    wd1T, wd1T_b = stage_weight_hbm(cp, wd1, 64, 2048, "wd1T",
                                    transpose=True, stage=stage, ident=ident_pair)
    wd2T, wd2T_b = stage_weight_hbm(cp, wd2, 2048, 64, "wd2T",
                                    transpose=True, stage=stage, ident=ident_pair)
    wBT, wBT_b = stage_weight_hbm(cp, wB, 16, 2048, "wBT",
                                  transpose=True, stage=stage, ident=ident_pair)
    wCT, wCT_b = stage_weight_hbm(cp, wC, 16, 2048, "wCT",
                                  transpose=True, stage=stage, ident=ident_pair)

    wd1s = cp.load_tiles(wd1T, 2048, 64, dtype_out=BF16, name="wd1s", direct=True)
    wd2s = cp.load_tiles(wd2T, 64, 2048, dtype_out=BF16, name="wd2s", direct=True)
    wbs = cp.load_tiles(wBT, 2048, 16, dtype_out=BF16, name="wbs", direct=True)
    wcs = cp.load_tiles(wCT, 2048, 16, dtype_out=BF16, name="wcs", direct=True)

    Rt = [cp.sb(f"Rt{j}", (128, WQ), F32) for j in range(NT_D)]
    xn = [cp.sb(f"xn{j}", (128, WQ), BF16) for j in range(NT_D)]
    xb = [cp.sb(f"xb{j}", (128, TQ), BF16) for j in range(NT_E)]
    xzf, xzf_b = cp.sb("xzf", (128, WQ), F32)
    wstr = [cp.sb(f"wstr{i}", (128, 2048), BF16) for i in range(2)]
    bsb, bsb_b = cp.sb("bsb", (16, TQ), BF16)
    csb, csb_b = cp.sb("csb", (16, TQ), BF16)
    rsb, rsb_b = cp.sb("rsb", (64, TQ), BF16)
    dtt, dtt_b = cp.sb("dtt", (128, TQ), F32)
    sps, sps_b = cp.sb("sps", (128, TQ), F32)
    spm, spm_b = cp.sb("spm", (128, TQ), F32)
    dtx, dtx_b = cp.sb("dtx", (128, TQ), BF16)
    a_scr = [cp.sb(f"ascr{i}", (128, TQ), F32) for i in range(2)]
    u_scr = [cp.sb(f"uscr{i}", (128, TQ), BF16) for i in range(2)]
    h_all, h_all_b = cp.sb("h_all", (128, 16 * TQ), BF16)
    yv, yv_b = cp.sb("yv", (128, TQ), F32)
    y2 = [cp.sb(f"y2{j}", (128, TQ), BF16) for j in range(NT_E)]
    zs, zs_b = cp.sb("zs", (128, TQ), BF16)
    t1s, t1s_b = cp.sb("t1s", (128, TQ), F32)
    rm, rm_b = cp.sb("rm", (128, TQ), F32)
    rout_s, rout_b = cp.sb("rout_s", (128, TQ), F32)

    for q in range(4):
        e0 = q * TQ
        for j in range(NT_D):
            mk.dma("sp", Rt[j][0][:], R[j * 128:(j + 1) * 128, e0:e0 + WQ],
                   rd=(Rt[j][1],), wr=(Rt[j][1],))
        arow0, brow0 = emit_ln_stats(cp, Rt, WQ, (ones_bf[0][:], ones_bf[1]))
        emit_rowscale_apply(cp, Rt, Rt, WQ, arow0, brow0,
                            (ones_f[0][:], ones_f[1]))
        arow, brow = emit_ln_stats(cp, Rt, WQ, (ones_bf[0][:], ones_bf[1]),
                                   mean_free=True)
        affine = [((nw_t[j][0][:, 0:1], nw_t[j][1]), cp.const_col(0.0))
                  for j in range(NT_D)]
        emit_rowscale_apply(cp, Rt, xn, WQ, arow, brow, (ones_f[0][:], ones_f[1]),
                            affine=affine)
        for ot in range(NT_E):
            ws, ws_b = wstr[ot % 2]
            mk.dma("sp", ws[:, 0:1024].rearrange("p (k c) -> p k c", k=8),
                   inwT.rearrange("(k p) o -> p k o", p=128)[:, :, ot * 128:(ot + 1) * 128],
                   rd=(ws_b, inwT_b), wr=(ws_b,))
            lhs = [(ws[:, k * 128:(k + 1) * 128], ws_b) for k in range(8)]
            rhs0 = [(xn[j][0][:, 0:512], xn[j][1]) for j in range(NT_D)]
            rhs1 = [(xn[j][0][:, 3:515], xn[j][1]) for j in range(NT_D)]
            cp.mm_acc(2, 128, 512, lhs, rhs0)
            cp.mm_acc(3, 128, 512, lhs, rhs1)
            pt2, pb2 = cp.pb[2]
            pt3, pb3 = cp.pb[3]
            mk.op("act", lambda e: e.activation(out=xzf[:, 0:512], in_=pt2[:, 0:512],
                                                func=AF.Copy), rd=(pb2,), wr=(xzf_b,))
            mk.op("act", lambda e: e.activation(out=xzf[:, 512:515], in_=pt3[:, 509:512],
                                                func=AF.Copy), rd=(pb3,), wr=(xzf_b,))
            cwj, cwj_b = cw_t[ot]
            cbj, cbj_b = vec_t[ot]
            xbj, xbj_b = xb[ot]
            mk.op("dve", lambda e, cwj=cwj, cbj=cbj: e.scalar_tensor_tensor(
                out=t1s[:], in0=xzf[:, 0:512], scalar=cwj[:, 0:1],
                in1=cbj[:, 0:1].broadcast_to([128, 512]), op0=ALU.mult, op1=ALU.add),
                rd=(xzf_b, cwj_b, cbj_b), wr=(t1s_b,))
            for kk in range(1, 4):
                mk.op("dve", lambda e, cwj=cwj, kk=kk: e.scalar_tensor_tensor(
                    out=t1s[:], in0=xzf[:, kk:kk + 512], scalar=cwj[:, kk:kk + 1],
                    in1=t1s[:], op0=ALU.mult, op1=ALU.add),
                    rd=(xzf_b, cwj_b, t1s_b), wr=(t1s_b,))
            mk.op("act", lambda e, xbj=xbj: e.activation(
                out=xbj[:], in_=t1s[:], func=AF.Silu), rd=(t1s_b,), wr=(xbj_b,))
        lhs = [(wd1s[k][0][:], wd1s[k][1]) for k in range(NT_E)]
        rhs = [(xb[k][0][:], xb[k][1]) for k in range(NT_E)]
        pt, pbx = cp.mm_acc(0, 64, 512, lhs, rhs)
        mk.op("dve", lambda e, pt=pt: e.tensor_copy(out=rsb[:], in_=pt[0:64, :]),
              rd=(pbx,), wr=(rsb_b,))
        for (wt, dst, dst_b) in ((wbs, bsb, bsb_b), (wcs, csb, csb_b)):
            lhs = [(wt[k][0][:], wt[k][1]) for k in range(NT_E)]
            pt, pbx = cp.mm_acc(0, 16, 512, lhs, rhs)
            mk.op("dve", lambda e, dst=dst, pt=pt: e.tensor_copy(
                out=dst[:], in_=pt[0:16, :]), rd=(pbx,), wr=(dst_b,))
        selap, sel_b = sel_t[0]
        for n in range(16):
            for (src, src_b, dst, dst_b2) in ((bsb, bsb_b, brep, brep_b),
                                              (csb, csb_b, crep, crep_b)):
                pt, pbx = cp.pb[1]
                mk.op("pe", lambda e, n=n, src=src, pt=pt: e.matmul(
                    pt[:, 0:512], selap[:, n * 128:(n + 1) * 128], src[:],
                    start=True, stop=True), rd=(sel_b, src_b, pbx), wr=(pbx,))
                mk.op("dve", lambda e, n=n, dst=dst, pt=pt: e.tensor_copy(
                    out=dst[:, n * TQ:(n + 1) * TQ], in_=pt[:, 0:512]),
                    rd=(pbx,), wr=(dst_b2,))
        for j in range(NT_E):
            wd2j = wd2s[0][0][:, j * 128:(j + 1) * 128]
            ptd, pbd = cp.pb[2]
            mk.op("pe", lambda e, wd2j=wd2j, ptd=ptd: e.matmul(
                ptd[:, 0:512], wd2j, rsb[:], start=True, stop=True),
                rd=(wd2s[0][1], rsb_b, pbd), wr=(pbd,))
            bdt_col = vec_t[j][0][:, 1:2]
            mk.op("dve", lambda e, bdt_col=bdt_col, ptd=ptd: e.tensor_scalar_add(
                sps[:], ptd[:, 0:512], bdt_col),
                rd=(pbd, vec_t[j][1]), wr=(sps_b,))
            mk.op("dve", lambda e: e.tensor_scalar_max(spm[:], sps[:], 0.0),
                  rd=(sps_b,), wr=(spm_b,))
            mk.op("dve", lambda e: e.scalar_tensor_tensor(
                out=sps[:], in0=sps[:], scalar=-1.0, in1=sps[:],
                op0=ALU.mult, op1=ALU.min), rd=(sps_b,), wr=(sps_b,))
            mk.op("act", lambda e: e.activation(out=sps[:], in_=sps[:], func=AF.Exp),
                  rd=(sps_b,), wr=(sps_b,))
            mk.op("act", lambda e: e.activation(out=sps[:], in_=sps[:], func=AF.Ln,
                                                bias=1.0), rd=(sps_b,), wr=(sps_b,))
            mk.op("dve", lambda e: e.tensor_add(dtt[:], spm[:], sps[:]),
                  rd=(spm_b, sps_b), wr=(dtt_b,))
            mk.op("dve", lambda e, j=j: e.tensor_mul(dtx[:], dtt[:], xb[j][0][:]),
                  rd=(dtt_b, xb[j][1]), wr=(dtx_b,))
            ws2, ws2_b = wstr[(j + 1) % 2]
            mk.dma("sp", ws2[:, 0:1024].rearrange("p (k c) -> p k c", k=8),
                   skipwT.rearrange("(k p) o -> p k o", p=128)[:, :, j * 128:(j + 1) * 128],
                   rd=(ws2_b, skipwT_b), wr=(ws2_b,))
            lhsz = [(ws2[:, k * 128:(k + 1) * 128], ws2_b) for k in range(8)]
            rhsz = [(xn[jj][0][:, MAIN0:MAIN0 + 512], xn[jj][1]) for jj in range(NT_D)]
            ptz, pbz = cp.mm_acc(3, 128, 512, lhsz, rhsz)
            mk.op("act", lambda e, ptz=ptz: e.activation(
                out=zs[:], in_=ptz[:, 0:512], func=AF.Silu), rd=(pbz,), wr=(zs_b,))
            for n in range(16):
                asc, asc_b = a_scr[n % 2]
                usc, usc_b = u_scr[n % 2]
                an_col = aneg[j][0][:, n:n + 1]
                mk.op("act", lambda e, an_col=an_col, asc=asc: e.activation(
                    out=asc[:], in_=dtt[:], func=AF.Exp, scale=an_col),
                    rd=(dtt_b, aneg[j][1]), wr=(asc_b,))
                mk.op("pool", lambda e, n=n, usc=usc: e.tensor_tensor(
                    out=usc[:], in0=dtx[:], in1=brep[:, n * TQ:(n + 1) * TQ],
                    op=ALU.mult), rd=(dtx_b, brep_b), wr=(usc_b,))
                hj, hj_b = hcur[j]
                mk.op("dve", lambda e, n=n, asc=asc, usc=usc, hj=hj: e.tensor_tensor_scan(
                    out=h_all[:, n * TQ:(n + 1) * TQ], data0=asc[:], data1=usc[:],
                    initial=hj[:, n:n + 1], op0=ALU.mult, op1=ALU.add),
                    rd=(asc_b, usc_b, hj_b), wr=(h_all_b,))
            hj, hj_b = hcur[j]
            for n in range(16):
                mk.op("dve", lambda e, hj=hj, n=n: e.tensor_copy(
                    out=hj[:, n:n + 1],
                    in_=h_all[:, (n + 1) * TQ - 1:(n + 1) * TQ]),
                    rd=(h_all_b,), wr=(hj_b,))
            mk.op("pool", lambda e: e.tensor_tensor(
                out=h_all[:], in0=h_all[:], in1=crep[:], op=ALU.mult),
                rd=(h_all_b, crep_b), wr=(h_all_b,))
            mk.op("dve", lambda e: e.tensor_reduce(
                out=yv[:], in_=h_all[:].rearrange("p (n t) -> p t n", n=16),
                axis=AX.X, op=ALU.add), rd=(h_all_b,), wr=(yv_b,))
            d_col = vec_t[j][0][:, 2:3]
            mk.op("dve", lambda e, j=j, d_col=d_col: e.scalar_tensor_tensor(
                out=t1s[:], in0=xb[j][0][:], scalar=d_col, in1=yv[:],
                op0=ALU.mult, op1=ALU.add), rd=(xb[j][1], vec_t[j][1], yv_b),
                wr=(t1s_b,))
            mk.op("dve", lambda e, j=j: e.tensor_mul(y2[j][0][:], t1s[:], zs[:]),
                  rd=(t1s_b, zs_b), wr=(y2[j][1],))
        for k in range(NT_E):
            wk, wk_b = wstr[k % 2]
            mk.dma("sp", wk[:, 0:1024], outwT[k * 128:(k + 1) * 128, :],
                   rd=(wk_b, outwT_b), wr=(wk_b,))
            for ot in range(8):
                pt, pbx = cp.pb[ot]
                mk.op("pe", lambda e, k=k, ot=ot, wk=wk, pt=pt: e.matmul(
                    pt[:], wk[:, ot * 128:(ot + 1) * 128], y2[k][0][:],
                    start=(k == 0), stop=(k == NT_E - 1)),
                    rd=(wk_b, y2[k][1]) + ((pbx,) if k == 0 else ()), wr=(pbx,))
        for ot in range(8):
            pt, pbx = cp.pb[ot]
            mk.dma("sp", rm[:], R[ot * 128:(ot + 1) * 128, MAIN0 + e0:MAIN0 + e0 + TQ],
                   rd=(rm_b,), wr=(rm_b,))
            mk.op("dve", lambda e, pt=pt: e.tensor_add(rout_s[:], pt[:], rm[:]),
                  rd=(pbx, rm_b), wr=(rout_b,))
            mk.op("dve", lambda e, ot=ot: e.tensor_add(
                rout_s[:], rout_s[:], Rt[ot][0][:, MAIN0:MAIN0 + TQ]),
                rd=(rout_b, Rt[ot][1]), wr=(rout_b,))
            mk.dma("sp", Rout[ot * 128:(ot + 1) * 128, e0:e0 + TQ], rout_s[:],
                   rd=(rout_b,), wr=())

    for j in range(NT_E):
        hj, hj_b = hcur[j]
        mk.dma("sp", hout[j * 128:(j + 1) * 128, :], hj[:], rd=(hj_b,), wr=())

    _CONST_VALS["l0b"] = list(cp.const_vals)
    return cp.mk.finalize()


def _run_l0b(Rs, P):
    nc = _get_built("l0b", build_l0b)
    vecs = np.stack([P["conv_b"], P["wd2_b"], P["W_D"]], axis=1).astype(np.float32)
    base = dict(skipw=P["skip_w"], inw=P["in_w"], wd1=P["wd1"], wd2=P["wd2"],
                wB=P["wB"], wC=P["wC"], outw=P["out_w"], Alog=P["A_log"],
                vecs=vecs, convw=P["conv_w"],
                nrmw=P["norm_w"].reshape(-1, 1), sel16=_sel16(),
                ident=_ident128(), cconst=_cconst_arr("l0b"))
    base = {k: np.ascontiguousarray(v) for k, v in base.items()}

    def mk_maps(houts):
        maps = []
        for c in range(8):
            h = c % 2
            Rext = np.zeros((1024, 2112), np.float32)
            Rext[:, 3:2051] = Rs[c]
            if h == 1:
                Rext[:, 0:3] = Rs[c - 1][:, -3:]
            hin = (houts[c - 1] if (h == 1 and houts is not None)
                   else np.zeros((2048, 16), np.float32))
            m = dict(base)
            m["R"] = Rext
            m["hinit"] = hin
            maps.append(m)
        return maps

    r1 = run_bass_kernel_spmd(nc, mk_maps(None), core_ids=list(range(8)))
    houts = [np.asarray(r1.results[c]["hout"], np.float32) for c in range(8)]
    r2 = run_bass_kernel_spmd(nc, mk_maps(houts), core_ids=list(range(8)))
    out = []
    for c in range(8):
        res = r1 if c % 2 == 0 else r2
        out.append(np.asarray(res.results[c]["Rout"], np.float32))
    return out


def build_l1a():
    cp = CP("l1a")
    cp.lnw = 520
    mk = cp.mk
    TQ = 512
    NT_D, NT_E = 8, 16
    MAIN0 = 3
    WQ = TQ + 3

    R = cp.din("R", (1024, 2112))
    Win = cp.din("Win", (4096, 1024))
    Wx = cp.din("Wx", (96, 2048))
    Wdt = cp.din("Wdt", (2048, 64))
    Wout = cp.din("Wout", (1024, 2048))
    Alog = cp.din("Alog", (2048, 16))
    vecs = cp.din("vecs", (2048, 3))
    convw = cp.din("convw", (2048, 4))
    hinit = cp.din("hinit", (2048, 16))
    sel16 = cp.din("sel16", (16, 2048), BF16)
    ident = cp.din("ident", (128, 128), BF16)
    Rout = cp.dout("Rout", (1024, 2048))
    hout = cp.dout("hout", (2048, 16))

    ones_bf = cp.sb("ones_bf", (128, 1), BF16)
    mk.op("dve", lambda e: e.memset(ones_bf[0][:], 1.0), wr=(ones_bf[1],))
    ones_f = cp.sb("ones_f", (1, 128), F32)
    mk.op("dve", lambda e: e.memset(ones_f[0][:], 1.0), wr=(ones_f[1],))

    vec_t = cp.load_tiles(vecs, 2048, 3, dtype_out=F32, name="vec")
    cw_t = cp.load_tiles(convw, 2048, 4, dtype_out=F32, name="cw")
    alog_t = cp.load_tiles(Alog, 2048, 16, dtype_out=F32, name="alog")
    hcur = cp.load_tiles(hinit, 2048, 16, dtype_out=F32, name="hcur")
    sel_t = cp.load_tiles(sel16, 16, 2048, dtype_out=BF16, name="sel", direct=True)
    id_t = cp.load_tiles(ident, 128, 128, dtype_out=BF16, name="id", direct=True)

    aneg = []
    for j in range(NT_E):
        t, tb = cp.sb(f"aneg{j}", (128, 16), F32)
        src, srcb = alog_t[j]
        mk.op("act", lambda e, src=src, t=t: e.activation(
            out=t[:], in_=src[:], func=AF.Exp), rd=(srcb,), wr=(tb,))
        mk.op("dve", lambda e, t=t: e.tensor_scalar_mul(t[:], t[:], -1.0),
              rd=(tb,), wr=(tb,))
        aneg.append((t, tb))

    brep, brep_b = cp.sb("brep", (128, 16 * 512), BF16)
    crep, crep_b = cp.sb("crep", (128, 16 * 512), BF16)
    stg = crep[:].bitcast(F32)[:, 0:2048]
    stg_b = crep_b
    stgb = brep[:, 0:4096]
    stgb_b = brep_b
    ident_pair = (id_t[0][0][:], id_t[0][1])
    stage = ((stg, stg_b), (stgb, stgb_b))
    winT, winT_b = stage_weight_hbm(cp, Win, 4096, 1024, "winT",
                                    transpose=True, stage=stage, ident=ident_pair)
    wxT, wxT_b = stage_weight_hbm(cp, Wx, 96, 2048, "wxT",
                                  transpose=True, stage=stage, ident=ident_pair)
    wdtT, wdtT_b = stage_weight_hbm(cp, Wdt, 2048, 64, "wdtT",
                                    transpose=True, stage=stage, ident=ident_pair)
    woutT, woutT_b = stage_weight_hbm(cp, Wout, 1024, 2048, "woutT",
                                      transpose=True, stage=stage, ident=ident_pair)

    wxs = cp.load_tiles(wxT, 2048, 96, dtype_out=BF16, name="wxs", direct=True)
    wdts = cp.load_tiles(wdtT, 64, 2048, dtype_out=BF16, name="wdts", direct=True)

    Rt = [cp.sb(f"Rt{j}", (128, WQ), F32) for j in range(NT_D)]
    xn = [cp.sb(f"xn{j}", (128, WQ), BF16) for j in range(NT_D)]
    xb = [cp.sb(f"xb{j}", (128, TQ), BF16) for j in range(NT_E)]
    xzf, xzf_b = cp.sb("xzf", (128, WQ), F32)
    wstr = [cp.sb(f"wstr{i}", (128, 2048), BF16) for i in range(2)]
    bsb, bsb_b = cp.sb("bsb", (16, TQ), BF16)
    csb, csb_b = cp.sb("csb", (16, TQ), BF16)
    rsb, rsb_b = cp.sb("rsb", (64, TQ), BF16)
    bcs, bcs_b = cp.sb("bcs", (32, TQ), BF16)
    dtt, dtt_b = cp.sb("dtt", (128, TQ), F32)
    sps, sps_b = cp.sb("sps", (128, TQ), F32)
    spm, spm_b = cp.sb("spm", (128, TQ), F32)
    dtx, dtx_b = cp.sb("dtx", (128, TQ), BF16)
    a_scr = [cp.sb(f"ascr{i}", (128, TQ), F32) for i in range(2)]
    u_scr = [cp.sb(f"uscr{i}", (128, TQ), BF16) for i in range(2)]
    h_all, h_all_b = cp.sb("h_all", (128, 16 * TQ), BF16)
    yv, yv_b = cp.sb("yv", (128, TQ), F32)
    y2 = [cp.sb(f"y2{j}", (128, TQ), BF16) for j in range(NT_E)]
    zs, zs_b = cp.sb("zs", (128, TQ), BF16)
    t1s, t1s_b = cp.sb("t1s", (128, TQ), F32)
    rm, rm_b = cp.sb("rm", (128, TQ), F32)
    rout_s, rout_b = cp.sb("rout_s", (128, TQ), F32)

    for q in range(4):
        e0 = q * TQ
        for j in range(NT_D):
            mk.dma("sp", Rt[j][0][:], R[j * 128:(j + 1) * 128, e0:e0 + WQ],
                   rd=(Rt[j][1],), wr=(Rt[j][1],))
        arow0, brow0 = emit_ln_stats(cp, Rt, WQ, (ones_bf[0][:], ones_bf[1]))
        emit_rowscale_apply(cp, Rt, xn, WQ, arow0, brow0,
                            (ones_f[0][:], ones_f[1]))
        for ot in range(NT_E):
            ws, ws_b = wstr[ot % 2]
            mk.dma("sp", ws[:, 0:1024].rearrange("p (k c) -> p k c", k=8),
                   winT.rearrange("(k p) o -> p k o", p=128)[:, :, ot * 128:(ot + 1) * 128],
                   rd=(ws_b, winT_b), wr=(ws_b,))
            lhs = [(ws[:, k * 128:(k + 1) * 128], ws_b) for k in range(8)]
            rhs0 = [(xn[j][0][:, 0:512], xn[j][1]) for j in range(NT_D)]
            rhs1 = [(xn[j][0][:, 3:515], xn[j][1]) for j in range(NT_D)]
            cp.mm_acc(2, 128, 512, lhs, rhs0)
            cp.mm_acc(3, 128, 512, lhs, rhs1)
            pt2, pb2 = cp.pb[2]
            pt3, pb3 = cp.pb[3]
            mk.op("act", lambda e: e.activation(out=xzf[:, 0:512], in_=pt2[:, 0:512],
                                                func=AF.Copy), rd=(pb2,), wr=(xzf_b,))
            mk.op("act", lambda e: e.activation(out=xzf[:, 512:515], in_=pt3[:, 509:512],
                                                func=AF.Copy), rd=(pb3,), wr=(xzf_b,))
            cwj, cwj_b = cw_t[ot]
            cbj, cbj_b = vec_t[ot]
            xbj, xbj_b = xb[ot]
            mk.op("dve", lambda e, cwj=cwj, cbj=cbj: e.scalar_tensor_tensor(
                out=t1s[:], in0=xzf[:, 0:512], scalar=cwj[:, 0:1],
                in1=cbj[:, 0:1].broadcast_to([128, 512]), op0=ALU.mult, op1=ALU.add),
                rd=(xzf_b, cwj_b, cbj_b), wr=(t1s_b,))
            for kk in range(1, 4):
                mk.op("dve", lambda e, cwj=cwj, kk=kk: e.scalar_tensor_tensor(
                    out=t1s[:], in0=xzf[:, kk:kk + 512], scalar=cwj[:, kk:kk + 1],
                    in1=t1s[:], op0=ALU.mult, op1=ALU.add),
                    rd=(xzf_b, cwj_b, t1s_b), wr=(t1s_b,))
            mk.op("act", lambda e, xbj=xbj: e.activation(
                out=xbj[:], in_=t1s[:], func=AF.Silu), rd=(t1s_b,), wr=(xbj_b,))
        lhs = [(wxs[k][0][:], wxs[k][1]) for k in range(NT_E)]
        rhs = [(xb[k][0][:], xb[k][1]) for k in range(NT_E)]
        pt, pbx = cp.mm_acc(0, 96, 512, lhs, rhs)
        mk.op("dve", lambda e, pt=pt: e.tensor_copy(out=rsb[:], in_=pt[0:64, :]),
              rd=(pbx,), wr=(rsb_b,))
        mk.op("dve", lambda e, pt=pt: e.tensor_copy(out=bcs[:], in_=pt[64:96, :]),
              rd=(pbx,), wr=(bcs_b,))
        mk.dma("sp", bsb[:], bcs[0:16, :], rd=(bcs_b,), wr=(bsb_b,))
        mk.dma("sp", csb[:], bcs[16:32, :], rd=(bcs_b,), wr=(csb_b,))
        selap, sel_b = sel_t[0]
        for n in range(16):
            for (src, src_b, dst, dst_b2) in ((bsb, bsb_b, brep, brep_b),
                                              (csb, csb_b, crep, crep_b)):
                pt, pbx = cp.pb[1]
                mk.op("pe", lambda e, n=n, src=src, pt=pt: e.matmul(
                    pt[:, 0:512], selap[:, n * 128:(n + 1) * 128], src[:],
                    start=True, stop=True), rd=(sel_b, src_b, pbx), wr=(pbx,))
                mk.op("dve", lambda e, n=n, dst=dst, pt=pt: e.tensor_copy(
                    out=dst[:, n * TQ:(n + 1) * TQ], in_=pt[:, 0:512]),
                    rd=(pbx,), wr=(dst_b2,))
        for j in range(NT_E):
            wdtj = wdts[0][0][:, j * 128:(j + 1) * 128]
            ptd, pbd = cp.pb[2]
            mk.op("pe", lambda e, wdtj=wdtj, ptd=ptd: e.matmul(
                ptd[:, 0:512], wdtj, rsb[:], start=True, stop=True),
                rd=(wdts[0][1], rsb_b, pbd), wr=(pbd,))
            bdt_col = vec_t[j][0][:, 1:2]
            mk.op("dve", lambda e, bdt_col=bdt_col, ptd=ptd: e.tensor_scalar_add(
                sps[:], ptd[:, 0:512], bdt_col),
                rd=(pbd, vec_t[j][1]), wr=(sps_b,))
            mk.op("dve", lambda e: e.tensor_scalar_max(spm[:], sps[:], 0.0),
                  rd=(sps_b,), wr=(spm_b,))
            mk.op("dve", lambda e: e.scalar_tensor_tensor(
                out=sps[:], in0=sps[:], scalar=-1.0, in1=sps[:],
                op0=ALU.mult, op1=ALU.min), rd=(sps_b,), wr=(sps_b,))
            mk.op("act", lambda e: e.activation(out=sps[:], in_=sps[:], func=AF.Exp),
                  rd=(sps_b,), wr=(sps_b,))
            mk.op("act", lambda e: e.activation(out=sps[:], in_=sps[:], func=AF.Ln,
                                                bias=1.0), rd=(sps_b,), wr=(sps_b,))
            mk.op("dve", lambda e: e.tensor_add(dtt[:], spm[:], sps[:]),
                  rd=(spm_b, sps_b), wr=(dtt_b,))
            mk.op("dve", lambda e, j=j: e.tensor_mul(dtx[:], dtt[:], xb[j][0][:]),
                  rd=(dtt_b, xb[j][1]), wr=(dtx_b,))
            ws2, ws2_b = wstr[(j + 1) % 2]
            mk.dma("sp", ws2[:, 0:1024].rearrange("p (k c) -> p k c", k=8),
                   winT.rearrange("(k p) o -> p k o", p=128)[:, :, 2048 + j * 128:2048 + (j + 1) * 128],
                   rd=(ws2_b, winT_b), wr=(ws2_b,))
            lhsz = [(ws2[:, k * 128:(k + 1) * 128], ws2_b) for k in range(8)]
            rhsz = [(xn[jj][0][:, MAIN0:MAIN0 + 512], xn[jj][1]) for jj in range(NT_D)]
            ptz, pbz = cp.mm_acc(3, 128, 512, lhsz, rhsz)
            mk.op("act", lambda e, ptz=ptz: e.activation(
                out=zs[:], in_=ptz[:, 0:512], func=AF.Silu), rd=(pbz,), wr=(zs_b,))
            for n in range(16):
                asc, asc_b = a_scr[n % 2]
                usc, usc_b = u_scr[n % 2]
                an_col = aneg[j][0][:, n:n + 1]
                mk.op("act", lambda e, an_col=an_col, asc=asc: e.activation(
                    out=asc[:], in_=dtt[:], func=AF.Exp, scale=an_col),
                    rd=(dtt_b, aneg[j][1]), wr=(asc_b,))
                mk.op("pool", lambda e, n=n, usc=usc: e.tensor_tensor(
                    out=usc[:], in0=dtx[:], in1=brep[:, n * TQ:(n + 1) * TQ],
                    op=ALU.mult), rd=(dtx_b, brep_b), wr=(usc_b,))
                hj, hj_b = hcur[j]
                mk.op("dve", lambda e, n=n, asc=asc, usc=usc, hj=hj: e.tensor_tensor_scan(
                    out=h_all[:, n * TQ:(n + 1) * TQ], data0=asc[:], data1=usc[:],
                    initial=hj[:, n:n + 1], op0=ALU.mult, op1=ALU.add),
                    rd=(asc_b, usc_b, hj_b), wr=(h_all_b,))
            hj, hj_b = hcur[j]
            for n in range(16):
                mk.op("dve", lambda e, hj=hj, n=n: e.tensor_copy(
                    out=hj[:, n:n + 1],
                    in_=h_all[:, (n + 1) * TQ - 1:(n + 1) * TQ]),
                    rd=(h_all_b,), wr=(hj_b,))
            mk.op("pool", lambda e: e.tensor_tensor(
                out=h_all[:], in0=h_all[:], in1=crep[:], op=ALU.mult),
                rd=(h_all_b, crep_b), wr=(h_all_b,))
            mk.op("dve", lambda e: e.tensor_reduce(
                out=yv[:], in_=h_all[:].rearrange("p (n t) -> p t n", n=16),
                axis=AX.X, op=ALU.add), rd=(h_all_b,), wr=(yv_b,))
            d_col = vec_t[j][0][:, 2:3]
            mk.op("dve", lambda e: e.tensor_mul(t1s[:], yv[:], zs[:]),
                  rd=(yv_b, zs_b), wr=(t1s_b,))
            mk.op("dve", lambda e, j=j, d_col=d_col: e.scalar_tensor_tensor(
                out=y2[j][0][:], in0=xb[j][0][:], scalar=d_col, in1=t1s[:],
                op0=ALU.mult, op1=ALU.add), rd=(xb[j][1], vec_t[j][1], t1s_b),
                wr=(y2[j][1],))
        for k in range(NT_E):
            wk, wk_b = wstr[k % 2]
            mk.dma("sp", wk[:, 0:1024], woutT[k * 128:(k + 1) * 128, :],
                   rd=(wk_b, woutT_b), wr=(wk_b,))
            for ot in range(8):
                pt, pbx = cp.pb[ot]
                mk.op("pe", lambda e, k=k, ot=ot, wk=wk, pt=pt: e.matmul(
                    pt[:], wk[:, ot * 128:(ot + 1) * 128], y2[k][0][:],
                    start=(k == 0), stop=(k == NT_E - 1)),
                    rd=(wk_b, y2[k][1]) + ((pbx,) if k == 0 else ()), wr=(pbx,))
        for ot in range(8):
            pt, pbx = cp.pb[ot]
            mk.dma("sp", rm[:], R[ot * 128:(ot + 1) * 128, MAIN0 + e0:MAIN0 + e0 + TQ],
                   rd=(rm_b,), wr=(rm_b,))
            mk.op("dve", lambda e, pt=pt: e.tensor_add(rout_s[:], pt[:], rm[:]),
                  rd=(pbx, rm_b), wr=(rout_b,))
            mk.dma("sp", Rout[ot * 128:(ot + 1) * 128, e0:e0 + TQ], rout_s[:],
                   rd=(rout_b,), wr=())

    for j in range(NT_E):
        hj, hj_b = hcur[j]
        mk.dma("sp", hout[j * 128:(j + 1) * 128, :], hj[:], rd=(hj_b,), wr=())

    _CONST_VALS["l1a"] = list(cp.const_vals)
    return cp.mk.finalize()


def _run_l1a(Rs, P):
    nc = _get_built("l1a", build_l1a)
    vecs = np.stack([P["conv_b"], P["b_dt"], P["D"]], axis=1).astype(np.float32)
    base = dict(Win=P["W_in"], Wx=P["W_x"], Wdt=P["W_dt"], Wout=P["W_out"],
                Alog=P["A_log"], vecs=vecs,
                convw=P["conv_w"].reshape(2048, 4), sel16=_sel16(),
                ident=_ident128(), cconst=_cconst_arr("l1a"))
    base = {k: np.ascontiguousarray(v) for k, v in base.items()}

    def mk_maps(houts):
        maps = []
        for c in range(8):
            h = c % 2
            Rext = np.zeros((1024, 2112), np.float32)
            Rext[:, 3:2051] = Rs[c]
            if h == 1:
                Rext[:, 0:3] = Rs[c - 1][:, -3:]
            hin = (houts[c - 1] if (h == 1 and houts is not None)
                   else np.zeros((2048, 16), np.float32))
            m = dict(base)
            m["R"] = Rext
            m["hinit"] = hin
            maps.append(m)
        return maps

    r1 = run_bass_kernel_spmd(nc, mk_maps(None), core_ids=list(range(8)))
    houts = [np.asarray(r1.results[c]["hout"], np.float32) for c in range(8)]
    r2 = run_bass_kernel_spmd(nc, mk_maps(houts), core_ids=list(range(8)))
    out = []
    for c in range(8):
        res = r1 if c % 2 == 0 else r2
        out.append(np.asarray(res.results[c]["Rout"], np.float32))
    return out


def build_l1b(debug=False):
    cp = CP("l1b")
    cp.lnw = 520
    mk = cp.mk
    TQ = 512
    NT_D = 8
    T = 2048
    C = 64
    NCH = 32
    PI = math.pi

    R = cp.din("R", (1024, 2048))
    w_in = cp.din("w_in", (128, 1024))
    w_out = cp.din("w_out", (1024, 128))
    w_gate = cp.din("w_gate", (1024, 1024))
    w_mlp1 = cp.din("w_mlp1", (4096, 1024))
    w_mlp2 = cp.din("w_mlp2", (1024, 4096))
    coupT = cp.din("coupT", (64, 16), BF16)
    ldec = cp.din("ldec", (64, 1))
    freq = cp.din("freq", (64, 1))
    bgate = cp.din("bgate", (1024, 1))
    bmlp1 = cp.din("bmlp1", (4096, 1))
    bmlp2 = cp.din("bmlp2", (1024, 1))
    lng = cp.din("lng", (1024, 4))
    jramp = cp.din("jramp", (64, 64))
    carry = cp.din("carry", (64, 2), BF16)
    ident = cp.din("ident", (128, 128), BF16)
    Rout = cp.dout("Rout", (1024, 2048))
    cout = cp.dout("cout", (64, 2), BF16)

    ones_bf = cp.sb("ones_bf", (128, 1), BF16)
    mk.op("dve", lambda e: e.memset(ones_bf[0][:], 1.0), wr=(ones_bf[1],))
    ones_f = cp.sb("ones_f", (1, 128), F32)
    mk.op("dve", lambda e: e.memset(ones_f[0][:], 1.0), wr=(ones_f[1],))

    id_t = cp.load_tiles(ident, 128, 128, dtype_out=BF16, name="id", direct=True)
    ld_t = cp.load_tiles(ldec, 64, 1, dtype_out=F32, name="ld")
    fq_t = cp.load_tiles(freq, 64, 1, dtype_out=F32, name="fq")
    bg_t = cp.load_tiles(bgate, 1024, 1, dtype_out=F32, name="bg")
    bm1_t = cp.load_tiles(bmlp1, 4096, 1, dtype_out=F32, name="bm1")
    bm2_t = cp.load_tiles(bmlp2, 1024, 1, dtype_out=F32, name="bm2")
    lng_t = cp.load_tiles(lng, 1024, 4, dtype_out=F32, name="lng")
    jr_t = cp.load_tiles(jramp, 64, 64, dtype_out=F32, name="jr")
    cy_t = cp.load_tiles(carry, 64, 2, dtype_out=BF16, name="cy", direct=True)
    cpT_t = cp.load_tiles(coupT, 64, 16, dtype_out=BF16, name="cpT", direct=True)

    brep, brep_b = cp.sb("bigb", (128, 8192), BF16)
    stg = brep[:].bitcast(F32)
    stg_b = brep_b
    stgb2, stgb2_b = cp.sb("stgb2", (128, 4096), BF16)
    ident_pair = (id_t[0][0][:], id_t[0][1])
    stage = ((stg, stg_b), (stgb2, stgb2_b))
    winT, winT_b = stage_weight_hbm(cp, w_in, 128, 1024, "winT",
                                    transpose=True, stage=stage, ident=ident_pair)
    woutT, woutT_b = stage_weight_hbm(cp, w_out, 1024, 128, "woutT",
                                      transpose=True, stage=stage, ident=ident_pair)
    wgT, wgT_b = stage_weight_hbm(cp, w_gate, 1024, 1024, "wgT",
                                  transpose=True, stage=stage, ident=ident_pair)
    wm1T, wm1T_b = stage_weight_hbm(cp, w_mlp1, 4096, 1024, "wm1T",
                                    transpose=True, stage=stage, ident=ident_pair)
    wm2T, wm2T_b = stage_weight_hbm(cp, w_mlp2, 1024, 4096, "wm2T",
                                    transpose=True, stage=stage, ident=ident_pair)
    wins = cp.load_tiles(winT, 1024, 128, dtype_out=BF16, name="wins", direct=True)
    wouts = cp.load_tiles(woutT, 128, 1024, dtype_out=BF16, name="wouts", direct=True)

    mcol, mcol_b = cp.sb("mcol", (64, 1), F32)
    lnm, lnm_b = cp.sb("lnm", (64, 1), F32)
    arc, arc_b = cp.sb("arc", (64, 1), F32)
    naic, naic_b = cp.sb("naic", (64, 1), F32)
    aic, aic_b = cp.sb("aic", (64, 1), F32)
    tv, tv_b = cp.sb("tv", (64, 1), F32)
    tw, tw_b = cp.sb("tw", (64, 1), F32)
    ti32, ti32_b = cp.sb("ti32", (64, 1), mybir.dt.int32)

    def sin_reduced(dst, dst_b, src_ap, src_b, shift):
        mk.op("dve", lambda e: e.tensor_scalar_add(tv[:], src_ap, shift),
              rd=(src_b,), wr=(tv_b,))
        mk.op("dve", lambda e: e.tensor_scalar_mul(tv[:], tv[:], 1.0 / (2 * PI)),
              rd=(tv_b,), wr=(tv_b,))
        mk.op("dve", lambda e: e.tensor_copy(out=ti32[:], in_=tv[:]),
              rd=(tv_b,), wr=(ti32_b,))
        mk.op("dve", lambda e: e.tensor_copy(out=tw[:], in_=ti32[:]),
              rd=(ti32_b,), wr=(tw_b,))
        mk.op("dve", lambda e: e.tensor_sub(tv[:], tv[:], tw[:]),
              rd=(tv_b, tw_b), wr=(tv_b,))
        mk.op("dve", lambda e: e.tensor_scalar_mul(tv[:], tv[:], 2 * PI),
              rd=(tv_b,), wr=(tv_b,))
        mk.op("dve", lambda e: e.tensor_single_scalar(
            out=tw[:], in_=tv[:], scalar=PI, op=ALU.is_ge),
            rd=(tv_b,), wr=(tw_b,))
        mk.op("dve", lambda e: e.scalar_tensor_tensor(
            out=tv[:], in0=tw[:], scalar=-2 * PI, in1=tv[:],
            op0=ALU.mult, op1=ALU.add), rd=(tw_b, tv_b), wr=(tv_b,))
        mk.op("dve", lambda e: e.tensor_single_scalar(
            out=tw[:], in_=tv[:], scalar=-PI, op=ALU.is_le),
            rd=(tv_b,), wr=(tw_b,))
        mk.op("dve", lambda e: e.scalar_tensor_tensor(
            out=tv[:], in0=tw[:], scalar=2 * PI, in1=tv[:],
            op0=ALU.mult, op1=ALU.add), rd=(tw_b, tv_b), wr=(tv_b,))
        mk.op("act", lambda e: e.activation(out=dst, in_=tv[:], func=AF.Sin),
              rd=(tv_b,), wr=(dst_b,))

    ldap = ld_t[0][0][:, 0:1]
    fqap = fq_t[0][0][:, 0:1]
    mk.op("act", lambda e: e.activation(out=mcol[:], in_=ldap, func=AF.Sigmoid),
          rd=(ld_t[0][1],), wr=(mcol_b,))
    sin_reduced(arc[:], arc_b, fqap, fq_t[0][1], PI / 2)
    mk.op("dve", lambda e: e.tensor_mul(arc[:], arc[:], mcol[:]),
          rd=(arc_b, mcol_b), wr=(arc_b,))
    sin_reduced(aic[:], aic_b, fqap, fq_t[0][1], 0.0)
    mk.op("dve", lambda e: e.tensor_mul(aic[:], aic[:], mcol[:]),
          rd=(aic_b, mcol_b), wr=(aic_b,))
    mk.op("dve", lambda e: e.tensor_scalar_mul(naic[:], aic[:], -1.0),
          rd=(aic_b,), wr=(naic_b,))

    Rt = [cp.sb(f"Rt{j}", (128, TQ), F32) for j in range(NT_D)]
    x1n = [cp.sb(f"x1n{j}", (128, T), BF16) for j in range(NT_D)]
    x1q = [cp.sb(f"x1q{j}", (128, TQ), F32) for j in range(NT_D)]
    x2n = [cp.sb(f"x2n{j}", (128, TQ), BF16) for j in range(NT_D)]
    mbuf = [cp.sb(f"mb{j}", (128, TQ), BF16) for j in range(24)]
    mbuf = mbuf + [(stgb2[:, (j - 24) * TQ:(j - 23) * TQ], stgb2_b)
                   for j in range(24, 32)]
    wstr = [cp.sb(f"wstr{i}", (128, 2048), BF16) for i in range(2)]
    tt1, tt1_b = cp.sb("tt1", (128, TQ), F32)
    gs, gs_b = cp.sb("gs", (128, TQ), BF16)
    h2s, h2s_b = cp.sb("h2s", (128, TQ), F32)
    rout_s, rout_b = cp.sb("rout_s", (128, TQ), F32)
    rm_l1b = cp.sb("rm_l1b", (128, TQ), F32)

    betar, betar_b = cp.sb("betar", (64, T), BF16)
    betai, betai_b = cp.sb("betai", (64, T), BF16)
    intr, intr_b = cp.sb("intr", (64, T), F32)
    inti, inti_b = cp.sb("inti", (64, T), F32)
    t1e, t1e_b = cp.sb("t1e", (64, NCH), F32)
    t3e, t3e_b = cp.sb("t3e", (64, NCH), F32)
    orv, orv_b = betar, betar_b
    oiv, oiv_b = betai, betai_b
    catb, catb_b = cp.sb("catb", (128, T), BF16)

    BD, BD_b = cp.sb("BD", (64, 64), BF16)
    mk.op("dve", lambda e: e.memset(BD[:], 0.0), wr=(BD_b,))
    for h in range(4):
        mk.dma("sp", BD[h * 16:(h + 1) * 16, h * 16:(h + 1) * 16],
               coupT[h * 16:(h + 1) * 16, :], rd=(cpT_t[0][1], BD_b), wr=(BD_b,))

    for q in range(4):
        e0 = q * TQ
        for j in range(NT_D):
            mk.dma("sp", Rt[j][0][:], R[j * 128:(j + 1) * 128, e0:e0 + TQ],
                   rd=(Rt[j][1],), wr=(Rt[j][1],))
        arow0, brow0 = emit_ln_stats(cp, Rt, TQ, (ones_bf[0][:], ones_bf[1]))
        emit_rowscale_apply(cp, Rt, Rt, TQ, arow0, brow0,
                            (ones_f[0][:], ones_f[1]))
        arow, brow = emit_ln_stats(cp, Rt, TQ, (ones_bf[0][:], ones_bf[1]))
        affine = [((lng_t[j][0][:, 0:1], lng_t[j][1]),
                   (lng_t[j][0][:, 1:2], lng_t[j][1])) for j in range(NT_D)]
        x1n_q = [(x1n[j][0][:, e0:e0 + TQ], x1n[j][1]) for j in range(NT_D)]
        emit_rowscale_apply(cp, Rt, x1n_q, TQ, arow, brow,
                            (ones_f[0][:], ones_f[1]), affine=affine)
        for (colr, dst, dst_b) in ((0, betar, betar_b), (64, betai, betai_b)):
            lhs = [(wins[k][0][:, colr:colr + 64], wins[k][1]) for k in range(8)]
            rhs = [(x1n[j][0][:, e0:e0 + TQ], x1n[j][1]) for j in range(NT_D)]
            pt, pbx = cp.mm_acc(0, 64, 512, lhs, rhs)
            mk.op("dve", lambda e, dst=dst, pt=pt, e0=e0: e.tensor_copy(
                out=dst[:, e0:e0 + TQ], in_=pt[0:64, 0:512]),
                rd=(pbx,), wr=(dst_b,))

    mk.op("dve", lambda e: e.memset(intr[:], 0.0), wr=(intr_b,))
    mk.op("dve", lambda e: e.memset(inti[:], 0.0), wr=(inti_b,))
    zc = cp.const_col(0.0)
    for j in range(C - 1, -1, -1):
        if j == C - 1:
            s_ar, s_ai, s_nai = zc[0][0:64, :], zc[0][0:64, :], zc[0][0:64, :]
            ab = (zc[1],)
        else:
            s_ar, s_ai, s_nai = arc[:], aic[:], naic[:]
            ab = (arc_b, aic_b, naic_b)
        pr = (j + 1) % C
        rv = intr[:].rearrange("p (c j) -> p c j", j=C)
        iv = inti[:].rearrange("p (c j) -> p c j", j=C)
        brj = betar[:].rearrange("p (c j) -> p c j", j=C)[:, :, j]
        bij = betai[:].rearrange("p (c j) -> p c j", j=C)[:, :, j]
        mk.op("dve", lambda e, s_ai=s_ai, rv=rv, pr=pr, bij=bij: e.scalar_tensor_tensor(
            out=t3e[:], in0=rv[:, :, pr], scalar=s_ai, in1=bij,
            op0=ALU.mult, op1=ALU.add), rd=(intr_b, betai_b) + ab, wr=(t3e_b,))
        mk.op("dve", lambda e, s_ar=s_ar, rv=rv, pr=pr, brj=brj: e.scalar_tensor_tensor(
            out=t1e[:], in0=rv[:, :, pr], scalar=s_ar, in1=brj,
            op0=ALU.mult, op1=ALU.add), rd=(intr_b, betar_b) + ab, wr=(t1e_b,))
        mk.op("dve", lambda e, s_nai=s_nai, iv=iv, rv=rv, pr=pr, j=j: e.scalar_tensor_tensor(
            out=rv[:, :, j], in0=iv[:, :, pr], scalar=s_nai, in1=t1e[:],
            op0=ALU.mult, op1=ALU.add), rd=(inti_b, t1e_b) + ab, wr=(intr_b,))
        mk.op("dve", lambda e, s_ar=s_ar, iv=iv, pr=pr, j=j: e.scalar_tensor_tensor(
            out=iv[:, :, j], in0=iv[:, :, pr], scalar=s_ar, in1=t3e[:],
            op0=ALU.mult, op1=ALU.add), rd=(inti_b, t3e_b) + ab, wr=(inti_b,))

    mk.dma("sp", cout[0:64, 0:1], betar[:, T - 1:T], rd=(betar_b,), wr=(),
           allow_slow_non_contiguous=True)
    mk.dma("sp", cout[0:64, 1:2], betai[:, T - 1:T], rd=(betai_b,), wr=(),
           allow_slow_non_contiguous=True)
    rv = intr[:].rearrange("p (c j) -> p c j", j=C)
    iv = inti[:].rearrange("p (c j) -> p c j", j=C)
    brv = betar[:].rearrange("p (c j) -> p c j", j=C)
    biv = betai[:].rearrange("p (c j) -> p c j", j=C)
    mk.op("dve", lambda e: e.tensor_add(rv[:, 1:NCH, 0], rv[:, 1:NCH, 0],
                                        brv[:, 0:NCH - 1, C - 1]),
          rd=(intr_b, betar_b), wr=(intr_b,))
    mk.op("dve", lambda e: e.tensor_add(iv[:, 1:NCH, 0], iv[:, 1:NCH, 0],
                                        biv[:, 0:NCH - 1, C - 1]),
          rd=(inti_b, betai_b), wr=(inti_b,))
    mk.op("dve", lambda e: e.tensor_add(intr[:, 0:1], intr[:, 0:1],
                                        cy_t[0][0][:, 0:1]),
          rd=(intr_b, cy_t[0][1]), wr=(intr_b,))
    mk.op("dve", lambda e: e.tensor_add(inti[:, 0:1], inti[:, 0:1],
                                        cy_t[0][0][:, 1:2]),
          rd=(inti_b, cy_t[0][1]), wr=(inti_b,))
    mk.op("dve", lambda e: e.tensor_copy(out=orv[:], in_=intr[:]),
          rd=(intr_b,), wr=(orv_b,))
    mk.op("dve", lambda e: e.tensor_copy(out=oiv[:], in_=inti[:]),
          rd=(inti_b,), wr=(oiv_b,))

    for q in range(4):
        e0 = q * TQ
        pt, pbx = cp.pb[0]
        mk.op("pe", lambda e, e0=e0, pt=pt: e.matmul(
            pt[0:64, 0:512], BD[:], orv[:, e0:e0 + TQ], start=True, stop=True),
            rd=(BD_b, orv_b, pbx), wr=(pbx,))
        mk.op("dve", lambda e, e0=e0, pt=pt: e.tensor_copy(
            out=catb[0:64, e0:e0 + TQ], in_=pt[0:64, 0:512]),
            rd=(pbx,), wr=(catb_b,))
        pt1, pbx1 = cp.pb[1]
        mk.op("pe", lambda e, e0=e0, pt1=pt1: e.matmul(
            pt1[64:128, 0:512], BD[:], oiv[:, e0:e0 + TQ], start=True, stop=True),
            rd=(BD_b, oiv_b, pbx1), wr=(pbx1,))
        mk.op("dve", lambda e, e0=e0, pt1=pt1: e.tensor_copy(
            out=catb[64:128, e0:e0 + TQ], in_=pt1[64:128, 0:512]),
            rd=(pbx1,), wr=(catb_b,))

    for q in range(4):
        e0 = q * TQ
        for j in range(NT_D):
            mk.dma("sp", Rt[j][0][:], R[j * 128:(j + 1) * 128, e0:e0 + TQ],
                   rd=(Rt[j][1],), wr=(Rt[j][1],))
        arow0, brow0 = emit_ln_stats(cp, Rt, TQ, (ones_bf[0][:], ones_bf[1]))
        emit_rowscale_apply(cp, Rt, Rt, TQ, arow0, brow0,
                            (ones_f[0][:], ones_f[1]))
        for ot in range(NT_D):
            ws, ws_b = wstr[ot % 2]
            mk.dma("sp", ws[:, 0:1024].rearrange("p (k c) -> p k c", k=8),
                   wgT.rearrange("(k p) o -> p k o", p=128)[:, :, ot * 128:(ot + 1) * 128],
                   rd=(ws_b, wgT_b), wr=(ws_b,))
            lhs = [(ws[:, k * 128:(k + 1) * 128], ws_b) for k in range(8)]
            rhs = [(x1n[j][0][:, e0:e0 + TQ], x1n[j][1]) for j in range(NT_D)]
            ptg, pbg = cp.mm_acc(2, 128, 512, lhs, rhs)
            bgc = bg_t[ot][0][:, 0:1]
            mk.op("dve", lambda e, ptg=ptg, bgc=bgc, ot=ot: e.tensor_scalar_add(
                tt1[:], ptg[:, 0:512], bgc), rd=(pbg, bg_t[ot][1]), wr=(tt1_b,))
            mk.op("act", lambda e: e.activation(out=gs[:], in_=tt1[:],
                                                func=AF.Sigmoid),
                  rd=(tt1_b,), wr=(gs_b,))
            wo = wouts[0][0][:, ot * 128:(ot + 1) * 128]
            pth, pbh = cp.pb[3]
            mk.op("pe", lambda e, wo=wo, e0=e0, pth=pth: e.matmul(
                pth[:, 0:512], wo, catb[:, e0:e0 + TQ], start=True, stop=True),
                rd=(wouts[0][1], catb_b, pbh), wr=(pbh,))
            mk.op("dve", lambda e, pth=pth: e.tensor_copy(out=h2s[:], in_=pth[:, 0:512]),
                  rd=(pbh,), wr=(h2s_b,))
            mk.op("dve", lambda e, ot=ot: e.tensor_mul(h2s[:], h2s[:], gs[:]),
                  rd=(h2s_b, gs_b), wr=(h2s_b,))
            mk.op("dve", lambda e, ot=ot: e.tensor_add(
                x1q[ot][0][:], Rt[ot][0][:], h2s[:]),
                rd=(Rt[ot][1], h2s_b), wr=(x1q[ot][1],))
        arow2, brow2 = emit_ln_stats(cp, x1q, TQ, (ones_bf[0][:], ones_bf[1]))
        affine2 = [((lng_t[j][0][:, 2:3], lng_t[j][1]),
                    (lng_t[j][0][:, 3:4], lng_t[j][1])) for j in range(NT_D)]
        emit_rowscale_apply(cp, x1q, x2n, TQ, arow2, brow2,
                            (ones_f[0][:], ones_f[1]), affine=affine2)
        for ot in range(32):
            ws, ws_b = wstr[ot % 2]
            mk.dma("sp", ws[:, 0:1024].rearrange("p (k c) -> p k c", k=8),
                   wm1T.rearrange("(k p) o -> p k o", p=128)[:, :, ot * 128:(ot + 1) * 128],
                   rd=(ws_b, wm1T_b), wr=(ws_b,))
            lhs = [(ws[:, k * 128:(k + 1) * 128], ws_b) for k in range(8)]
            rhs = [(x2n[j][0][:], x2n[j][1]) for j in range(NT_D)]
            pt, pbx = cp.mm_acc(4 + ot % 2, 128, TQ, lhs, rhs)
            bcol = bm1_t[ot][0][:, 0:1]
            mk.op("dve", lambda e, pt=pt, bcol=bcol, ot=ot: e.tensor_scalar_add(
                tt1[:], pt[:, 0:TQ], bcol), rd=(pbx, bm1_t[ot][1]), wr=(tt1_b,))
            mk.op("act", lambda e, ot=ot: e.activation(
                out=mbuf[ot][0][:, 0:TQ], in_=tt1[:], func=AF.Silu),
                rd=(tt1_b,), wr=(mbuf[ot][1],))
        for ot in range(NT_D):
            ws, ws_b = wstr[ot % 2]
            mk.dma("sp", ws[:, 0:2048].rearrange("p (k c) -> p k c", k=16),
                   wm2T.rearrange("(k p) o -> p k o", p=128)[:, 0:16, ot * 128:(ot + 1) * 128],
                   rd=(ws_b, wm2T_b), wr=(ws_b,))
            ws2, ws2_b = wstr[(ot + 1) % 2]
            mk.dma("sp", ws2[:, 0:2048].rearrange("p (k c) -> p k c", k=16),
                   wm2T.rearrange("(k p) o -> p k o", p=128)[:, 16:32, ot * 128:(ot + 1) * 128],
                   rd=(ws2_b, wm2T_b), wr=(ws2_b,))
            lhs = [(ws[:, k * 128:(k + 1) * 128], ws_b) for k in range(16)] + \
                  [(ws2[:, k * 128:(k + 1) * 128], ws2_b) for k in range(16)]
            rhs = [(mbuf[k][0][:, 0:TQ], mbuf[k][1]) for k in range(32)]
            pt, pbx = cp.mm_acc(6 + ot % 2, 128, TQ, lhs, rhs)
            bcol = bm2_t[ot][0][:, 0:1]
            mk.op("dve", lambda e, pt=pt, bcol=bcol, ot=ot: e.tensor_scalar_add(
                rout_s[:], pt[:, 0:TQ], bcol), rd=(pbx, bm2_t[ot][1]), wr=(rout_b,))
            mk.op("dve", lambda e, ot=ot: e.tensor_add(
                rout_s[:], rout_s[:], x1q[ot][0][:]),
                rd=(rout_b, x1q[ot][1]), wr=(rout_b,))
            mk.dma("sp", rm_l1b[0][:], R[ot * 128:(ot + 1) * 128, e0:e0 + TQ],
                   rd=(rm_l1b[1],), wr=(rm_l1b[1],))
            mk.op("dve", lambda e: e.tensor_add(rout_s[:], rout_s[:], rm_l1b[0][:]),
                  rd=(rout_b, rm_l1b[1]), wr=(rout_b,))
            mk.dma("sp", Rout[ot * 128:(ot + 1) * 128, e0:e0 + TQ], rout_s[:],
                   rd=(rout_b,), wr=())

    _CONST_VALS["l1b"] = list(cp.const_vals)
    return cp.mk.finalize()


def _run_l1b(Rs, P):
    nc = _get_built("l1b", build_l1b)
    import ml_dtypes
    coupT = np.zeros((64, 16), ml_dtypes.bfloat16)
    cp_in = np.asarray(P["coupling"], np.float32)
    for h in range(4):
        coupT[h * 16:(h + 1) * 16, :] = cp_in[h].T.astype(ml_dtypes.bfloat16)
    jramp = np.broadcast_to(np.arange(64, dtype=np.float32)[None, :], (64, 64)).copy()
    lng = np.stack([P["ln1_g"], P["ln1_b"], P["ln2_g"], P["ln2_b"]], axis=1).astype(np.float32)
    base = dict(w_in=P["w_in"], w_out=P["w_out"], w_gate=P["w_gate"],
                w_mlp1=P["w_mlp1"], w_mlp2=P["w_mlp2"], coupT=coupT,
                ldec=P["log_decay"].reshape(-1, 1), freq=P["frequency"].reshape(-1, 1),
                bgate=P["b_gate"].reshape(-1, 1), bmlp1=P["b_mlp1"].reshape(-1, 1),
                bmlp2=P["b_mlp2"].reshape(-1, 1), lng=lng, jramp=jramp,
                ident=_ident128(), cconst=_cconst_arr("l1b"))
    base = {k: np.ascontiguousarray(v) for k, v in base.items()}

    def mk_maps(couts):
        maps = []
        for c in range(8):
            h = c % 2
            import ml_dtypes
            cin = (couts[c - 1] if (h == 1 and couts is not None)
                   else np.zeros((64, 2), ml_dtypes.bfloat16))
            m = dict(base)
            m["R"] = np.ascontiguousarray(Rs[c])
            m["carry"] = np.ascontiguousarray(cin)
            maps.append(m)
        return maps

    r1 = run_bass_kernel_spmd(nc, mk_maps(None), core_ids=list(range(8)))
    couts = [np.asarray(r1.results[c]["cout"]) for c in range(8)]
    r2 = run_bass_kernel_spmd(nc, mk_maps(couts), core_ids=list(range(8)))
    out = []
    for c in range(8):
        res = r1 if c % 2 == 0 else r2
        out.append(np.asarray(res.results[c]["Rout"], np.float32))
    return out


def build_l0ffn():
    cp = CP("l0ffn")
    cp.lnw = 512
    mk = cp.mk
    W = 512
    NT_D = 8
    NW = 7
    EXTW = 1 + NW * W

    R = cp.din("R", (1024, 3588))
    wqkv = cp.din("wqkv", (2048, 1536))
    wo = cp.din("wo", (1024, 1024))
    w1 = cp.din("w1", (2048, 5460))
    w2 = cp.din("w2", (5460, 1024))
    qk_sc = cp.din("qk_sc", (64, 4))
    lnab = cp.din("lnab", (1024, 4))
    b1v = cp.din("b1v", (5460, 2))
    b2v = cp.din("b2v", (1024, 2))
    tabs = cp.din("tabs", (64, 4 * 3588), BF16)
    masks = cp.din("masks", (128, 4 * 1024), BF16)
    flags = cp.din("flags", (128, 8))
    rotm = cp.din("rotm", (64, 64), BF16)
    ident = cp.din("ident", (128, 128), BF16)
    Rout = cp.dout("Rout", (1024, 2048))

    ones_bf = cp.sb("ones_bf", (128, 1), BF16)
    mk.op("dve", lambda e: e.memset(ones_bf[0][:], 1.0), wr=(ones_bf[1],))
    ones_f = cp.sb("ones_f", (1, 128), F32)
    mk.op("dve", lambda e: e.memset(ones_f[0][:], 1.0), wr=(ones_f[1],))

    qs_t = cp.load_tiles(qk_sc, 64, 4, dtype_out=F32, name="qs")
    ln_t = cp.load_tiles(lnab, 1024, 4, dtype_out=F32, name="lnv")
    b1_t = cp.load_tiles(b1v, 5460, 2, dtype_out=F32, name="b1v")
    b2_t = cp.load_tiles(b2v, 1024, 2, dtype_out=F32, name="b2v")
    mask_t = cp.load_tiles(masks, 128, 4096, dtype_out=BF16, name="mask", direct=True)
    flag_t = cp.load_tiles(flags, 128, 8, dtype_out=F32, name="flag")
    rot_t = cp.load_tiles(rotm, 64, 64, dtype_out=BF16, name="rot", direct=True)
    id_t = cp.load_tiles(ident, 128, 128, dtype_out=BF16, name="id", direct=True)

    agpool, agpool_b = cp.sb("agpool", (128, 11264), BF16)
    stgf = agpool[:].bitcast(F32)[:, 0:1536]
    stgf_b = agpool_b
    stgbq = agpool[:, 3072:4608]
    stgbq_b = agpool_b
    wqkvb = cp.nc.dram_tensor("wqkvb", [2048, 1536], BF16, kind="Internal").ap()
    wob = cp.nc.dram_tensor("wob", [1024, 1024], BF16, kind="Internal").ap()
    wqkvb_buf = mk.buf("wqkvb")
    wob_buf = mk.buf("wob")
    for k in range(16):
        mk.dma("sp", stgf[:, 0:1536], wqkv[k * 128:(k + 1) * 128, :],
               rd=(stgf_b,), wr=(stgf_b,))
        mk.op("pool", lambda e: e.tensor_copy(out=stgbq[:, 0:1536],
                                              in_=stgf[:, 0:1536]),
              rd=(stgf_b, stgbq_b), wr=(stgbq_b,))
        mk.dma("sp", wqkvb[k * 128:(k + 1) * 128, :], stgbq[:, 0:1536],
               rd=(stgbq_b,), wr=(wqkvb_buf,))
    for k in range(8):
        mk.dma("sp", stgf[:, 0:1024], wo[k * 128:(k + 1) * 128, :],
               rd=(stgf_b,), wr=(stgf_b,))
        mk.op("pool", lambda e: e.tensor_copy(out=stgbq[:, 0:1024],
                                              in_=stgf[:, 0:1024]),
              rd=(stgf_b, stgbq_b), wr=(stgbq_b,))
        mk.dma("sp", wob[k * 128:(k + 1) * 128, :], stgbq[:, 0:1024],
               rd=(stgbq_b,), wr=(wob_buf,))

    x0c = [cp.sb(f"x0c{j}", (128, 1), F32) for j in range(NT_D)]
    y1w = [cp.sb(f"y1w{j}", (128, W), F32) for j in range(NT_D)]
    y1c = [cp.sb(f"y1c{j}", (128, 1), F32) for j in range(NT_D)]
    y2w = [cp.sb(f"y2w{j}", (128, W), F32) for j in range(NT_D)]
    pcol = [[cp.sb(f"pc{l}_{j}", (128, 2), F32) for j in range(NT_D)]
            for l in range(2)]
    Rt = [cp.sb(f"Rt{j}", (128, W), F32) for j in range(NT_D)]
    sxw = [cp.sb(f"sxw{j}", (128, W), BF16) for j in range(NT_D)]
    kf = [[[cp.sb(f"kf{l}_{h}_{p}", (64, W), BF16) for p in range(2)]
           for h in range(8)] for l in range(2)]
    vt = [[[cp.sb(f"vt{l}_{h}_{p}", (128, 4 * 64), BF16) for p in range(2)]
           for h in range(8)] for l in range(2)]
    tabw = [cp.sb(f"tab{i}", (64, W), BF16) for i in range(4)]
    qh, qh_b = cp.sb("qh", (64, W), F32)
    gq, gq_b = cp.sb("gq", (64, W), BF16)
    q_all = [cp.sb(f"qall{h}", (64, W), BF16) for h in range(8)]
    sS = agpool[:].bitcast(F32)[:, 2816:2816 + 2 * W]
    sS_b = agpool_b
    mcol, mcol_b = cp.sb("mS", (128, 1), F32)
    scol, scol_b = cp.sb("sScol", (128, 1), F32)
    Pb2 = [cp.sb(f"Pb{i}", (128, 2 * W), BF16) for i in range(2)]
    _ptt = cp.sb("Ptt0", (128, 128), BF16)
    Ptt2 = [_ptt, _ptt]
    oh = [cp.sb(f"oh{hp}", (128, W), BF16) for hp in range(4)]
    ag = [(agpool[:, i * W:(i + 1) * W], agpool_b) for i in range(22)]
    agf, agf_b = cp.sb("agf", (128, W), F32)
    ggf, ggf_b = cp.sb("ggf", (128, W), BF16)
    wstr = [cp.sb(f"wstr{i}", (128, 1024), BF16) for i in range(2)]
    wstf2 = [cp.sb(f"wstf{i}", (128, 128), F32) for i in range(2)]
    t1s, t1s_b = cp.sb("t1s", (128, W), F32)
    t2s, t2s_b = cp.sb("t2s", (128, W), F32)
    routs, routs_b = t2s, t2s_b

    def shift_into_sxw(xin, col_src, col_idx):
        for j in range(NT_D):
            xj, xj_b = xin[j]
            sj, sj_b = sxw[j]
            if j < 4:
                mk.op("act", lambda e, xj=xj, sj=sj: e.activation(
                    out=sj[:], in_=xj[:], func=AF.Copy), rd=(xj_b,), wr=(sj_b,))
            else:
                cj, cj_b = col_src[j]
                mk.op("act", lambda e, xj=xj, sj=sj: e.activation(
                    out=sj[:, 1:W], in_=xj[:, 0:W - 1], func=AF.Copy),
                    rd=(xj_b,), wr=(sj_b,))
                mk.op("act", lambda e, cj=cj, sj=sj, ci=col_idx: e.activation(
                    out=sj[:, 0:1], in_=cj[:, ci:ci + 1], func=AF.Copy),
                    rd=(cj_b,), wr=(sj_b,))

    def emit_layer(l, ew, xin, xin_col, xin_col_idx, yout, kv_only,
                   attn_only=False):
        par = ew % 2
        shift_into_sxw(xin, xin_col, xin_col_idx)
        for i in range(4):
            mk.dma("sp", tabw[i][0][:],
                   tabs[:, i * 3588 + 1 + ew * W:i * 3588 + 1 + (ew + 1) * W],
                   rd=(tabw[i][1],), wr=(tabw[i][1],))
        for ot in range(12):
            ws, ws_b = wstr[ot % 2]
            mk.dma("sp", ws[:, 0:1024].rearrange("p (k c) -> p k c", k=8),
                   wqkvb.rearrange("(k p) o -> p k o", p=128)[:, l * 8:(l + 1) * 8,
                                                             ot * 128:(ot + 1) * 128],
                   rd=(ws_b, wqkvb_buf), wr=(ws_b,))
            lhs = [(ws[:, k * 128:(k + 1) * 128], ws_b) for k in range(8)]
            rhs = [(sxw[j][0][:], sxw[j][1]) for j in range(NT_D)]
            pt, pbx = cp.mm_acc(ot % 2, 128, W, lhs, rhs)
            for half in range(2):
                h = (ot % 4) * 2 + half
                if ot < 8:
                    is_q = ot < 4
                    mk.op("dve", lambda e, pt=pt, half=half: e.tensor_copy(
                        out=qh[:], in_=pt[half * 64:(half + 1) * 64, 0:W]),
                        rd=(pbx,), wr=(qh_b,))
                    mk.op("act", lambda e: e.activation(out=gq[:], in_=qh[:],
                                                        func=AF.Square),
                          rd=(qh_b,), wr=(gq_b,))
                    ptn, pbn = cp.pb[2]
                    mk.op("pe", lambda e, ptn=ptn: e.matmul(
                        ptn[0:1, 0:W], ones_bf[0][0:64, :], gq[:],
                        start=True, stop=True), rd=(ones_bf[1], gq_b, pbn), wr=(pbn,))
                    nrow, nrow_b = cp.scratch("l2row", (1, W), F32)
                    mk.op("dve", lambda e, ptn=ptn: e.tensor_scalar_max(
                        nrow[:], ptn[0:1, 0:W], 1e-24), rd=(pbn,), wr=(nrow_b,))
                    mk.op("act", lambda e: e.activation(out=nrow[:], in_=nrow[:],
                                                        func=AF.Ln),
                          rd=(nrow_b,), wr=(nrow_b,))
                    mh, mh_b = cp.const_col(-0.5)
                    mk.op("act", lambda e, mh=mh: e.activation(
                        out=nrow[:], in_=nrow[:], func=AF.Exp, scale=mh[0:1, :]),
                        rd=(nrow_b, mh_b), wr=(nrow_b,))
                    ptr, pbr = cp.pb[3]
                    mk.op("pe", lambda e, ptr=ptr: e.matmul(
                        ptr[0:64, 0:W], ones_f[0][:, 0:64], nrow[:],
                        start=True, stop=True), rd=(ones_f[1], nrow_b, pbr), wr=(pbr,))
                    mk.op("dve", lambda e, ptr=ptr: e.tensor_mul(
                        qh[:], qh[:], ptr[0:64, 0:W]), rd=(qh_b, pbr), wr=(qh_b,))
                    sc_col = qs_t[0][0][:, (0 if is_q else 1) + 2 * l:
                                        (1 if is_q else 2) + 2 * l]
                    mk.op("dve", lambda e, sc_col=sc_col: e.tensor_scalar(
                        out=gq[:], in0=qh[:], scalar1=sc_col, scalar2=None,
                        op0=ALU.mult), rd=(qh_b, qs_t[0][1]), wr=(gq_b,))
                    ptm, pbm = cp.pb[2]
                    mk.op("pe", lambda e, ptm=ptm: e.matmul(
                        ptm[0:64, 0:W], rot_t[0][0][:], gq[:], start=True,
                        stop=True), rd=(rot_t[0][1], gq_b, pbm), wr=(pbm,))
                    ci = 0 if is_q else 2
                    mk.op("dve", lambda e, ci=ci: e.tensor_mul(
                        t1s[0:64, :], gq[:], tabw[ci][0][:]),
                        rd=(gq_b, tabw[ci][1]), wr=(t1s_b,))
                    mk.op("dve", lambda e, ci=ci, ptm=ptm: e.tensor_mul(
                        t2s[0:64, :], ptm[0:64, 0:W], tabw[ci + 1][0][:]),
                        rd=(pbm, tabw[ci + 1][1]), wr=(t2s_b,))
                    dst, dst_b = ((q_all[h][0], q_all[h][1]) if is_q
                                  else (kf[l][h][par][0], kf[l][h][par][1]))
                    mk.op("dve", lambda e, dst=dst: e.tensor_add(
                        dst[:], t1s[0:64, :], t2s[0:64, :]),
                        rd=(t1s_b, t2s_b), wr=(dst_b,))
                else:
                    mk.op("dve", lambda e, pt=pt, half=half: e.tensor_copy(
                        out=gq[:], in_=pt[half * 64:(half + 1) * 64, 0:W]),
                        rd=(pbx,), wr=(gq_b,))
                    vtt, vtt_b = vt[l][h][par]
                    for c in range(4):
                        ptb_f, ptb_b2 = cp.pb[2]
                        ptb = ptb_f[:].bitcast(BF16)
                        mk.op("pe", lambda e, c=c, ptb=ptb: e.transpose(
                            ptb[0:128, 0:64], gq[:, c * 128:(c + 1) * 128],
                            id_t[0][0][0:64, 0:64]),
                            rd=(gq_b, id_t[0][1], ptb_b2), wr=(ptb_b2,))
                        mk.op("dve", lambda e, c=c, vtt=vtt, ptb=ptb: e.tensor_copy(
                            out=vtt[:, c * 64:(c + 1) * 64], in_=ptb[0:128, 0:64]),
                            rd=(ptb_b2,), wr=(vtt_b,))
        if kv_only:
            for j in range(NT_D):
                cj, cj_b = xin_col[j]
                mk.op("act", lambda e, j=j, cj=cj: e.activation(
                    out=cj[:, 0:1], in_=xin[j][0][:, W - 1:W], func=AF.Copy),
                    rd=(xin[j][1],), wr=(cj_b,))
            return
        for h in range(8):
            for qs in range(4):
                pp = (h * 4 + qs) % 2
                Pb, Pb_b = Pb2[pp]
                Ptt, Ptt_b = Ptt2[pp]
                sb0, sb1 = (4, 5) if pp == 0 else (0, 1)
                pvb = 6 if pp == 0 else 2
                trb = 7 if pp == 0 else 3
                qsl = q_all[h][0][:, qs * 128:(qs + 1) * 128]
                for wi in range(2):
                    src, src_b = kf[l][h][(ew - 1 + wi) % 2]
                    pt, pbx = cp.pb[(sb0, sb1)[wi]]
                    mk.op("pe", lambda e, qsl=qsl, src=src, pt=pt: e.matmul(
                        pt[:, 0:W], qsl, src[:], start=True, stop=True),
                        rd=(q_all[h][1], src_b, pbx), wr=(pbx,))
                pt4, pb4 = cp.pb[sb0]
                pt5, pb5 = cp.pb[sb1]
                mk.op("dve", lambda e, qs=qs, pt4=pt4: e.tensor_add(
                    sS[:, 0:W], pt4[:, 0:W],
                    mask_t[0][0][:, qs * 1024:qs * 1024 + W]),
                    rd=(pb4, mask_t[0][1]), wr=(sS_b,))
                mk.op("dve", lambda e, qs=qs, pt5=pt5: e.tensor_add(
                    sS[:, W:2 * W], pt5[:, 0:W],
                    mask_t[0][0][:, qs * 1024 + W:qs * 1024 + 2 * W]),
                    rd=(pb5, mask_t[0][1]), wr=(sS_b,))
                fcol = flag_t[0][0][:, ew:ew + 1]
                mk.op("dve", lambda e, fcol=fcol: e.tensor_scalar_add(
                    sS[:, 0:W], sS[:, 0:W], fcol), rd=(sS_b, flag_t[0][1]),
                    wr=(sS_b,))
                mk.op("dve", lambda e: e.reduce_max(mcol[:], sS[:], axis=AX.X),
                      rd=(sS_b,), wr=(mcol_b,))
                mk.op("dve", lambda e: e.tensor_scalar_mul(mcol[:], mcol[:], -1.0),
                      rd=(mcol_b,), wr=(mcol_b,))
                mk.op("act", lambda e, Pb=Pb: e.activation(out=Pb[:], in_=sS[:],
                                                           func=AF.Exp,
                                                           bias=mcol[:]),
                      rd=(sS_b, mcol_b), wr=(Pb_b,))
                mk.op("dve", lambda e, Pb=Pb: e.tensor_reduce(
                    scol[:], Pb[:], axis=AX.X, op=ALU.add),
                    rd=(Pb_b,), wr=(scol_b,))
                mk.op("dve", lambda e: e.reciprocal(scol[:], scol[:]),
                      rd=(scol_b,), wr=(scol_b,))
                mk.op("dve", lambda e, Pb=Pb: e.tensor_scalar(
                    out=Pb[:], in0=Pb[:], scalar1=scol[:], scalar2=None,
                    op0=ALU.mult), rd=(Pb_b, scol_b), wr=(Pb_b,))
                pto, pbo = cp.pb[pvb]
                for kt in range(8):
                    ptb_f, ptb_b2 = cp.pb[trb]
                    ptb = ptb_f[:].bitcast(BF16)
                    mk.op("pe", lambda e, kt=kt, ptb=ptb, Pb=Pb: e.transpose(
                        ptb[0:128, 0:128], Pb[:, kt * 128:(kt + 1) * 128],
                        id_t[0][0][:]), rd=(Pb_b, id_t[0][1], ptb_b2), wr=(ptb_b2,))
                    mk.op("act", lambda e, ptb=ptb, Ptt=Ptt: e.activation(
                        out=Ptt[:], in_=ptb[0:128, 0:128], func=AF.Copy),
                        rd=(ptb_b2,), wr=(Ptt_b,))
                    wi = kt // 4
                    vtt, vtt_b = vt[l][h][(ew - 1 + wi) % 2]
                    c = kt % 4
                    mk.op("pe", lambda e, c=c, vtt=vtt, pto=pto, kt=kt,
                          Ptt=Ptt: e.matmul(
                        pto[0:64, 0:128], vtt[:, c * 64:(c + 1) * 64], Ptt[:],
                        start=(kt == 0), stop=(kt == 7)),
                        rd=(vtt_b, Ptt_b) + ((pbo,) if kt == 0 else ()), wr=(pbo,))
                hp = h // 2
                off = (h % 2) * 64
                mk.op("dve", lambda e, hp=hp, off=off, qs=qs, pto=pto: e.tensor_copy(
                    out=oh[hp][0][off:off + 64, qs * 128:(qs + 1) * 128],
                    in_=pto[0:64, 0:128]), rd=(pbo,), wr=(oh[hp][1],))
        for ot in range(NT_D):
            ws, ws_b = wstr[ot % 2]
            mk.dma("sp", ws[:, 0:512].rearrange("p (k c) -> p k c", k=4),
                   wob.rearrange("(k p) o -> p k o", p=128)[:, l * 4:(l + 1) * 4,
                                                           ot * 128:(ot + 1) * 128],
                   rd=(ws_b, wob_buf), wr=(ws_b,))
            lhs = [(ws[:, k * 128:(k + 1) * 128], ws_b) for k in range(4)]
            rhs = [(oh[k][0][:], oh[k][1]) for k in range(4)]
            pt, pbx = cp.mm_acc(ot % 2, 128, W, lhs, rhs)
            mk.op("dve", lambda e, pt=pt, ot=ot: e.tensor_add(
                yout[ot][0][:], pt[:, 0:W], xin[ot][0][:]),
                rd=(pbx, xin[ot][1]), wr=(yout[ot][1],))
        for j in range(NT_D):
            mk.op("act", lambda e, j=j: e.activation(
                out=pcol[l][j][0][:, par:par + 1], in_=yout[j][0][:, W - 1:W],
                func=AF.Copy), rd=(yout[j][1],), wr=(pcol[l][j][1],))
        for j in range(NT_D):
            cj, cj_b = xin_col[j]
            mk.op("act", lambda e, j=j, cj=cj: e.activation(
                out=cj[:, 0:1], in_=xin[j][0][:, W - 1:W], func=AF.Copy),
                rd=(xin[j][1],), wr=(cj_b,))
        if attn_only:
            return
        shift_into_sxw(yout, [pcol[l][j] for j in range(NT_D)], 1 - par)
        sxf = Rt
        for j in range(NT_D):
            mk.op("dve", lambda e, j=j: e.tensor_copy(out=sxf[j][0][:],
                                                      in_=sxw[j][0][:]),
                  rd=(sxw[j][1],), wr=(sxf[j][1],))
        arow, brow = emit_ln_stats(cp, sxf, W, (ones_bf[0][:], ones_bf[1]))
        affine = [((ln_t[j][0][:, 2 * l:2 * l + 1], ln_t[j][1]),
                   (ln_t[j][0][:, 2 * l + 1:2 * l + 2], ln_t[j][1]))
                  for j in range(NT_D)]
        emit_rowscale_apply(cp, sxf, sxw, W, arow, brow,
                            (ones_f[0][:], ones_f[1]), affine=affine)
        for i in range(22):
            r = min(128, 2730 - i * 128)
            for (coff, ws_i) in ((i * 128, 0), (2730 + i * 128, 1)):
                ws, ws_b = wstr[ws_i]
                for k in range(8):
                    wstf, wstf_b = wstf2[k % 2]
                    mk.dma("sp", wstf[:, 0:r],
                           w1[l * 1024 + k * 128:l * 1024 + (k + 1) * 128,
                              coff:coff + r],
                           rd=(wstf_b,), wr=(wstf_b,))
                    mk.op("pool", lambda e, k=k, r=r, ws=ws, wstf=wstf:
                          e.tensor_copy(
                        out=ws[:, k * 128:k * 128 + r], in_=wstf[:, 0:r]),
                        rd=(wstf_b, ws_b), wr=(ws_b,))
            rhs = [(sxw[j][0][:], sxw[j][1]) for j in range(NT_D)]
            lhs = [(wstr[0][0][:, k * 128:k * 128 + r], wstr[0][1]) for k in range(8)]
            pta, pba = cp.mm_acc(2, r, W, lhs, rhs)
            abias, abias_b = cp.scratch("abias", (128, 1), F32)
            mk.dma("sp", abias[0:r, :], b1v[i * 128:i * 128 + r, l:l + 1],
                   rd=(abias_b,), wr=(abias_b,), allow_slow_non_contiguous=True)
            mk.op("dve", lambda e, pta=pta, r=r: e.tensor_scalar_add(
                agf[0:r, :], pta[0:r, 0:W], abias[0:r, 0:1]),
                rd=(pba, abias_b), wr=(agf_b,))
            lhs2 = [(wstr[1][0][:, k * 128:k * 128 + r], wstr[1][1]) for k in range(8)]
            ptg, pbg = cp.mm_acc(3, r, W, lhs2, rhs)
            gbias, gbias_b = cp.scratch("gbias", (128, 1), F32)
            mk.dma("sp", gbias[0:r, :], b1v[2730 + i * 128:2730 + i * 128 + r, l:l + 1],
                   rd=(gbias_b,), wr=(gbias_b,), allow_slow_non_contiguous=True)
            mk.op("dve", lambda e, ptg=ptg, r=r: e.tensor_scalar_add(
                t1s[0:r, :], ptg[0:r, 0:W], gbias[0:r, 0:1]),
                rd=(pbg, gbias_b), wr=(t1s_b,))
            mk.op("act", lambda e, r=r: e.activation(out=ggf[0:r, :],
                                                     in_=t1s[0:r, :],
                                                     func=AF.Gelu),
                  rd=(t1s_b,), wr=(ggf_b,))
            mk.op("dve", lambda e, i=i, r=r: e.tensor_mul(
                ag[i][0][0:r, 0:W], agf[0:r, :], ggf[0:r, :]),
                rd=(agf_b, ggf_b), wr=(ag[i][1],))
        for ot in range(NT_D):
            pt, pbx = cp.pb[2 + ot % 2]
            for k in range(22):
                r = min(128, 2730 - k * 128)
                wstf, wstf_b = wstf2[k % 2]
                mk.dma("sp", wstf[0:r, 0:128],
                       w2[l * 2730 + k * 128:l * 2730 + k * 128 + r,
                          ot * 128:(ot + 1) * 128],
                       rd=(wstf_b,), wr=(wstf_b,))
                wsb2, wsb2_b = cp.scratch(("w2b", k % 2), (128, 128), BF16)
                mk.op("pool", lambda e, r=r, wsb2=wsb2, wstf=wstf:
                      e.tensor_copy(
                    out=wsb2[0:r, :], in_=wstf[0:r, 0:128]),
                    rd=(wstf_b, wsb2_b), wr=(wsb2_b,))
                mk.op("pe", lambda e, k=k, r=r, wsb2=wsb2, pt=pt: e.matmul(
                    pt[:, 0:W], wsb2[0:r, 0:128], ag[k][0][0:r, 0:W],
                    start=(k == 0), stop=(k == 21)),
                    rd=(wsb2_b, ag[k][1]) + ((pbx,) if k == 0 else ()), wr=(pbx,))
            b2c = b2_t[ot][0][:, l:l + 1]
            mk.op("dve", lambda e, pt=pt, b2c=b2c, ot=ot: e.tensor_scalar_add(
                t2s[:], pt[:, 0:W], b2c), rd=(pbx, b2_t[ot][1]), wr=(t2s_b,))
            mk.op("dve", lambda e, ot=ot: e.tensor_add(
                yout[ot][0][:], yout[ot][0][:], t2s[:]),
                rd=(yout[ot][1], t2s_b), wr=(yout[ot][1],))

    for j in range(NT_D):
        mk.op("dve", lambda e, j=j: e.memset(x0c[j][0][:], 0.0), wr=(x0c[j][1],))
        mk.op("dve", lambda e, j=j: e.memset(y1c[j][0][:], 0.0), wr=(y1c[j][1],))
        mk.op("dve", lambda e, j=j, l=0: e.memset(pcol[0][j][0][:], 0.0),
              wr=(pcol[0][j][1],))
        mk.op("dve", lambda e, j=j: e.memset(pcol[1][j][0][:], 0.0),
              wr=(pcol[1][j][1],))

    for ew in range(NW):
        for j in range(NT_D):
            mk.dma("sp", Rt[j][0][:],
                   R[j * 128:(j + 1) * 128, 1 + ew * W:1 + (ew + 1) * W],
                   rd=(Rt[j][1],), wr=(Rt[j][1],))
        arow, brow = emit_ln_stats(cp, Rt, W, (ones_bf[0][:], ones_bf[1]))
        emit_rowscale_apply(cp, Rt, Rt, W, arow, brow, (ones_f[0][:], ones_f[1]))
        emit_layer(0, ew, Rt, x0c, 0, y1w, kv_only=(ew == 0))
        if ew == 1:
            emit_layer(1, 1, y1w, y1c, 0, y2w, kv_only=True)
        elif ew == 2:
            emit_layer(1, 2, y1w, y1c, 0, y2w, kv_only=False, attn_only=True)
        elif ew >= 3:
            emit_layer(1, ew, y1w, y1c, 0, y2w, kv_only=False)
        if ew >= 3:
            for ot in range(NT_D):
                mk.dma("sp", Rt[ot][0][:],
                       R[ot * 128:(ot + 1) * 128, 1 + ew * W:1 + (ew + 1) * W],
                       rd=(Rt[ot][1],), wr=(Rt[ot][1],))
                mk.op("dve", lambda e, ot=ot: e.tensor_add(
                    routs[:], Rt[ot][0][:], y2w[ot][0][:]),
                    rd=(Rt[ot][1], y2w[ot][1]), wr=(routs_b,))
                mk.dma("sp", Rout[ot * 128:(ot + 1) * 128, (ew - 3) * W:(ew - 2) * W],
                       routs[:], rd=(routs_b,), wr=())

    _CONST_VALS["l0ffn"] = list(cp.const_vals)
    return cp.mk.finalize()


def _l0ffn_tables(t0):
    import ml_dtypes
    DIM_HEAD, WIN = 64, 512
    cols = 3588
    g = t0 - 1537 + np.arange(cols, dtype=np.float64)
    half = np.arange(0, DIM_HEAD, 2, dtype=np.float64)
    inv = 1.0 / 10000.0 ** (half / DIM_HEAD)
    base = (half + 0.4 * DIM_HEAD) / (1.4 * DIM_HEAD)
    power = (g - 2048.0) / (WIN // 2)
    fr = g[None, :] * inv[:, None]
    sc = base[:, None] ** power[None, :]
    cs = np.cos(fr); sn = np.sin(fr)
    csq = np.concatenate([cs * sc, cs * sc], 0) * 8.0
    ssq = np.concatenate([sn * sc, sn * sc], 0) * 8.0
    csk = np.concatenate([cs / sc, cs / sc], 0)
    ssk = np.concatenate([sn / sc, sn / sc], 0)
    tabs = np.concatenate([csq, ssq, csk, ssk], 1).astype(ml_dtypes.bfloat16)
    return np.ascontiguousarray(tabs)


def _l0ffn_masks():
    import ml_dtypes
    m = np.zeros((128, 4 * 1024), np.float32)
    for qs in range(4):
        r = np.arange(128)[:, None]
        c = np.arange(1024)[None, :]
        qq = qs * 128 + r
        bad = (c < qq) | (c > qq + 512)
        m[:, qs * 1024:(qs + 1) * 1024] = np.where(bad, -30000.0, 0.0)
    return np.ascontiguousarray(m.astype(ml_dtypes.bfloat16))


def _rotm():
    import ml_dtypes
    m = np.zeros((64, 64), np.float32)
    for i in range(32):
        m[i + 32, i] = -1.0
    for i in range(32, 64):
        m[i - 32, i] = 1.0
    return np.ascontiguousarray(m.astype(ml_dtypes.bfloat16))


def _run_l0ffn(Rs, P):
    nc = _get_built("l0ffn", build_l0ffn)
    qk = np.stack([P["q_scale"][0], P["k_scale"][0],
                   P["q_scale"][1], P["k_scale"][1]], axis=1).astype(np.float32)
    lnab = np.stack([P["ln_g"][0], P["ln_b"][0], P["ln_g"][1], P["ln_b"][1]],
                    axis=1).astype(np.float32)
    b1v = np.stack([P["b1"][0], P["b1"][1]], axis=1).astype(np.float32)
    b2v = np.stack([P["b2"][0], P["b2"][1]], axis=1).astype(np.float32)
    base = dict(wqkv=P["wqkv"].reshape(2048, 1536), wo=P["wo"].reshape(1024, 1024),
                w1=P["w1"].reshape(2048, 5460), w2=P["w2"].reshape(5460, 1024),
                qk_sc=qk, lnab=lnab, b1v=b1v, b2v=b2v, masks=_l0ffn_masks(),
                rotm=_rotm(), ident=_ident128(), cconst=_cconst_arr("l0ffn"))
    base = {k: np.ascontiguousarray(v) for k, v in base.items()}
    maps = []
    for c in range(8):
        h = c % 2
        t0 = h * 2048
        Rext = np.zeros((1024, 3588), np.float32)
        if h == 0:
            Rext[:, 1537:3585] = Rs[c]
        else:
            Rext[:, 1:1537] = Rs[c - 1][:, 512:2048]
            Rext[:, 1537:3585] = Rs[c]
            Rext[:, 0] = Rs[c - 1][:, 511]
        flags = np.zeros((128, 8), np.float32)
        if h == 0:
            flags[:, 3] = -30000.0
        m = dict(base)
        m["R"] = Rext
        m["tabs"] = _l0ffn_tables(t0)
        m["flags"] = np.ascontiguousarray(flags)
        maps.append(m)
    res = run_bass_kernel_spmd(nc, maps, core_ids=list(range(8)))
    return [np.asarray(res.results[c]["Rout"], np.float32) for c in range(8)]


def kernel(**inputs):
    Rs = _feat_major_cores(np.asarray(inputs["L0_attn_a_x"], np.float32))
    Rs = _run_l0a(Rs, _sub_params("l0a", inputs))
    pf = _sub_params("l0ffn", inputs)
    Rs = _run_l0ffn(Rs, pf)
    Rs = _run_l0b(Rs, _sub_params("l0b", inputs))
    Rs = _run_l0ffn(Rs, pf)
    Rs = _run_l1a(Rs, _sub_params("l1a", inputs))
    pl1f = _sub_params("l1ffn", inputs)
    Rs = _run_l1ffn(Rs, pl1f)
    Rs = _run_l1b(Rs, _sub_params("l1b", inputs))
    Rs = _run_l1ffn(Rs, pl1f)
    return _cores_to_full(Rs)
